# version 09
# baseline (speedup 1.0000x reference)
import numpy as np
import ml_dtypes
import concourse.bass as bass
import concourse.mybir as mybir
from concourse.bass import ds
from concourse.bass_utils import run_bass_kernel_spmd

F32 = mybir.dt.float32
BF16 = mybir.dt.bfloat16
I32 = mybir.dt.int32
ET = mybir.EngineType
AF = mybir.ActivationFunctionType
ALU = mybir.AluOpType

P = 128
T = 2048
D = 1024
L3_DBG_STATIC = False
L3_DBG_DUMP = False
CONST_VALS = [1e-5, 1e-6, 1e-8, 1.0, -1.0, 0.5, 1e9, -1e9]
NT = T // P
ND = D // P
HEAP = 57344
MISC = 4096
BF = ml_dtypes.bfloat16


class Ctx:
    def __init__(self, nc):
        self.nc = nc
        self._sems = {}
        self.win = {}

    def sem(self, name):
        if name not in self._sems:
            self._sems[name] = [self.nc.alloc_semaphore(name), 0]
        return self._sems[name][0]

    def bump(self, name, n=1):
        self.sem(name)
        s = self._sems[name]
        s[1] += n
        return s[1]

    def val(self, name):
        self.sem(name)
        return self._sems[name][1]

    def edge(self, instr, eng, name):
        instr.then_inc(self.sem(name), 1)
        eng.wait_ge(self.sem(name), self.bump(name))


def load_w(ctx, g, dram, off, K, M, sem):
    hp = ctx.heap
    nk = (K + P - 1) // P
    if K > P:
        assert K % P == 0
        src = dram[:, :].rearrange("(j p) n -> p j n", p=P)
        dst = bass.AP(hp, off, [[hp.shape[1], P], [M, nk], [1, M]])
    else:
        src = dram[:, :]
        dst = bass.AP(hp, off, [[hp.shape[1], K], [1, M]])
    g.dma_start(dst, src).then_inc(ctx.sem(sem), 16)
    return ctx.bump(sem, 16)


def load_row(ctx, g, dram2d, off, M, sem):
    hp = ctx.heap
    dst = bass.AP(hp, off, [[hp.shape[1], 1], [1, M]])
    g.dma_start(dst, dram2d[:, :]).then_inc(ctx.sem(sem), 16)
    return ctx.bump(sem, 16)


def load_cols(ctx, g, dram2d, off, nk, sem):
    ms = ctx.misc
    dst = bass.AP(ms, off, [[ms.shape[1], P], [1, nk]])
    g.dma_start(dst, dram2d[:, :]).then_inc(ctx.sem(sem), 16)
    return ctx.bump(sem, 16)


def load_cols_sl(ctx, g, dram2d, c0, off, nk, sem):
    ms = ctx.misc
    dst = bass.AP(ms, off, [[ms.shape[1], P], [1, nk]])
    g.dma_start(dst, dram2d[:, c0:c0 + nk]).then_inc(ctx.sem(sem), 16)
    return ctx.bump(sem, 16)


def load_row_sl(ctx, g, dram2d, r, off, M, sem):
    hp = ctx.heap
    dst = bass.AP(hp, off, [[hp.shape[1], 1], [1, M]])
    g.dma_start(dst, dram2d[r:r + 1, :]).then_inc(ctx.sem(sem), 16)
    return ctx.bump(sem, 16)


def load_w_dyn(ctx, g, dram2d, dyn_c0, off, K, M, sem):
    hp = ctx.heap
    assert K % P == 0
    nk = K // P
    src = dram2d[:, ds(dyn_c0, M)]
    dst = bass.AP(hp, off, [[M, nk], [hp.shape[1], P], [1, M]])
    g.dma_start(dst, src).then_inc(ctx.sem(sem), 16)
    return ctx.bump(sem, 16)


def load_cols_dyn(ctx, g, dram2d, dyn0, off, nk, sem):
    ms = ctx.misc
    dst = bass.AP(ms, off, [[ms.shape[1], P], [1, nk]])
    g.dma_start(dst, dram2d[:, ds(dyn0, nk)]).then_inc(ctx.sem(sem), 16)
    return ctx.bump(sem, 16)


def load_row_dyn(ctx, g, dram2d, dynr, off, M, sem):
    hp = ctx.heap
    dst = bass.AP(hp, off, [[hp.shape[1], 1], [1, M]])
    g.dma_start(dst, dram2d[ds(dynr, 1), :]).then_inc(ctx.sem(sem), 16)
    return ctx.bump(sem, 16)


def emit_ln(ctx, xt_off=0, tok_off=None, tag="", gcols=None, bcols=None,
            tok_in_x=False):
    nc = ctx.nc
    x_sb, heap, misc = ctx.x, ctx.heap, ctx.misc
    sA, sV, sT, sE = "lnA", "lnV", "lnT", "lnE"
    bA, bV, bT, bE = ctx.val(sA), ctx.val(sV), ctx.val(sT), ctx.val(sE)
    tmp_off = ctx.ln_tmp_off

    def xt_dst(i):
        hp = heap
        return bass.AP(hp, xt_off + i * P, [[hp.shape[1], P], [T, ND], [1, P]])

    with nc.Block() as blk:
        @blk.scalar
        def _(s):
            scrA = misc[:, 0:1024]
            for i in range(NT + 1):
                if i >= 1:
                    s.wait_ge(ctx.sem(sT), bT + i)
                    if gcols is None:
                        s.copy(xt_dst(i - 1), ctx.ps_bf[:, 0:ND * P]).then_inc(ctx.sem(sE), 1)
                    else:
                        for j in range(ND):
                            ins = s.activation(
                                heap[:, xt_off + j * T + (i - 1) * P:
                                     xt_off + j * T + i * P],
                                ctx.ps_bf[:, j * P:(j + 1) * P], AF.Identity,
                                bias=misc[:, bcols + j:bcols + j + 1],
                                scale=misc[:, gcols + j:gcols + j + 1])
                        ins.then_inc(ctx.sem(sE), 1)
                if i >= NT:
                    break
                pa = i % 2
                xt = x_sb[:, i * D:(i + 1) * D]
                scrB = misc[:, 1024 + pa * D:1024 + (pa + 1) * D]
                st = misc[:, 3072 + pa * 4:3072 + pa * 4 + 4]
                if i >= 2:
                    s.wait_ge(ctx.sem(sV), bV + 2 * (i - 2) + 2)
                s.activation(scrA, xt, AF.Copy, bias=0.0, scale=1.0 / D,
                             accum_out=st[:, 0:1]).then_inc(ctx.sem(sA), 1)
                s.wait_ge(ctx.sem(sV), bV + 2 * i + 1)
                s.activation(scrA, scrB, AF.Square, bias=0.0, scale=1.0,
                             accum_out=st[:, 1:2]).then_inc(ctx.sem(sA), 1)
                s.wait_ge(ctx.sem(sA), bA + 4 * i + 2)
                s.activation(st[:, 2:3], st[:, 1:2], AF.Ln, bias=1e-5,
                             scale=1.0 / D).then_inc(ctx.sem(sA), 1)
                s.wait_ge(ctx.sem(sA), bA + 4 * i + 3)
                s.activation(st[:, 3:4], st[:, 2:3], AF.Exp, bias=0.0,
                             scale=-0.5).then_inc(ctx.sem(sA), 1)
        ctx.bump(sA, 4 * NT)
        ctx.bump(sE, NT)

        @blk.vector
        def _(v):
            for i in range(NT):
                pa = i % 2
                xt = x_sb[:, i * D:(i + 1) * D]
                scrB = misc[:, 1024 + pa * D:1024 + (pa + 1) * D]
                st = misc[:, 3072 + pa * 4:3072 + pa * 4 + 4]
                v.wait_ge(ctx.sem(sA), bA + 4 * i + 1)
                v.tensor_scalar(scrB, xt, st[:, 0:1], None,
                                ALU.subtract).then_inc(ctx.sem(sV), 1)
                v.wait_ge(ctx.sem(sA), bA + 4 * i + 4)
                if tok_off is not None:
                    dst = heap[:, tok_off + i * D: tok_off + (i + 1) * D]
                else:
                    dst = heap[:, tmp_off + pa * D: tmp_off + (pa + 1) * D]
                    if i >= 2:
                        v.wait_ge(ctx.sem(sT), bT + i - 1)
                if tok_in_x:
                    v.wait_ge(ctx.sem(sV), bV + 2 * i + 1)
                    ins = v.tensor_scalar(scrB, scrB, st[:, 3:4], None,
                                          ALU.mult)
                    ctx.edge(ins, v, "chV")
                    ins = v.tensor_copy(xt, scrB)
                    ctx.edge(ins, v, "chV")
                    v.tensor_copy(dst, scrB).then_inc(ctx.sem(sV), 1)
                else:
                    v.tensor_scalar(dst, scrB, st[:, 3:4], None,
                                    ALU.mult).then_inc(ctx.sem(sV), 1)
        ctx.bump(sV, 2 * NT)

        @blk.tensor
        def _(t):
            for i in range(NT):
                pa = i % 2
                if tok_off is not None:
                    src = heap[:, tok_off + i * D: tok_off + (i + 1) * D]
                else:
                    src = heap[:, tmp_off + pa * D: tmp_off + (pa + 1) * D]
                t.wait_ge(ctx.sem(sV), bV + 2 * i + 2)
                if i >= 1:
                    t.wait_ge(ctx.sem(sE), bE + i)
                for j in range(ND):
                    ins = t.transpose(ctx.ps_bf[:, j * P:(j + 1) * P],
                                      src[:, j * P:(j + 1) * P], ctx.ident[:, :])
                ins.then_inc(ctx.sem(sT), 1)
        ctx.bump(sT, NT)


def linearA(ctx, tag, src_off, w_off, K, M, wait, evac, bias_row=None,
            n_banks=2, evac_eng="scalar"):
    nc = ctx.nc
    sMM, sEV = "amm", "aev"
    bMM, bEV = ctx.val(sMM), ctx.val(sEV)
    nk = K // P
    nn = (M + 511) // 512
    assert nn <= n_banks
    with nc.Block() as blk:
        @blk.tensor
        def _(t):
            hp = ctx.heap
            for name, v in wait:
                t.wait_ge(ctx.sem(name), v)
            for i in range(NT):
                if i >= 2:
                    t.wait_ge(ctx.sem(sEV), bEV + i - 1)
                b0 = (i % 2) * 2
                ins = None
                for j in range(nk):
                    lhsT = hp[:, src_off + j * T + i * P: src_off + j * T + (i + 1) * P]
                    for n in range(nn):
                        mw = min(512, M - n * 512)
                        ins = t.matmul(ctx.ps[:, (b0 + n) * 512:(b0 + n) * 512 + mw],
                                       lhsT,
                                       hp[:, w_off + j * M + n * 512: w_off + j * M + n * 512 + mw],
                                       start=(j == 0), stop=(j == nk - 1 and bias_row is None))
                if bias_row is not None:
                    for n in range(nn):
                        mw = min(512, M - n * 512)
                        ins = t.matmul(ctx.ps[:, (b0 + n) * 512:(b0 + n) * 512 + mw],
                                       ctx.ones1[0:1, 0:P],
                                       hp[0:1, bias_row + n * 512: bias_row + n * 512 + mw],
                                       start=False, stop=True)
                ins.then_inc(ctx.sem(sMM), 1)
        ctx.bump(sMM, NT)

        def evac_section(s):
            for i in range(NT):
                b0 = (i % 2) * 2
                s.wait_ge(ctx.sem(sMM), bMM + i + 1)
                ps = ctx.ps[:, b0 * 512: b0 * 512 + M]
                ins = evac(i, ps, s)
                ins.then_inc(ctx.sem(sEV), 1)
        if evac_eng == "scalar":
            blk.scalar(evac_section)
        else:
            blk.vector(evac_section)
        ctx.bump(sEV, NT)


def linearB(ctx, tag, src_off, w_off, K, M, wait, evac, tw=512, t_total=T,
            src_row=T):
    nc = ctx.nc
    sMM, sEV = "bmm", "bev"
    bMM, bEV = ctx.val(sMM), ctx.val(sEV)
    nk = K // P
    nm = M // P
    nw = t_total // tw
    with nc.Block() as blk:
        @blk.tensor
        def _(t):
            hp = ctx.heap
            for name, v in wait:
                t.wait_ge(ctx.sem(name), v)
            idx = 0
            for mi in range(nm):
                for w in range(nw):
                    if idx >= 4:
                        t.wait_ge(ctx.sem(sEV), bEV + idx - 3)
                    ps = ctx.ps[:, (idx % 4) * 512:(idx % 4) * 512 + tw]
                    for j in range(nk):
                        ins = t.matmul(ps,
                                       hp[:, w_off + j * M + mi * P: w_off + j * M + (mi + 1) * P],
                                       hp[:, src_off + j * src_row + w * tw: src_off + j * src_row + w * tw + tw],
                                       start=(j == 0), stop=(j == nk - 1))
                    ins.then_inc(ctx.sem(sMM), 1)
                    idx += 1
        ctx.bump(sMM, nm * nw)

        @blk.scalar
        def _(s):
            idx = 0
            for mi in range(nm):
                for w in range(nw):
                    s.wait_ge(ctx.sem(sMM), bMM + idx + 1)
                    ps = ctx.ps[:, (idx % 4) * 512:(idx % 4) * 512 + tw]
                    ins = evac(mi, w, ps, s)
                    ins.then_inc(ctx.sem(sEV), 1)
                    idx += 1
        ctx.bump(sEV, nm * nw)


def emit_attn_pair(ctx, tag, hp_idx, q2t, k2t, v2, ot_off, scale, pt_off,
                   stg_off, causal=False, bias_fn=None, kset_fn=None,
                   heads=(0, 1), nqg=4, mode="ot", mask_fn=None,
                   null_kcol=None, null_vrow=None):
    nc = ctx.nc
    hp_ = ctx.heap
    qgw = 512
    nsub = qgw // P
    sS, sP, sPM, sAV = "atS", "atP", "atPM", "atAV"
    sOD, sST, sTR, sOE = "atOD", "atST", "atTR", "atOE"

    plan = []
    for h in heads:
        for qig in range(nqg):
            if kset_fn is not None:
                kset = list(kset_fn(qig))
            elif causal:
                kset = list(range(qig * nsub + nsub))
            else:
                kset = list(range(NT))
            steps = []
            for idx, ki in enumerate(kset):
                if mask_fn is not None:
                    cols, masks = mask_fn(qig, ki)
                elif causal:
                    cols = [qs for qs in range(nsub) if qig * nsub + qs >= ki]
                    dq = ki - qig * nsub
                    masks = [(dq, "tril")] if 0 <= dq < nsub else []
                else:
                    cols = list(range(nsub))
                    masks = []
                steps.append((idx, ki, cols, masks))
            plan.append((h, qig, steps))

    b0 = dict(S=ctx.val(sS), p=ctx.val(sP), PM=ctx.val(sPM), AV=ctx.val(sAV),
              OD=ctx.val(sOD), ST=ctx.val(sST), TR=ctx.val(sTR), OE=ctx.val(sOE))

    with nc.Block() as blk:
        @blk.tensor
        def _(t):
            nS = b0["S"]; stg = 0
            pdone = b0["p"]
            avdone = b0["AV"]
            for pi, (h, qig, steps) in enumerate(plan):
                pbase = pdone
                col_first = {}
                col_last = {}
                for idx, ki, cols, masked in steps:
                    for qs in cols:
                        col_first.setdefault(qs, idx)
                        col_last[qs] = idx
                if null_kcol is not None:
                    for qs in col_last:
                        col_last[qs] = -99
                def emit_av(idx, ki, cols):
                    nonlocal nS
                    if pi >= 1 and idx == 0:
                        t.wait_ge(ctx.sem(sST), b0["ST"] + pi * nsub)
                    ins = None
                    for qs in cols:
                        vs = hp_[:, v2 + ki * 130 + h * 65: v2 + ki * 130 + h * 65 + 65]
                        ins = t.matmul(ctx.ps[:, (2 + qs) * 512:(2 + qs) * 512 + 65],
                                       hp_[:, pt_off + (idx % 2) * qgw + qs * P:
                                           pt_off + (idx % 2) * qgw + (qs + 1) * P],
                                       vs, start=(col_first[qs] == idx),
                                       stop=(col_last[qs] == idx))
                    ins.then_inc(ctx.sem(sAV), 1)
                for idx, ki, cols, masked in steps:
                    if idx >= 2:
                        t.wait_ge(ctx.sem(sP), pbase + idx - 1)
                    ins = t.matmul(ctx.ps[:, (idx % 2) * 512:(idx % 2) * 512 + qgw],
                                   (hp_[0:64, k2t + ki * P: k2t + (ki + 1) * P] if h == 0
                                    else hp_[64:128, k2t + ki * P: k2t + (ki + 1) * P]),
                                   (hp_[0:64, q2t + qig * qgw: q2t + (qig + 1) * qgw] if h == 0
                                    else hp_[64:128, q2t + qig * qgw: q2t + (qig + 1) * qgw]),
                                   start=True, stop=True)
                    ins.then_inc(ctx.sem(sS), 1); nS += 1
                    if idx >= 1:
                        pidx, pki, pcols, _pm = steps[idx - 1]
                        t.wait_ge(ctx.sem(sP), pbase + idx)
                        emit_av(pidx, pki, pcols)
                lidx, lki, lcols, _lm = steps[-1]
                t.wait_ge(ctx.sem(sP), pbase + len(steps))
                emit_av(lidx, lki, lcols)
                if null_kcol is not None:
                    nidx = len(steps)
                    ins = t.matmul(ctx.ps[0:1, (nidx % 2) * 512:(nidx % 2) * 512 + qgw],
                                   (hp_[0:64, null_kcol + h: null_kcol + h + 1] if h == 0
                                    else hp_[64:128, null_kcol + h: null_kcol + h + 1]),
                                   (hp_[0:64, q2t + qig * qgw: q2t + (qig + 1) * qgw] if h == 0
                                    else hp_[64:128, q2t + qig * qgw: q2t + (qig + 1) * qgw]),
                                   start=True, stop=True)
                    ins.then_inc(ctx.sem(sS), 1); nS += 1
                    t.wait_ge(ctx.sem(sP), pbase + nidx + 1)
                    for qs in range(nsub):
                        ins = t.matmul(ctx.ps[:, (2 + qs) * 512:(2 + qs) * 512 + 65],
                                       hp_[0:1, pt_off + (nidx % 2) * qgw + qs * P:
                                           pt_off + (nidx % 2) * qgw + qs * P + P],
                                       hp_[0:1, null_vrow + h * 68: null_vrow + h * 68 + 65],
                                       start=False, stop=True)
                    ins.then_inc(ctx.sem(sAV), 1)
                nullx = 1 if null_kcol is not None else 0
                pdone = pbase + len(steps) + nullx
                avdone += len(steps) + nullx
                if mode == "ot":
                    for qs in range(nsub):
                        gl = pi * nsub + qs
                        t.wait_ge(ctx.sem(sST), b0["ST"] + gl + 1)
                        if stg >= 1:
                            t.wait_ge(ctx.sem(sOE), b0["OE"] + stg - 1)
                        ins = t.transpose(ctx.ps_bf[:, (stg % 2) * 1024:
                                                    (stg % 2) * 1024 + P],
                                          hp_[:, stg_off + (gl % 2) * P:
                                              stg_off + (gl % 2) * P + P],
                                          ctx.ident[:, :])
                        ins.then_inc(ctx.sem(sTR), 1)
                        stg += 1

        @blk.scalar
        def _(s):
            stg = 0
            sdone = b0["S"]
            avbase = b0["AV"]
            for pi, (h, qig, steps) in enumerate(plan):
                for idx, ki, cols, masked in steps:
                    if idx >= 2:
                        s.wait_ge(ctx.sem(sAV), avbase + idx - 1)
                    elif pi >= 1:
                        s.wait_ge(ctx.sem(sAV), avbase)
                    s.wait_ge(ctx.sem(sS), sdone + idx + 1)
                    bias = bias_fn(qig, ki) if bias_fn is not None else 0.0
                    ins = s.activation(hp_[:, pt_off + (idx % 2) * qgw:
                                           pt_off + (idx % 2) * qgw + qgw],
                                       ctx.ps[:, (idx % 2) * 512:(idx % 2) * 512 + qgw],
                                       AF.Exp, bias=bias, scale=scale)
                    if not masked:
                        ins.then_inc(ctx.sem(sP), 1)
                    else:
                        ins.then_inc(ctx.sem(sPM), 1)
                if null_kcol is not None:
                    nidx = len(steps)
                    if nidx >= 2:
                        s.wait_ge(ctx.sem(sAV), avbase + nidx - 1)
                    s.wait_ge(ctx.sem(sS), sdone + nidx + 1)
                    ins = s.activation(hp_[0:1, pt_off + (nidx % 2) * qgw:
                                           pt_off + (nidx % 2) * qgw + qgw],
                                       ctx.ps[0:1, (nidx % 2) * 512:(nidx % 2) * 512 + qgw],
                                       AF.Exp, bias=0.0, scale=scale)
                    ins.then_inc(ctx.sem(sP), 1)
                    sdone += 1
                    avbase += 1
                sdone += len(steps)
                avbase += len(steps)
                if mode != "ot":
                    continue
                for qs in range(nsub):
                    gl = pi * nsub + qs
                    s.wait_ge(ctx.sem(sTR), b0["TR"] + gl + 1)
                    gh = 2 * hp_idx + h
                    chunk, rb = gh // 2, (gh % 2) * 64
                    qtile = qig * nsub + qs
                    if rb == 0:
                        dst = hp_[0:64, ot_off + chunk * T + qtile * P:
                                  ot_off + chunk * T + (qtile + 1) * P]
                    else:
                        dst = hp_[64:128, ot_off + chunk * T + qtile * P:
                                  ot_off + chunk * T + (qtile + 1) * P]
                    s.copy(dst, ctx.ps_bf[0:64, (stg % 2) * 1024:(stg % 2) * 1024 + P]
                           ).then_inc(ctx.sem(sOE), 1)
                    stg += 1

        @blk.vector
        def _(v):
            pmdone = b0["PM"]
            avdone = b0["AV"]
            for pi, (h, qig, steps) in enumerate(plan):
                for idx, ki, cols, masked in steps:
                    if masked:
                        v.wait_ge(ctx.sem(sPM), pmdone + 1)
                        pmdone += 1
                        nm_ = len(masked)
                        for mj, (dq, kind) in enumerate(masked):
                            mt = ctx.mtril if kind == "tril" else ctx.mstrict
                            pt = hp_[:, pt_off + (idx % 2) * qgw + dq * P:
                                     pt_off + (idx % 2) * qgw + (dq + 1) * P]
                            ins = v.tensor_tensor(pt, pt, mt[:, :], ALU.mult)
                        ins.then_inc(ctx.sem(sP), 1)
                avdone += len(steps) + (1 if null_kcol is not None else 0)
                v.wait_ge(ctx.sem(sAV), avdone)
                for qs in range(nsub):
                    gl = pi * nsub + qs
                    r = ctx.misc[:, 3090 + (gl % 2):3091 + (gl % 2)]
                    ins = v.reciprocal(r, ctx.ps[:, (2 + qs) * 512 + 64:
                                                (2 + qs) * 512 + 65])
                    ctx.edge(ins, v, "chV")
                    if mode == "ot":
                        if gl >= 2:
                            v.wait_ge(ctx.sem(sTR), b0["TR"] + gl - 1)
                        ins = v.tensor_scalar(hp_[:, stg_off + (gl % 2) * P:
                                                  stg_off + (gl % 2) * P + 64],
                                              ctx.ps[:, (2 + qs) * 512:
                                                     (2 + qs) * 512 + 64],
                                              r, None, ALU.mult)
                    else:
                        gh = 2 * hp_idx + h
                        qtile = qig * nsub + qs
                        xsl = ctx.x[:, qtile * D + gh * 64: qtile * D + gh * 64 + 64]
                        ins = v.scalar_tensor_tensor(xsl,
                                                     ctx.ps[:, (2 + qs) * 512:
                                                            (2 + qs) * 512 + 64],
                                                     r, xsl, ALU.mult, ALU.add)
                    ins.then_inc(ctx.sem(sST), 1)

    tot_ki = sum(len(st) for _, _, st in plan)
    tot_m = sum(1 for _, _, st in plan for _i, _k, _c, m in st if m)
    extra = len(plan) if null_kcol is not None else 0
    ctx.bump(sS, tot_ki + extra)
    ctx.bump(sP, tot_ki + extra)
    ctx.bump(sPM, tot_m)
    ctx.bump(sAV, tot_ki + extra)
    ctx.bump(sST, len(plan) * nsub)
    if mode == "ot":
        ctx.bump(sTR, len(plan) * nsub)
        ctx.bump(sOE, len(plan) * nsub)


def make_L1_ffn(L):
    pre = f"L{L}_ffn_"

    def host(inputs):
        return {
            f"{pre}WrT": np.ascontiguousarray(inputs[f"{pre}W_r"].T),
            f"{pre}WkT": np.ascontiguousarray(inputs[f"{pre}W_k"].T),
            f"{pre}WvT": np.ascontiguousarray(inputs[f"{pre}W_v"].T),
            f"{pre}mu_r": np.ascontiguousarray(np.asarray(inputs[f"{pre}mu_r"]).reshape(ND, P).T),
            f"{pre}mu_k": np.ascontiguousarray(np.asarray(inputs[f"{pre}mu_k"]).reshape(ND, P).T),
        }

    def decl(ctx):
        nc = ctx.nc
        for n in ["WrT", "WkT", "WvT"]:
            ctx.win[pre + n] = nc.dram_tensor(pre + n, [D, D], F32, kind="ExternalInput")
        for n in ["mu_r", "mu_k"]:
            ctx.win[pre + n] = nc.dram_tensor(pre + n, [P, ND], F32, kind="ExternalInput")

    def emit(ctx, uid):
        nc = ctx.nc
        XT, XRT, XKT, W = 0, 16384, 32768, 49152
        RTOK, KAT = 0, 16384
        MU_R, MU_K, MU1R, MU1K = 3400, 3410, 3420, 3430
        sW = "wdma"

        emit_ln(ctx, xt_off=XT, tag=uid)

        with nc.Block() as blk:
            @blk.gpsimd
            def _(g):
                load_cols(ctx, g, ctx.win[pre + "mu_r"], MU_R, ND, sW)
                load_cols(ctx, g, ctx.win[pre + "mu_k"], MU_K, ND, sW)
                load_w(ctx, g, ctx.win[pre + "WrT"], W, D, D, sW)
            wmu = ctx.val(sW)

            @blk.vector
            def _(v):
                sv = "chV"
                v.wait_ge(ctx.sem(sW), wmu)
                ms = ctx.misc
                v.tensor_scalar(ms[:, MU1R:MU1R + ND], ms[:, MU_R:MU_R + ND],
                                -1.0, 1.0, ALU.mult, ALU.add)
                v.tensor_scalar(ms[:, MU1K:MU1K + ND], ms[:, MU_K:MU_K + ND],
                                -1.0, 1.0, ALU.mult, ALU.add).then_inc(ctx.sem(sv), 1)
                v.wait_ge(ctx.sem(sv), ctx.bump(sv))
                hp = ctx.heap
                k = 0
                for j in range(ND):
                    src = hp[:, XT + j * T: XT + (j + 1) * T]
                    for dst_off, mu, mu1 in ((XRT, MU_R, MU1R), (XKT, MU_K, MU1K)):
                        dst = hp[:, dst_off + j * T: dst_off + (j + 1) * T]
                        v.memset(dst[:, 0:1], 0)
                        v.tensor_scalar(dst[:, 1:T], src[:, 0:T - 1],
                                        ms[:, mu1 + j:mu1 + j + 1], None,
                                        ALU.mult).then_inc(ctx.sem(sv), 1)
                        k += 1
                        v.wait_ge(ctx.sem(sv), ctx.bump(sv))
                        v.scalar_tensor_tensor(dst, src, ms[:, mu + j:mu + j + 1],
                                               dst, ALU.mult, ALU.add)

        def evac_r(i, ps, s):
            return s.activation(ctx.heap[:, RTOK + i * D: RTOK + (i + 1) * D], ps,
                                AF.Sigmoid, bias=0.0, scale=1.0)

        linearA(ctx, f"r{uid}", XRT, W, D, D, [(sW, ctx.val(sW))], evac_r)

        with nc.Block() as blk:
            @blk.gpsimd
            def _(g):
                load_w(ctx, g, ctx.win[pre + "WkT"], W, D, D, sW)

        def evac_k(mi, w, ps, s):
            return s.activation(ctx.heap[:, KAT + mi * T + w * 512: KAT + mi * T + w * 512 + 512],
                                ps, AF.Relu, bias=0.0, scale=1.0)

        kb = ctx.val("bev")
        linearB(ctx, f"k{uid}", XKT, W, D, D, [(sW, ctx.val(sW))], evac_k)
        with nc.Block() as blk:
            @blk.vector
            def _(v):
                hp = ctx.heap
                idx = 0
                for mi in range(ND):
                    for w in range(T // 512):
                        v.wait_ge(ctx.sem("bev"), kb + idx + 1)
                        dst = hp[:, KAT + mi * T + w * 512: KAT + mi * T + w * 512 + 512]
                        v.tensor_mul(dst, dst, dst)
                        idx += 1

        with nc.Block() as blk:
            @blk.gpsimd
            def _(g):
                load_w(ctx, g, ctx.win[pre + "WvT"], W, D, D, sW)
        wv = ctx.val(sW)

        sMM, sS, sF = "omm", "oev", "ofn"
        bMM4, bS4, bF4 = ctx.val(sMM), ctx.val(sS), ctx.val(sF)
        with nc.Block() as blk:
            @blk.tensor
            def _(t):
                hp = ctx.heap
                t.wait_ge(ctx.sem(sW), wv)
                for i in range(NT):
                    if i >= 2:
                        t.wait_ge(ctx.sem(sF), bF4 + i - 1)
                    b0 = (i % 2) * 2
                    for j in range(ND):
                        lhsT = hp[:, KAT + j * T + i * P: KAT + j * T + (i + 1) * P]
                        for n in range(2):
                            ins = t.matmul(ctx.ps[:, (b0 + n) * 512:(b0 + n + 1) * 512],
                                           lhsT,
                                           hp[:, W + j * D + n * 512: W + j * D + (n + 1) * 512],
                                           start=(j == 0), stop=(j == ND - 1))
                    ins.then_inc(ctx.sem(sMM), 1)
            ctx.bump(sMM, NT)

            @blk.scalar
            def _(s):
                hp = ctx.heap
                for i in range(NT):
                    pa = i % 2
                    if i >= 2:
                        s.wait_ge(ctx.sem(sF), bF4 + i - 1)
                    s.wait_ge(ctx.sem(sMM), bMM4 + i + 1)
                    o1 = hp[:, ctx.ln_tmp_off + pa * D: ctx.ln_tmp_off + (pa + 1) * D]
                    s.copy(o1, ctx.ps[:, pa * 1024: pa * 1024 + 1024]).then_inc(ctx.sem(sS), 1)
            ctx.bump(sS, NT)

            @blk.vector
            def _(v):
                hp = ctx.heap
                for i in range(NT):
                    pa = i % 2
                    v.wait_ge(ctx.sem(sS), bS4 + i + 1)
                    o1 = hp[:, ctx.ln_tmp_off + pa * D: ctx.ln_tmp_off + (pa + 1) * D]
                    o3 = ctx.misc[:, 1024 + pa * D: 1024 + (pa + 1) * D]
                    ins = v.tensor_tensor(o3, o1,
                                          hp[:, RTOK + i * D: RTOK + (i + 1) * D],
                                          ALU.mult)
                    ctx.edge(ins, v, "chV")
                    v.tensor_add(ctx.x[:, i * D:(i + 1) * D],
                                 ctx.x[:, i * D:(i + 1) * D],
                                 o3).then_inc(ctx.sem(sF), 1)
            ctx.bump(sF, NT)

    return decl, host, emit



def make_mha(L, slot="attn_a", scale=0.125):
    pre = f"L{L}_{slot}_"

    def host(inputs):
        return {
            f"{pre}WqT": np.ascontiguousarray(np.asarray(inputs[f"{pre}Wq"]).T),
            f"{pre}WkT": np.ascontiguousarray(np.asarray(inputs[f"{pre}Wk"]).T),
            f"{pre}WvT": np.ascontiguousarray(np.asarray(inputs[f"{pre}Wv"]).T),
            f"{pre}WoT": np.ascontiguousarray(np.asarray(inputs[f"{pre}Wo"]).T),
            f"{pre}bqc": np.ascontiguousarray(np.asarray(inputs[f"{pre}bq"]).reshape(ND, P).T),
            f"{pre}bkc": np.ascontiguousarray(np.asarray(inputs[f"{pre}bk"]).reshape(ND, P).T),
            f"{pre}bvr": np.ascontiguousarray(np.asarray(inputs[f"{pre}bv"]).reshape(1, D)),
            f"{pre}bor": np.ascontiguousarray(np.asarray(inputs[f"{pre}bo"]).reshape(1, D)),
        }

    def decl(ctx):
        nc = ctx.nc
        for n in ["WqT", "WkT", "WvT", "WoT"]:
            ctx.win[pre + n] = nc.dram_tensor(pre + n, [D, D], F32, kind="ExternalInput")
        ctx.win[pre + "bqc"] = nc.dram_tensor(pre + "bqc", [P, ND], F32, kind="ExternalInput")
        ctx.win[pre + "bkc"] = nc.dram_tensor(pre + "bkc", [P, ND], F32, kind="ExternalInput")
        ctx.win[pre + "bvr"] = nc.dram_tensor(pre + "bvr", [1, D], F32, kind="ExternalInput")
        ctx.win[pre + "bor"] = nc.dram_tensor(pre + "bor", [1, D], F32, kind="ExternalInput")

    def emit(ctx, uid):
        emit_mha_body(ctx, uid, pre, scale, causal=False)

    return decl, host, emit


def emit_mha_body(ctx, uid, pre, scale, causal, has_bias=True, kv_is_q=False,
                  fused_qkv=False, has_out=True, mask_fn=None, kset_fn=None):
    nc = ctx.nc
    XT, OT = 0, 16384
    Q2T, K2T = 32768, 34816
    V2 = 36864
    PT, STG = 38944, 39968
    WO, WQ2, WK2, WV2 = 40960, 49152, 50176, 51200
    BVROW, BOROW = 52224, 53248
    BQC, BKC = 3200, 3208
    sW = "wdma"

    emit_ln(ctx, xt_off=XT, tag=uid)

    with nc.Block() as blk:
        @blk.gpsimd
        def _(g):
            if has_out:
                load_w(ctx, g, ctx.win[pre + "WoT"], WO, D, D, sW)
                if has_bias:
                    load_row(ctx, g, ctx.win[pre + "bor"], BOROW, D, sW)
            if has_bias:
                load_cols(ctx, g, ctx.win[pre + "bqc"], BQC, ND, sW)
                load_cols(ctx, g, ctx.win[pre + "bkc"], BKC, ND, sW)
                if not kv_is_q:
                    load_row(ctx, g, ctx.win[pre + "bvr"], BVROW, D, sW)
                else:
                    load_row(ctx, g, ctx.win[pre + "bkr"], BVROW, D, sW)

        @blk.vector
        def _(v):
            hp_ = ctx.heap
            v.memset(bass.AP(hp_, V2 + 64, [[hp_.shape[1], P], [65, 2 * NT], [1, 1]]), 1.0)

    for hpair in range(ND):
        with nc.Block() as blk:
            @blk.gpsimd
            def _(g):
                if fused_qkv:
                    wt = ctx.win[pre + "WT"]
                    load_w(ctx, g, wt[:, hpair * P:(hpair + 1) * P], WQ2, D, P, sW)
                    load_w(ctx, g, wt[:, D + hpair * P: D + (hpair + 1) * P], WK2, D, P, sW)
                    load_w(ctx, g, wt[:, 2 * D + hpair * P: 2 * D + (hpair + 1) * P], WV2, D, P, sW)
                else:
                    load_w(ctx, g, ctx.win[pre + "WqT"][:, hpair * P:(hpair + 1) * P],
                           WQ2, D, P, sW)
                    load_w(ctx, g, ctx.win[pre + "WkT"][:, hpair * P:(hpair + 1) * P],
                           WK2, D, P, sW)
                    if not kv_is_q:
                        load_w(ctx, g, ctx.win[pre + "WvT"][:, hpair * P:(hpair + 1) * P],
                               WV2, D, P, sW)
        wv = ctx.val(sW)

        def evac_q(mi, w, ps, s):
            return s.activation(ctx.heap[:, Q2T + w * 512: Q2T + (w + 1) * 512], ps,
                                AF.Identity,
                                bias=(ctx.misc[:, BQC + hpair:BQC + hpair + 1]
                                      if has_bias else 0.0), scale=1.0)

        linearB(ctx, f"q{uid}_{hpair}", XT, WQ2, D, P, [(sW, wv)], evac_q)

        def evac_k(mi, w, ps, s):
            return s.activation(ctx.heap[:, K2T + w * 512: K2T + (w + 1) * 512], ps,
                                AF.Identity,
                                bias=(ctx.misc[:, BKC + hpair:BKC + hpair + 1]
                                      if has_bias else 0.0), scale=1.0)

        linearB(ctx, f"k{uid}_{hpair}", XT, WK2, D, P, [(sW, wv)], evac_k)

        def evac_v(i, ps, s):
            hp_ = ctx.heap
            dst = bass.AP(hp_, V2 + i * 130, [[hp_.shape[1], P], [65, 2], [1, 64]])
            return s.copy(dst, ps[:, 0:P])

        linearA(ctx, f"v{uid}_{hpair}", XT, WK2 if kv_is_q else WV2, D, P,
                [(sW, wv)], evac_v,
                bias_row=(BVROW + hpair * P) if has_bias else None)

        emit_attn_pair(ctx, f"at{uid}_{hpair}", hpair, Q2T, K2T, V2, OT,
                       scale, PT, STG, causal=causal,
                       mode=("ot" if has_out else "resid"),
                       mask_fn=mask_fn, kset_fn=kset_fn)

    if has_out:
        def evac_o(i, ps, veng):
            return veng.tensor_tensor(ctx.x[:, i * D:(i + 1) * D], ps,
                                      ctx.x[:, i * D:(i + 1) * D], ALU.add)

        linearA(ctx, f"o{uid}", OT, WO, D, D, [(sW, ctx.val(sW))], evac_o,
                bias_row=BOROW if has_bias else None, evac_eng="vector")


def make_causal_mha_fused(L):
    pre = f"L{L}_attn_b_"

    def host(inputs):
        W = np.asarray(inputs[f"{pre}W"])
        b = np.asarray(inputs[f"{pre}b"])
        return {
            f"{pre}WT": np.ascontiguousarray(W.T),
            f"{pre}bqc": np.ascontiguousarray(b[0:D].reshape(ND, P).T),
            f"{pre}bkc": np.ascontiguousarray(b[D:2 * D].reshape(ND, P).T),
            f"{pre}bvr": np.ascontiguousarray(b[2 * D:3 * D].reshape(1, D)),
        }

    def decl(ctx):
        nc = ctx.nc
        ctx.win[pre + "WT"] = nc.dram_tensor(pre + "WT", [D, 3 * D], F32, kind="ExternalInput")
        ctx.win[pre + "bqc"] = nc.dram_tensor(pre + "bqc", [P, ND], F32, kind="ExternalInput")
        ctx.win[pre + "bkc"] = nc.dram_tensor(pre + "bkc", [P, ND], F32, kind="ExternalInput")
        ctx.win[pre + "bvr"] = nc.dram_tensor(pre + "bvr", [1, D], F32, kind="ExternalInput")

    def emit(ctx, uid):
        emit_mha_body(ctx, uid, pre, 0.125, causal=True, has_bias=True,
                      fused_qkv=True, has_out=False)

    return decl, host, emit


def make_sharedkv_mha(L):
    pre = f"L{L}_attn_b_"

    def host(inputs):
        return {
            f"{pre}WqT": np.ascontiguousarray(np.asarray(inputs[f"{pre}Wq"])),
            f"{pre}WkT": np.ascontiguousarray(np.asarray(inputs[f"{pre}Wk"])),
            f"{pre}bqc": np.ascontiguousarray(np.asarray(inputs[f"{pre}bq"]).reshape(ND, P).T),
            f"{pre}bkc": np.ascontiguousarray(np.asarray(inputs[f"{pre}bk"]).reshape(ND, P).T),
            f"{pre}bkr": np.ascontiguousarray(np.asarray(inputs[f"{pre}bk"]).reshape(1, D)),
        }

    def decl(ctx):
        nc = ctx.nc
        for n in ["WqT", "WkT"]:
            ctx.win[pre + n] = nc.dram_tensor(pre + n, [D, D], F32, kind="ExternalInput")
        for n in ["bqc", "bkc"]:
            ctx.win[pre + n] = nc.dram_tensor(pre + n, [P, ND], F32, kind="ExternalInput")
        ctx.win[pre + "bkr"] = nc.dram_tensor(pre + "bkr", [1, D], F32, kind="ExternalInput")

    def emit(ctx, uid):
        emit_mha_body(ctx, uid, pre, 1.0 / 32.0, causal=False, has_bias=True,
                      kv_is_q=True, has_out=False)

    return decl, host, emit


def make_swa_mha(L):
    pre = f"L{L}_attn_b_"

    def host(inputs):
        return {f"{pre}{n}T": np.ascontiguousarray(np.asarray(inputs[f"{pre}{n}"]).T)
                for n in ["Wq", "Wk", "Wv", "Wo"]}

    def decl(ctx):
        nc = ctx.nc
        for n in ["WqT", "WkT", "WvT", "WoT"]:
            ctx.win[pre + n] = nc.dram_tensor(pre + n, [D, D], F32, kind="ExternalInput")

    def kset_fn(qig):
        return range(max(0, qig * 4 - 4), qig * 4 + 4)

    def mask_fn(qig, ki):
        cols, masks = [], []
        for qs in range(4):
            qtile = qig * 4 + qs
            if ki < qtile - 4 or ki > qtile:
                continue
            cols.append(qs)
            if ki == qtile:
                masks.append((qs, "tril"))
            elif ki == qtile - 4:
                masks.append((qs, "strict"))
        return cols, masks

    def emit(ctx, uid):
        emit_mha_body(ctx, uid, pre, 0.125, causal=True, has_bias=False,
                      has_out=True, mask_fn=mask_fn, kset_fn=kset_fn)

    return decl, host, emit



def make_alibi_attn(L):
    pre = f"L{L}_attn_a_"

    def host(inputs):
        return {
            f"{pre}Wqk": np.ascontiguousarray(np.asarray(inputs[f"{pre}W_qk"])),
            f"{pre}WvT": np.ascontiguousarray(np.asarray(inputs[f"{pre}W_v"]).T),
            f"{pre}slope": np.ascontiguousarray(
                np.tile(np.asarray(inputs[f"{pre}alibi_slope"]).reshape(1, 1), (P, 1))),
        }

    def decl(ctx):
        nc = ctx.nc
        ctx.win[pre + "Wqk"] = nc.dram_tensor(pre + "Wqk", [D, D], F32, kind="ExternalInput")
        ctx.win[pre + "WvT"] = nc.dram_tensor(pre + "WvT", [D, D], F32, kind="ExternalInput")
        ctx.win[pre + "slope"] = nc.dram_tensor(pre + "slope", [P, 1], F32, kind="ExternalInput")

    def emit(ctx, uid):
        nc = ctx.nc
        hp_ = ctx.heap
        XT, QT, V2, W, PT = 0, 16384, 32768, 49536, 49280
        AL = 3300
        SL, SA, ST1 = 3320, 3321, 3322
        IOTA = 3098
        SCALE = 1.0 / 32.0
        sW = "wdma"

        emit_ln(ctx, xt_off=XT, tag=uid)

        with nc.Block() as blk:
            @blk.gpsimd
            def _(g):
                load_cols(ctx, g, ctx.win[pre + "slope"], SL, 1, sW)
                load_w(ctx, g, ctx.win[pre + "Wqk"], W, D, D, sW)
            wsl = ctx.val(sW)

            @blk.vector
            def _(v):
                ms = ctx.misc
                v.wait_ge(ctx.sem(sW), wsl)
                ins = v.tensor_scalar(ms[:, ST1:ST1 + 1], ms[:, IOTA:IOTA + 1],
                                      -127.0, None, ALU.add)
                ctx.edge(ins, v, "chV")
                ins = v.tensor_tensor(ms[:, SA:SA + 1], ms[:, ST1:ST1 + 1],
                                      ms[:, SL:SL + 1], ALU.mult)
                ctx.edge(ins, v, "chV")
                for dd in range(16):
                    ins = v.tensor_scalar(ms[:, ST1:ST1 + 1], ms[:, SL:SL + 1],
                                          float(128 * (dd - 15)), None, ALU.mult)
                    ctx.edge(ins, v, "chV")
                    ins = v.tensor_tensor(ms[:, AL + dd:AL + dd + 1],
                                          ms[:, ST1:ST1 + 1], ms[:, SA:SA + 1],
                                          ALU.add)
                    ctx.edge(ins, v, "chV")

        def evac_qt(mi, w, ps, s):
            return s.copy(ctx.heap[:, QT + mi * T + w * 512: QT + mi * T + w * 512 + 512], ps)

        linearB(ctx, f"qt{uid}", XT, W, D, D, [(sW, ctx.val(sW))], evac_qt)

        with nc.Block() as blk:
            @blk.gpsimd
            def _(g):
                load_w(ctx, g, ctx.win[pre + "WvT"], W, D, D, sW)

            @blk.vector
            def _(v):
                v.memset(bass.AP(hp_, V2 + 1024, [[hp_.shape[1], P], [1032, NT], [1, 1]]), 1.0)

        def evac_v(i, ps, s):
            return s.copy(ctx.heap[:, V2 + i * 1032: V2 + i * 1032 + 1024], ps)

        linearA(ctx, f"v{uid}", XT, W, D, D, [(sW, ctx.val(sW))], evac_v)

        sS, sP, sPM, sAV, sDV = "atS", "atP", "atPM", "atAV", "atST"
        bS, bP, bPM, bAV, bDV = (ctx.val(sS), ctx.val(sP), ctx.val(sPM),
                                 ctx.val(sAV), ctx.val(sDV))
        with nc.Block() as blk:
            @blk.tensor
            def _(t):
                nS = bS
                pdone = bP
                for i in range(NT):
                    pbase = pdone
                    if i >= 1:
                        t.wait_ge(ctx.sem(sDV), bDV + 2 * i)
                    for ki in range(i + 1):
                        if ki >= 2:
                            t.wait_ge(ctx.sem(sP), pbase + ki - 1)
                        for j in range(ND):
                            ins = t.matmul(ctx.ps[:, (ki % 2) * 512:(ki % 2) * 512 + P],
                                           hp_[:, XT + j * T + ki * P: XT + j * T + (ki + 1) * P],
                                           hp_[:, QT + j * T + i * P: QT + j * T + (i + 1) * P],
                                           start=(j == 0), stop=(j == ND - 1))
                        ins.then_inc(ctx.sem(sS), 1); nS += 1
                        t.wait_ge(ctx.sem(sP), pbase + ki + 1)
                        for c, (o0, w0) in enumerate([(0, 512), (512, 512), (1024, 1)]):
                            ins = t.matmul(ctx.ps[:, (2 + c) * 512:(2 + c) * 512 + w0],
                                           hp_[:, PT + (ki % 2) * P: PT + (ki % 2 + 1) * P],
                                           hp_[:, V2 + ki * 1032 + o0: V2 + ki * 1032 + o0 + w0],
                                           start=(ki == 0), stop=(ki == i))
                        ins.then_inc(ctx.sem(sAV), 1)
                    pdone = pbase + i + 1

            @blk.scalar
            def _(s):
                sdone = bS
                avbase = bAV
                for i in range(NT):
                    for ki in range(i + 1):
                        if ki >= 2:
                            s.wait_ge(ctx.sem(sAV), avbase + ki - 1)
                        elif i >= 1:
                            s.wait_ge(ctx.sem(sAV), avbase)
                        s.wait_ge(ctx.sem(sS), sdone + ki + 1)
                        ins = s.activation(hp_[:, PT + (ki % 2) * P: PT + (ki % 2 + 1) * P],
                                           ctx.ps[:, (ki % 2) * 512:(ki % 2) * 512 + P],
                                           AF.Exp,
                                           bias=ctx.misc[:, AL + ki - i + 15:AL + ki - i + 16],
                                           scale=SCALE)
                        if ki == i:
                            ins.then_inc(ctx.sem(sPM), 1)
                        else:
                            ins.then_inc(ctx.sem(sP), 1)
                    sdone += i + 1
                    avbase += i + 1

            @blk.vector
            def _(v):
                pm = bPM
                av = bAV
                for i in range(NT):
                    v.wait_ge(ctx.sem(sPM), pm + 1); pm += 1
                    pt = hp_[:, PT + (i % 2) * P: PT + (i % 2 + 1) * P]
                    v.tensor_tensor(pt, pt, ctx.mtril[:, :], ALU.mult).then_inc(ctx.sem(sP), 1)
                    av += i + 1
                    v.wait_ge(ctx.sem(sAV), av)
                    r = ctx.misc[:, 3090:3091]
                    ins = v.reciprocal(r, ctx.ps[:, 4 * 512: 4 * 512 + 1])
                    ctx.edge(ins, v, "chV")
                    for c in range(2):
                        xsl = ctx.x[:, i * D + c * 512: i * D + (c + 1) * 512]
                        ins = v.scalar_tensor_tensor(xsl,
                                                     ctx.ps[:, (2 + c) * 512:(3 + c) * 512],
                                                     r, xsl, ALU.mult, ALU.add)
                        ins.then_inc(ctx.sem(sDV), 1)

        ctx.bump(sS, 136); ctx.bump(sP, 136); ctx.bump(sPM, 16)
        ctx.bump(sAV, 136); ctx.bump(sDV, 32)

    return decl, host, emit


def make_performer(L):
    pre = f"L{L}_attn_a_"

    def host(inputs):
        return {
            f"{pre}WqT": np.ascontiguousarray(np.asarray(inputs[f"{pre}Wq"]).T),
            f"{pre}WkT": np.ascontiguousarray(np.asarray(inputs[f"{pre}Wk"]).T),
            f"{pre}WvT": np.ascontiguousarray(np.asarray(inputs[f"{pre}Wv"]).T),
            f"{pre}WoT": np.ascontiguousarray(np.asarray(inputs[f"{pre}Wo"]).T),
            f"{pre}R": np.ascontiguousarray(np.asarray(inputs[f"{pre}R"])),
            f"{pre}bqc": np.ascontiguousarray(np.asarray(inputs[f"{pre}bq"]).reshape(ND, P).T),
            f"{pre}bkc": np.ascontiguousarray(np.asarray(inputs[f"{pre}bk"]).reshape(ND, P).T),
            f"{pre}bvr": np.ascontiguousarray(np.asarray(inputs[f"{pre}bv"]).reshape(1, D)),
            f"{pre}bkr": np.ascontiguousarray(np.asarray(inputs[f"{pre}bk"]).reshape(1, D)),
            f"{pre}bor": np.ascontiguousarray(np.asarray(inputs[f"{pre}bo"]).reshape(1, D)),
        }

    def decl(ctx):
        nc = ctx.nc
        for n in ["WqT", "WkT", "WvT", "WoT"]:
            ctx.win[pre + n] = nc.dram_tensor(pre + n, [D, D], F32, kind="ExternalInput")
        ctx.win[pre + "R"] = nc.dram_tensor(pre + "R", [64, 64], F32, kind="ExternalInput")
        for n in ["bqc", "bkc"]:
            ctx.win[pre + n] = nc.dram_tensor(pre + n, [P, ND], F32, kind="ExternalInput")
        for n in ["bvr", "bkr", "bor"]:
            ctx.win[pre + n] = nc.dram_tensor(pre + n, [1, D], F32, kind="ExternalInput")

    def emit(ctx, uid):
        nc = ctx.nc
        hp_ = ctx.heap
        XT, OT = 0, 16384
        A, B, C, Dv2 = 32768, 34816, 38912, 40960
        STG, KV, RW = 43040, 43296, 43456
        BVR, BOR, BKR = 43520, 44544, 45568
        WO, WP = 46592, 54784
        BQC, BKC = 3200, 3208
        SQ = 3340
        sW = "wdma"

        emit_ln(ctx, xt_off=XT, tag=uid)

        with nc.Block() as blk:
            @blk.gpsimd
            def _(g):
                load_w(ctx, g, ctx.win[pre + "WoT"], WO, D, D, sW)
                g.dma_start(bass.AP(hp_, RW, [[hp_.shape[1], 64], [1, 64]]),
                            ctx.win[pre + "R"][:, :]).then_inc(ctx.sem(sW), 16)
                ctx.bump(sW, 16)
                dst2 = bass.AP(hp_, RW, [[hp_.shape[1], 64], [1, 64]])
                dst2 = hp_[64:128, RW:RW + 64]
                g.dma_start(dst2, ctx.win[pre + "R"][:, :]).then_inc(ctx.sem(sW), 16)
                ctx.bump(sW, 16)
                load_cols(ctx, g, ctx.win[pre + "bqc"], BQC, ND, sW)
                load_cols(ctx, g, ctx.win[pre + "bkc"], BKC, ND, sW)
                load_row(ctx, g, ctx.win[pre + "bvr"], BVR, D, sW)
                load_row(ctx, g, ctx.win[pre + "bkr"], BKR, D, sW)
                load_row(ctx, g, ctx.win[pre + "bor"], BOR, D, sW)

            @blk.vector
            def _(v):
                v.memset(bass.AP(hp_, Dv2 + 64, [[hp_.shape[1], P], [65, 2 * NT], [1, 1]]), 1.0)
                v.memset(hp_[0:64, KV + 64: KV + 65], 1.0)
                v.memset(hp_[0:64, KV + 68 + 64: KV + 68 + 65], 1.0)

        for hpair in range(ND):
            with nc.Block() as blk:
                @blk.gpsimd
                def _(g):
                    load_w(ctx, g, ctx.win[pre + "WqT"][:, hpair * P:(hpair + 1) * P],
                           WP, D, P, sW)
            wv = ctx.val(sW)

            def evac_q(mi, w, ps, s):
                return s.activation(hp_[:, A + w * 512: A + (w + 1) * 512], ps,
                                    AF.Identity,
                                    bias=ctx.misc[:, BQC + hpair:BQC + hpair + 1],
                                    scale=1.0)

            linearB(ctx, f"q{uid}_{hpair}", XT, WP, D, P, [(sW, wv)], evac_q)

            bS = ctx.val("atS")
            bP = ctx.val("atP")
            with nc.Block() as blk:
                @blk.tensor
                def _(t):
                    nq = 0
                    for h in range(2):
                        for w in range(4):
                            if nq >= 2:
                                t.wait_ge(ctx.sem("atP"), bP + nq - 1)
                            ins = t.matmul(ctx.ps[0:64, (nq % 2) * 512:(nq % 2) * 512 + 512],
                                           (hp_[0:64, RW:RW + 64] if h == 0
                                            else hp_[64:128, RW:RW + 64]),
                                           (hp_[0:64, A + w * 512: A + (w + 1) * 512] if h == 0
                                            else hp_[64:128, A + w * 512: A + (w + 1) * 512]),
                                           start=True, stop=True)
                            ins.then_inc(ctx.sem("atS"), 1)
                            nq += 1

                @blk.scalar
                def _(s):
                    nq = 0
                    for h in range(2):
                        for w in range(4):
                            s.wait_ge(ctx.sem("atS"), bS + nq + 1)
                            dst = hp_[0:64, B + h * 2048 + w * 512: B + h * 2048 + (w + 1) * 512]
                            s.activation(dst, ctx.ps[0:64, (nq % 2) * 512:(nq % 2) * 512 + 512],
                                         AF.Exp, bias=0.0,
                                         scale=1.0 / 8.0).then_inc(ctx.sem("atP"), 1)
                            nq += 1
            ctx.bump("atS", 8)
            ctx.bump("atP", 8)

            with nc.Block() as blk:
                @blk.gpsimd
                def _(g):
                    load_w(ctx, g, ctx.win[pre + "WkT"][:, hpair * P:(hpair + 1) * P],
                           WP, D, P, sW)
                    load_w(ctx, g, ctx.win[pre + "WvT"][:, hpair * P:(hpair + 1) * P],
                           WP + 1024, D, P, sW)
            wv = ctx.val(sW)

            def evac_k(mi, w, ps, s):
                return s.activation(hp_[:, A + w * 512: A + (w + 1) * 512], ps,
                                    AF.Identity,
                                    bias=ctx.misc[:, BKC + hpair:BKC + hpair + 1],
                                    scale=1.0)

            linearB(ctx, f"k{uid}_{hpair}", XT, WP, D, P, [(sW, wv)], evac_k)

            def evac_ktok(i, ps, s):
                return s.copy(hp_[:, C + i * P: C + (i + 1) * P], ps)

            linearA(ctx, f"kt{uid}_{hpair}", XT, WP, D, P, [(sW, wv)], evac_ktok,
                    bias_row=BKR + hpair * P)

            def evac_v(i, ps, s):
                dst = bass.AP(hp_, Dv2 + i * 130, [[hp_.shape[1], P], [65, 2], [1, 64]])
                return s.copy(dst, ps[:, 0:P])

            linearA(ctx, f"v{uid}_{hpair}", XT, WP + 1024, D, P, [(sW, wv)], evac_v,
                    bias_row=BVR + hpair * P)

            b1 = ctx.val("chS")
            b2 = ctx.val("pfB")
            with nc.Block() as blk:
                @blk.scalar
                def _(s):
                    for i in range(NT):
                        for h in range(2):
                            src = hp_[:, C + i * P + h * 64: C + i * P + h * 64 + 64]
                            ins = s.activation(ctx.misc[:, 0:64], src, AF.Square,
                                               bias=0.0, scale=1.0,
                                               accum_out=ctx.misc[:, SQ + 32 + i * 2 + h:
                                                                  SQ + 33 + i * 2 + h])
                            ctx.edge(ins, s, "chS")

                @blk.vector
                def _(v):
                    for i in range(NT):
                        for h in range(2):
                            v.wait_ge(ctx.sem("chS"), b1 + i * 2 + h + 1)
                            cc = SQ + 32 + i * 2 + h
                            v.tensor_scalar(ctx.misc[:, SQ + i * 2 + h:SQ + i * 2 + h + 1],
                                            ctx.misc[:, cc:cc + 1],
                                            -1.0 / 128.0, None,
                                            ALU.mult).then_inc(ctx.sem("pfB"), 1)
            ctx.bump("pfB", 32)

            bS = ctx.val("atS")
            bP = ctx.val("atP")
            with nc.Block() as blk:
                @blk.tensor
                def _(t):
                    nq = 0
                    for h in range(2):
                        for i in range(NT):
                            if nq >= 2:
                                t.wait_ge(ctx.sem("atP"), bP + nq - 1)
                            ins = t.matmul(ctx.ps[:, (nq % 2) * 512:(nq % 2) * 512 + 64],
                                           (hp_[0:64, A + i * P: A + (i + 1) * P] if h == 0
                                            else hp_[64:128, A + i * P: A + (i + 1) * P]),
                                           (hp_[0:64, RW:RW + 64] if h == 0
                                            else hp_[64:128, RW:RW + 64]),
                                           start=True, stop=True)
                            ins.then_inc(ctx.sem("atS"), 1)
                            nq += 1

                @blk.scalar
                def _(s):
                    nq = 0
                    for h in range(2):
                        for i in range(NT):
                            s.wait_ge(ctx.sem("atS"), bS + nq + 1)
                            s.activation(hp_[:, C + i * P + h * 64: C + i * P + h * 64 + 64],
                                         ctx.ps[:, (nq % 2) * 512:(nq % 2) * 512 + 64],
                                         AF.Exp,
                                         bias=ctx.misc[:, SQ + i * 2 + h:SQ + i * 2 + h + 1],
                                         scale=1.0 / 8.0).then_inc(ctx.sem("atP"), 1)
                            nq += 1
            ctx.bump("atS", 32)
            ctx.bump("atP", 32)

            bAV = ctx.val("atAV")
            bKVe = ctx.val("atPM")
            bS = ctx.val("atS")
            bDV = ctx.val("atST")
            bTR = ctx.val("atTR")
            bOE = ctx.val("atOE")
            with nc.Block() as blk:
                @blk.tensor
                def _(t):
                    for h in range(2):
                        ins = None
                        for i in range(NT):
                            ins = t.matmul(ctx.ps[0:64, (4 + h) * 512:(4 + h) * 512 + 64],
                                           hp_[:, C + i * P + h * 64: C + i * P + h * 64 + 64],
                                           hp_[:, Dv2 + i * 130 + h * 65: Dv2 + i * 130 + h * 65 + 64],
                                           start=(i == 0), stop=(i == NT - 1))
                        ins.then_inc(ctx.sem("atAV"), 1)
                    def tr(g2):
                        t.wait_ge(ctx.sem("atST"), bDV + g2 + 1)
                        if g2 >= 1:
                            t.wait_ge(ctx.sem("atOE"), bOE + g2)
                        ins2 = t.transpose(ctx.ps_bf[:, (g2 % 2) * 1024:(g2 % 2) * 1024 + P],
                                           hp_[:, STG + (g2 % 2) * P: STG + (g2 % 2) * P + P],
                                           ctx.ident[:, :])
                        ins2.then_inc(ctx.sem("atTR"), 1)
                    oidx = 0
                    for h in range(2):
                        t.wait_ge(ctx.sem("atPM"), bKVe + h + 1)
                        for i in range(NT):
                            ins = t.matmul(ctx.ps[:, (oidx % 2) * 512:(oidx % 2) * 512 + 65],
                                           hp_[0:64, B + h * 2048 + i * P: B + h * 2048 + (i + 1) * P],
                                           hp_[0:64, KV + h * 68: KV + h * 68 + 65],
                                           start=True, stop=True)
                            ins.then_inc(ctx.sem("atS"), 1)
                            if oidx >= 1:
                                tr(oidx - 1)
                            oidx += 1
                    tr(31)

                @blk.scalar
                def _(s):
                    for h in range(2):
                        s.wait_ge(ctx.sem("atAV"), bAV + h + 1)
                        s.copy(hp_[0:64, KV + h * 68: KV + h * 68 + 64],
                               ctx.ps[0:64, (4 + h) * 512:(4 + h) * 512 + 64]
                               ).then_inc(ctx.sem("atPM"), 1)
                    for g2 in range(32):
                        s.wait_ge(ctx.sem("atTR"), bTR + g2 + 1)
                        h = g2 // 16
                        i = g2 % 16
                        gh = 2 * hpair + h
                        chunk, rb = gh // 2, (gh % 2) * 64
                        if rb == 0:
                            dst = hp_[0:64, OT + chunk * T + i * P: OT + chunk * T + (i + 1) * P]
                        else:
                            dst = hp_[64:128, OT + chunk * T + i * P: OT + chunk * T + (i + 1) * P]
                        s.copy(dst, ctx.ps_bf[0:64, (g2 % 2) * 1024:(g2 % 2) * 1024 + P]
                               ).then_inc(ctx.sem("atOE"), 1)

                @blk.vector
                def _(v):
                    g2 = 0
                    for h in range(2):
                        for i in range(NT):
                            v.wait_ge(ctx.sem("atS"), bS + g2 + 1)
                            r = ctx.misc[:, 3090 + (g2 % 2):3091 + (g2 % 2)]
                            ins = v.tensor_scalar(r, ctx.ps[:, (g2 % 2) * 512 + 64:
                                                           (g2 % 2) * 512 + 65],
                                                  1e-8, None, ALU.add)
                            ctx.edge(ins, v, "chV")
                            ins = v.reciprocal(r, r)
                            ctx.edge(ins, v, "chV")
                            if g2 >= 2:
                                v.wait_ge(ctx.sem("atTR"), bTR + g2 - 1)
                            ins = v.tensor_scalar(hp_[:, STG + (g2 % 2) * P:
                                                      STG + (g2 % 2) * P + 64],
                                                  ctx.ps[:, (g2 % 2) * 512:
                                                         (g2 % 2) * 512 + 64],
                                                  r, None, ALU.mult)
                            ins.then_inc(ctx.sem("atST"), 1)
                            g2 += 1
            ctx.bump("atAV", 2)
            ctx.bump("atPM", 2)
            ctx.bump("atS", 32)
            ctx.bump("atST", 32)
            ctx.bump("atTR", 32)
            ctx.bump("atOE", 32)

        def evac_o(i, ps, veng):
            return veng.tensor_tensor(ctx.x[:, i * D:(i + 1) * D], ps,
                                      ctx.x[:, i * D:(i + 1) * D], ALU.add)

        linearA(ctx, f"o{uid}", OT, WO, D, D, [(sW, ctx.val(sW))], evac_o,
                bias_row=BOR, evac_eng="vector")

    return decl, host, emit



def make_linear_attn(L):
    pre = f"L{L}_attn_b_"

    def host(inputs):
        return {
            f"{pre}WT": np.ascontiguousarray(np.asarray(inputs[f"{pre}w_attn"]).T),
            f"{pre}WpT": np.ascontiguousarray(np.asarray(inputs[f"{pre}w_proj"]).T),
        }

    def decl(ctx):
        nc = ctx.nc
        ctx.win[pre + "WT"] = nc.dram_tensor(pre + "WT", [D, 3 * D], F32, kind="ExternalInput")
        ctx.win[pre + "WpT"] = nc.dram_tensor(pre + "WpT", [D, D], F32, kind="ExternalInput")

    def emit(ctx, uid):
        nc = ctx.nc
        hp_ = ctx.heap
        XT, OT = 0, 16384
        A, B, C, Dk = 32768, 34816, 38912, 40960
        V2, PT, STG, SZ = 45056, 47136, 47392, 47648
        W, WP = 47872, 56192
        sW = "wdma"

        emit_ln(ctx, xt_off=XT, tag=uid)

        with nc.Block() as blk:
            @blk.gpsimd
            def _(g):
                load_w(ctx, g, ctx.win[pre + "WpT"], W, D, D, sW)

            @blk.vector
            def _(v):
                v.memset(bass.AP(hp_, V2 + 64, [[hp_.shape[1], P], [65, 2 * NT], [1, 1]]), 1.0)

        for hpair in range(ND):
            for which, dst_tok, dstT, woff in (("q", A, B, 0), ("k", C, Dk, D)):
                with nc.Block() as blk:
                    @blk.gpsimd
                    def _(g):
                        load_w(ctx, g, ctx.win[pre + "WT"][:, woff + hpair * P: woff + (hpair + 1) * P],
                               WP, D, P, sW)
                wv = ctx.val(sW)

                def evac_t(i, ps, s, dt=dst_tok):
                    return s.copy(hp_[:, dt + i * P: dt + (i + 1) * P], ps)

                linearA(ctx, f"{which}{uid}_{hpair}", XT, WP, D, P, [(sW, wv)], evac_t)

                bC = ctx.val("chV")
                bS2 = ctx.val("chS")
                bTR = ctx.val("atTR")
                bOE = ctx.val("atOE")
                with nc.Block() as blk:
                    @blk.vector
                    def _(v):
                        for sl in range(4):
                            src = hp_[:, dst_tok + sl * 512: dst_tok + (sl + 1) * 512]
                            scr1 = ctx.misc[:, 0:512]
                            scr2 = ctx.misc[:, 512:1024]
                            ins = v.tensor_scalar_min(scr1, src, 0.0)
                            ctx.edge(ins, v, "chV")
                            v.wait_ge(ctx.sem("chS"), bS2 + sl + 1)
                            ins = v.tensor_scalar_max(scr2, src, 0.0)
                            ctx.edge(ins, v, "chV")
                            ins = v.tensor_add(src, ctx.misc[:, 1024:1536], scr2)
                            ins.then_inc(ctx.sem("atP"), 1)

                    @blk.scalar
                    def _(s):
                        for sl in range(4):
                            s.wait_ge(ctx.sem("chV"), bC + 2 * sl + 1)
                            ins = s.activation(ctx.misc[:, 1024:1536], ctx.misc[:, 0:512],
                                               AF.Exp, bias=0.0, scale=1.0)
                            ctx.edge(ins, s, "chS")

                    @blk.tensor
                    def _(t):
                        bP2 = ctx.val("atP")
                        for i in range(NT):
                            for h in range(2):
                                g2 = i * 2 + h
                                sl_needed = (i // 4) + 1
                                t.wait_ge(ctx.sem("atP"), bP2 + sl_needed)
                                if g2 >= 1:
                                    t.wait_ge(ctx.sem("atOE"), bOE + g2)
                                ins = t.transpose(ctx.ps_bf[0:64, (g2 % 2) * 1024:(g2 % 2) * 1024 + P],
                                                  hp_[:, dst_tok + i * P + h * 64: dst_tok + i * P + h * 64 + 64],
                                                  ctx.ident[:, :])
                                ins.then_inc(ctx.sem("atTR"), 1)

                    @blk.scalar
                    def _(s2):
                        for i in range(NT):
                            for h in range(2):
                                g2 = i * 2 + h
                                s2.wait_ge(ctx.sem("atTR"), bTR + g2 + 1)
                                s2.copy(hp_[0:64, dstT + h * 2048 + i * P: dstT + h * 2048 + (i + 1) * P],
                                        ctx.ps_bf[0:64, (g2 % 2) * 1024:(g2 % 2) * 1024 + P]
                                        ).then_inc(ctx.sem("atOE"), 1)
                ctx.bump("atP", 4)
                ctx.bump("atTR", 32)
                ctx.bump("atOE", 32)

            with nc.Block() as blk:
                @blk.gpsimd
                def _(g):
                    load_w(ctx, g, ctx.win[pre + "WT"][:, 2 * D + hpair * P: 2 * D + (hpair + 1) * P],
                           WP, D, P, sW)
            wv = ctx.val(sW)

            def evac_v(i, ps, s):
                dst = bass.AP(hp_, V2 + i * 130, [[hp_.shape[1], P], [65, 2], [1, 64]])
                return s.copy(dst, ps[:, 0:P])

            linearA(ctx, f"v{uid}_{hpair}", XT, WP, D, P, [(sW, wv)], evac_v)

            bS = ctx.val("atS")
            bP = ctx.val("atP")
            bAV = ctx.val("atAV")
            bST = ctx.val("atST")
            bTR = ctx.val("atTR")
            bOE = ctx.val("atOE")
            bPM = ctx.val("atPM")
            bSZ = ctx.val("pfB")
            with nc.Block() as blk:
                @blk.tensor
                def _(t):
                    stg = 0
                    nS = 0; nP = 0; nAV = 0; nPM = 0
                    for h in range(2):
                        for c in range(NT):
                            if c >= 2:
                                t.wait_ge(ctx.sem("atP"), bP + nP - 1)
                            ins = t.matmul(ctx.ps[:, (c % 2) * 512:(c % 2) * 512 + P],
                                           hp_[0:64, Dk + h * 2048 + c * P: Dk + h * 2048 + (c + 1) * P],
                                           hp_[0:64, B + h * 2048 + c * P: B + h * 2048 + (c + 1) * P],
                                           start=True, stop=True)
                            ins.then_inc(ctx.sem("atS"), 1); nS += 1
                            t.wait_ge(ctx.sem("atP"), bP + nP + 1)
                            nP += 1
                            ob = 2 + (c % 2)
                            if c >= 2:
                                t.wait_ge(ctx.sem("atST"), bST + (h * NT + c) - 1)
                            ins = t.matmul(ctx.ps[:, ob * 512: ob * 512 + 65],
                                           hp_[:, PT + (c % 2) * P: PT + (c % 2) * P + P],
                                           hp_[:, V2 + c * 130 + h * 65: V2 + c * 130 + h * 65 + 65],
                                           start=True, stop=(c == 0))
                            if c > 0:
                                t.wait_ge(ctx.sem("atPM"), bPM + nPM)
                                ins = t.matmul(ctx.ps[:, ob * 512: ob * 512 + 65],
                                               hp_[0:64, B + h * 2048 + c * P: B + h * 2048 + (c + 1) * P],
                                               hp_[0:64, SZ + (h % 2) * 72: SZ + (h % 2) * 72 + 65],
                                               start=False, stop=True)
                            ins.then_inc(ctx.sem("atAV"), 1); nAV += 1
                            if c > 0:
                                t.wait_ge(ctx.sem("pfB"), bSZ + nPM)
                            ins = t.matmul(ctx.ps[0:64, (4 + h) * 512:(4 + h) * 512 + 65],
                                           hp_[:, C + c * P + h * 64: C + c * P + h * 64 + 64],
                                           hp_[:, V2 + c * 130 + h * 65: V2 + c * 130 + h * 65 + 65],
                                           start=True, stop=True)
                            ins.then_inc(ctx.sem("atS"), 1); nS += 1
                            nPM += 1
                            if h * NT + c >= 1:
                                g2 = h * NT + c - 1
                                t.wait_ge(ctx.sem("atST"), bST + g2 + 1)
                                if g2 >= 1:
                                    t.wait_ge(ctx.sem("atOE"), bOE + g2)
                                ins = t.transpose(ctx.ps_bf[:, (g2 % 2) * 1024:(g2 % 2) * 1024 + P],
                                                  hp_[:, STG + (g2 % 2) * P: STG + (g2 % 2) * P + P],
                                                  ctx.ident[:, :])
                                ins.then_inc(ctx.sem("atTR"), 1)
                    g2 = 31
                    t.wait_ge(ctx.sem("atST"), bST + 32)
                    t.wait_ge(ctx.sem("atOE"), bOE + g2)
                    ins = t.transpose(ctx.ps_bf[:, (g2 % 2) * 1024:(g2 % 2) * 1024 + P],
                                      hp_[:, STG + (g2 % 2) * P: STG + (g2 % 2) * P + P],
                                      ctx.ident[:, :])
                    ins.then_inc(ctx.sem("atTR"), 1)

                @blk.vector
                def _(v):
                    nS = 0
                    nSZ = 0
                    for h in range(2):
                        v.memset(ctx.misc[0:64, 1536 + h * 72: 1536 + h * 72 + 65], 0.0)
                        for c in range(NT):
                            v.wait_ge(ctx.sem("atS"), bS + nS + 1)
                            nS += 2
                            ins = v.tensor_tensor(hp_[:, PT + (c % 2) * P: PT + (c % 2) * P + P],
                                                  ctx.ps[:, (c % 2) * 512:(c % 2) * 512 + P],
                                                  ctx.mtrilf[:, :], ALU.mult)
                            ins.then_inc(ctx.sem("atP"), 1)
                            gl = h * NT + c
                            v.wait_ge(ctx.sem("atAV"), bAV + gl + 1)
                            r = ctx.misc[:, 3090 + (gl % 2):3091 + (gl % 2)]
                            ins = v.tensor_scalar(r, ctx.ps[:, (2 + c % 2) * 512 + 64:
                                                           (2 + c % 2) * 512 + 65],
                                                  1e-6, None, ALU.add)
                            ctx.edge(ins, v, "chV")
                            ins = v.reciprocal(r, r)
                            ctx.edge(ins, v, "chV")
                            if gl >= 2:
                                v.wait_ge(ctx.sem("atTR"), bTR + gl - 1)
                            ins = v.tensor_scalar(hp_[:, STG + (gl % 2) * P: STG + (gl % 2) * P + 64],
                                                  ctx.ps[:, (2 + c % 2) * 512:(2 + c % 2) * 512 + 64],
                                                  r, None, ALU.mult)
                            ins.then_inc(ctx.sem("atST"), 1)
                            v.wait_ge(ctx.sem("atS"), bS + nS)
                            szf = ctx.misc[0:64, 1536 + h * 72: 1536 + h * 72 + 65]
                            ins = v.tensor_add(szf, szf,
                                               ctx.ps[0:64, (4 + h) * 512:(4 + h) * 512 + 65])
                            ins.then_inc(ctx.sem("pfB"), 1)
                            nSZ += 1

                @blk.scalar
                def _(s):
                    nSZ = 0
                    for h in range(2):
                        for c in range(NT):
                            s.wait_ge(ctx.sem("pfB"), bSZ + nSZ + 1)
                            nSZ += 1
                            s.copy(hp_[0:64, SZ + (h % 2) * 72: SZ + (h % 2) * 72 + 65],
                                   ctx.misc[0:64, 1536 + h * 72: 1536 + h * 72 + 65]
                                   ).then_inc(ctx.sem("atPM"), 1)
                            gl = h * NT + c
                            s.wait_ge(ctx.sem("atTR"), bTR + gl + 1)
                            gh = 2 * hpair + h
                            chunk, rb = gh // 2, (gh % 2) * 64
                            if rb == 0:
                                dst = hp_[0:64, OT + chunk * T + c * P: OT + chunk * T + (c + 1) * P]
                            else:
                                dst = hp_[64:128, OT + chunk * T + c * P: OT + chunk * T + (c + 1) * P]
                            s.copy(dst, ctx.ps_bf[0:64, (gl % 2) * 1024:(gl % 2) * 1024 + P]
                                   ).then_inc(ctx.sem("atOE"), 1)
            ctx.bump("atS", 64)
            ctx.bump("atP", 32)
            ctx.bump("atAV", 32)
            ctx.bump("atPM", 32)
            ctx.bump("pfB", 32)
            ctx.bump("atST", 32)
            ctx.bump("atTR", 32)
            ctx.bump("atOE", 32)

        def evac_o(i, ps, veng):
            return veng.tensor_tensor(ctx.x[:, i * D:(i + 1) * D], ps,
                                      ctx.x[:, i * D:(i + 1) * D], ALU.add)

        linearA(ctx, f"o{uid}", OT, W, D, D, [(sW, ctx.val(sW))], evac_o,
                evac_eng="vector")

    return decl, host, emit



def make_mamba(L):
    pre = f"L{L}_attn_a_"

    def host(inputs):
        ipw = np.asarray(inputs[f"{pre}in_proj_w"])
        opw = np.asarray(inputs[f"{pre}out_proj_w"])
        xpw = np.asarray(inputs[f"{pre}x_proj_w"])
        dtw = np.asarray(inputs[f"{pre}dt_proj_w"])
        sel = np.zeros((16, 16 * P), BF)
        for n in range(16):
            sel[n, n * P:(n + 1) * P] = 1.0
        return {
            f"{pre}ipT": np.ascontiguousarray(ipw.T),
            f"{pre}opT": np.ascontiguousarray(opw.T),
            f"{pre}xpT": np.ascontiguousarray(xpw.T),
            f"{pre}dtT": np.ascontiguousarray(dtw.T),
            f"{pre}ipb": np.ascontiguousarray(np.asarray(inputs[f"{pre}in_proj_b"]).reshape(32, P).T),
            f"{pre}dtb": np.ascontiguousarray(np.asarray(inputs[f"{pre}dt_proj_b"]).reshape(16, P).T),
            f"{pre}cw": np.ascontiguousarray(np.asarray(inputs[f"{pre}conv_w"]).reshape(16, P, 4).transpose(1, 0, 2).reshape(P, 64)),
            f"{pre}cb": np.ascontiguousarray(np.asarray(inputs[f"{pre}conv_b"]).reshape(16, P).T),
            f"{pre}alog": np.ascontiguousarray(np.asarray(inputs[f"{pre}A_log"]).reshape(16, P, 16).transpose(1, 0, 2).reshape(P, 256)),
            f"{pre}dpar": np.ascontiguousarray(np.asarray(inputs[f"{pre}D_param"]).reshape(16, P).T),
            f"{pre}opb": np.ascontiguousarray(np.asarray(inputs[f"{pre}out_proj_b"]).reshape(1, D)),
            f"{pre}sel": sel,
        }

    def decl(ctx):
        nc = ctx.nc
        w = ctx.win
        w[pre + "ipT"] = nc.dram_tensor(pre + "ipT", [D, 4096], F32, kind="ExternalInput")
        w[pre + "opT"] = nc.dram_tensor(pre + "opT", [2048, D], F32, kind="ExternalInput")
        w[pre + "xpT"] = nc.dram_tensor(pre + "xpT", [2048, 96], F32, kind="ExternalInput")
        w[pre + "dtT"] = nc.dram_tensor(pre + "dtT", [64, 2048], F32, kind="ExternalInput")
        w[pre + "ipb"] = nc.dram_tensor(pre + "ipb", [P, 32], F32, kind="ExternalInput")
        w[pre + "dtb"] = nc.dram_tensor(pre + "dtb", [P, 16], F32, kind="ExternalInput")
        w[pre + "cw"] = nc.dram_tensor(pre + "cw", [P, 64], F32, kind="ExternalInput")
        w[pre + "cb"] = nc.dram_tensor(pre + "cb", [P, 16], F32, kind="ExternalInput")
        w[pre + "alog"] = nc.dram_tensor(pre + "alog", [P, 256], F32, kind="ExternalInput")
        w[pre + "dpar"] = nc.dram_tensor(pre + "dpar", [P, 16], F32, kind="ExternalInput")
        w[pre + "opb"] = nc.dram_tensor(pre + "opb", [1, D], F32, kind="ExternalInput")
        w[pre + "sel"] = nc.dram_tensor(pre + "sel", [16, 16 * P], BF16, kind="ExternalInput")

    def emit(ctx, uid):
        nc = ctx.nc
        hp_ = ctx.heap
        XT, OUT, SEL = 0, 16384, 32768
        BT, CT, DRT, XDBL = 34816, 35328, 35840, 37888
        XB, XC, DLT, DU, DA, BB, CB = 39424, 41472, 43520, 45568, 47616, 49664, 51712
        WSL, WSLR, WOP = 53760, 54784, 55808
        DTW = 57344
        XPW = BB
        YF, RS, SCR = 0, 2048, 3504
        ACOL, DTB, CWC, CBC, INB, DPC = 3104, 3360, 3376, 3440, 3456, 3488
        sW = "wdma"

        emit_ln(ctx, xt_off=XT, tag=uid)

        with nc.Block() as blk:
            @blk.gpsimd
            def _(g):
                load_cols(ctx, g, ctx.win[pre + "ipb"], INB, 32, sW)
                load_cols(ctx, g, ctx.win[pre + "dtb"], DTB, 16, sW)
                load_cols(ctx, g, ctx.win[pre + "cw"], CWC, 64, sW)
                load_cols(ctx, g, ctx.win[pre + "cb"], CBC, 16, sW)
                load_cols(ctx, g, ctx.win[pre + "alog"], ACOL, 256, sW)
                load_cols(ctx, g, ctx.win[pre + "dpar"], DPC, 16, sW)
                g.dma_start(bass.AP(hp_, SEL, [[hp_.shape[1], 16], [1, 16 * P]]),
                            ctx.win[pre + "sel"][:, :]).then_inc(ctx.sem(sW), 16)
                ctx.bump(sW, 16)
                load_w(ctx, g, ctx.win[pre + "dtT"], DTW, 64, 2048, sW)
                load_w(ctx, g, ctx.win[pre + "xpT"], XPW, 2048, 96, sW)
            wv0 = ctx.val(sW)

            @blk.scalar
            def _(s):
                s.wait_ge(ctx.sem(sW), wv0)
                ins = s.activation(ctx.misc[:, ACOL:ACOL + 256],
                                   ctx.misc[:, ACOL:ACOL + 256], AF.Exp,
                                   bias=0.0, scale=1.0)
                ctx.edge(ins, s, "chS")
                s.mul(ctx.misc[:, ACOL:ACOL + 256], ctx.misc[:, ACOL:ACOL + 256], -1.0)

            @blk.vector
            def _(v):
                v.memset(hp_[:, OUT:OUT + 16384], 0)

        def emit_xc(ci):
            with nc.Block() as blk:
                @blk.gpsimd
                def _(g):
                    load_w(ctx, g, ctx.win[pre + "ipT"][:, ci * P:(ci + 1) * P],
                           WSL, D, P, sW)
            wv = ctx.val(sW)
            bS = ctx.val("atS"); bP = ctx.val("atP"); bC = ctx.val("chV")
            with nc.Block() as blk:
                @blk.tensor
                def _(t):
                    t.wait_ge(ctx.sem(sW), wv)
                    for w in range(4):
                        if w >= 2:
                            t.wait_ge(ctx.sem("atP"), bP + w - 1)
                        for j in range(ND):
                            ins = t.matmul(ctx.ps[:, (w % 2) * 512:(w % 2) * 512 + 512],
                                           hp_[:, WSL + j * P: WSL + (j + 1) * P],
                                           hp_[:, XT + j * T + w * 512: XT + j * T + (w + 1) * 512],
                                           start=(j == 0), stop=(j == ND - 1))
                        ins.then_inc(ctx.sem("atS"), 1)

                @blk.scalar
                def _(s):
                    for w in range(4):
                        s.wait_ge(ctx.sem("atS"), bS + w + 1)
                        ins = s.activation(hp_[:, XB + w * 512: XB + (w + 1) * 512],
                                           ctx.ps[:, (w % 2) * 512:(w % 2) * 512 + 512],
                                           AF.Identity,
                                           bias=ctx.misc[:, INB + ci:INB + ci + 1],
                                           scale=1.0)
                        ins.then_inc(ctx.sem("atP"), 1)
                    s.wait_ge(ctx.sem("chV"), bC + 4)
                    s.activation(hp_[:, XC:XC + 2048], hp_[:, XC:XC + 2048],
                                 AF.Silu, bias=ctx.misc[:, CBC + ci:CBC + ci + 1],
                                 scale=1.0).then_inc(ctx.sem("pfC"), 1)

                @blk.vector
                def _(v):
                    v.wait_ge(ctx.sem("atP"), bP + 4)
                    ins = v.tensor_scalar(hp_[:, XC:XC + 2048], hp_[:, XB:XB + 2048],
                                          ctx.misc[:, CWC + ci * 4 + 3:CWC + ci * 4 + 4],
                                          None, ALU.mult)
                    ctx.edge(ins, v, "chV")
                    for jj in range(3):
                        ins = v.scalar_tensor_tensor(
                            hp_[:, XC + (3 - jj):XC + 2048],
                            hp_[:, XB:XB + 2048 - (3 - jj)],
                            ctx.misc[:, CWC + ci * 4 + jj:CWC + ci * 4 + jj + 1],
                            hp_[:, XC + (3 - jj):XC + 2048],
                            ALU.mult, ALU.add)
                        ctx.edge(ins, v, "chV")
            ctx.bump("atS", 4)
            ctx.bump("atP", 4)
            ctx.bump("pfC", 1)

        xdbl0 = ctx.val("pfC")
        for ci in range(NT):
            emit_xc(ci)
            with nc.Block() as blk:
                @blk.tensor
                def _(t):
                    for tt in range(NT):
                        ins = t.matmul(ctx.ps[:, 1024 + tt * P: 1024 + tt * P + 96],
                                       hp_[:, XC + tt * P: XC + (tt + 1) * P],
                                       hp_[:, XPW + ci * 96: XPW + (ci + 1) * 96],
                                       start=(ci == 0 and tt % 4 == 0),
                                       stop=(ci == NT - 1 and tt % 4 == 3))
                    ins.then_inc(ctx.sem("atS"), 1)
            ctx.bump("atS", 1)

        b2 = ctx.val("chS"); bTR = ctx.val("atTR"); bOE = ctx.val("atOE")
        with nc.Block() as blk:
            @blk.scalar
            def _(s):
                src = bass.AP(ctx.ps, 1024, [[3072, P], [P, NT], [1, 96]])
                dst = bass.AP(hp_, XDBL, [[hp_.shape[1], P], [96, NT], [1, 96]])
                ins = s.copy(dst, src)
                ctx.edge(ins, s, "chS")

            @blk.tensor
            def _(t):
                t.wait_ge(ctx.sem("chS"), b2 + 1)
                k = 0
                for part, width, dsto in ((0, 64, DRT), (64, 16, BT), (80, 16, CT)):
                    for tt in range(NT):
                        if k >= 2:
                            t.wait_ge(ctx.sem("atOE"), bOE + k - 1)
                        ins = t.transpose(ctx.ps_bf[0:width, (k % 2) * 1024:(k % 2) * 1024 + P],
                                          hp_[:, XDBL + tt * 96 + part: XDBL + tt * 96 + part + width],
                                          ctx.ident[:, :])
                        ins.then_inc(ctx.sem("atTR"), 1)
                        k += 1

            @blk.scalar
            def _(s2):
                k = 0
                for part, width, dsto in ((0, 64, DRT), (64, 16, BT), (80, 16, CT)):
                    for tt in range(NT):
                        s2.wait_ge(ctx.sem("atTR"), bTR + k + 1)
                        s2.copy(hp_[0:width, dsto + tt * P: dsto + (tt + 1) * P],
                                ctx.ps_bf[0:width, (k % 2) * 1024:(k % 2) * 1024 + P]
                                ).then_inc(ctx.sem("atOE"), 1)
                        k += 1
        ctx.bump("atTR", 48)
        ctx.bump("atOE", 48)

        for ci in range(NT):
            emit_xc(ci)
            with nc.Block() as blk:
                @blk.gpsimd
                def _(g):
                    load_w(ctx, g, ctx.win[pre + "ipT"][:, 2048 + ci * P: 2048 + (ci + 1) * P],
                           WSLR, D, P, sW)
                    load_w(ctx, g, ctx.win[pre + "opT"][ci * P:(ci + 1) * P, :],
                           WOP, P, D, sW)
            wv = ctx.val(sW)

            bS = ctx.val("atS"); bP = ctx.val("atP")
            bV = ctx.val("pfB"); bAV = ctx.val("atAV"); bD0 = ctx.val("pfD")
            with nc.Block() as blk:
                @blk.tensor
                def _(t):
                    for w in range(4):
                        if w >= 2:
                            t.wait_ge(ctx.sem("atP"), bP + w - 1)
                        ins = t.matmul(ctx.ps[:, (w % 2) * 512:(w % 2) * 512 + 512],
                                       hp_[0:64, DTW + ci * P: DTW + (ci + 1) * P],
                                       hp_[0:64, DRT + w * 512: DRT + (w + 1) * 512],
                                       start=True, stop=True)
                        ins.then_inc(ctx.sem("atS"), 1)
                    k = 0
                    for n in range(16):
                        for src in (BT, CT):
                            for w in range(4):
                                if k >= 2:
                                    t.wait_ge(ctx.sem("atP"), bP + 4 + k - 1)
                                ins = t.matmul(ctx.ps[:, 1024 + (k % 2) * 512: 1024 + (k % 2) * 512 + 512],
                                               hp_[0:16, SEL + n * P: SEL + (n + 1) * P],
                                               hp_[0:16, src + w * 512: src + (w + 1) * 512],
                                               start=True, stop=True)
                                ins.then_inc(ctx.sem("atS"), 1)
                                k += 1

                @blk.scalar
                def _(s):
                    for w in range(4):
                        s.wait_ge(ctx.sem("atS"), bS + w + 1)
                        ins = s.activation(ctx.misc[:, SCR:SCR + 512],
                                           ctx.ps[:, (w % 2) * 512:(w % 2) * 512 + 512],
                                           AF.Exp, bias=ctx.misc[:, DTB + ci:DTB + ci + 1],
                                           scale=1.0)
                        ctx.edge(ins, s, "chS")
                        ins = s.activation(hp_[:, DLT + w * 512: DLT + (w + 1) * 512],
                                           ctx.misc[:, SCR:SCR + 512],
                                           AF.Ln, bias=1.0, scale=1.0)
                        ins.then_inc(ctx.sem("atP"), 1)
                        s.wait_ge(ctx.sem("atP"), bP + w + 1)
                    k = 0
                    for n in range(16):
                        if n >= 1:
                            s.wait_ge(ctx.sem("pfB"), bV + n)
                        for dst in (BB, CB):
                            for w in range(4):
                                s.wait_ge(ctx.sem("atS"), bS + 4 + k + 1)
                                ins = s.copy(hp_[:, dst + w * 512: dst + (w + 1) * 512],
                                             ctx.ps[:, 1024 + (k % 2) * 512: 1024 + (k % 2) * 512 + 512])
                                ins.then_inc(ctx.sem("atP"), 1)
                                k += 1
                        ins = s.activation(hp_[:, DA:DA + 2048], hp_[:, DLT:DLT + 2048],
                                           AF.Exp, bias=0.0,
                                           scale=ctx.misc[:, ACOL + ci * 16 + n:ACOL + ci * 16 + n + 1])
                        ins.then_inc(ctx.sem("atAV"), 1)

                @blk.vector
                def _(v):
                    v.wait_ge(ctx.sem("atP"), bP + 4)
                    ins = v.tensor_mul(hp_[:, DU:DU + 2048], hp_[:, DLT:DLT + 2048],
                                       hp_[:, XC:XC + 2048])
                    ctx.edge(ins, v, "chV")
                    for n in range(16):
                        v.wait_ge(ctx.sem("atP"), bP + 4 + 8 * n + 4)
                        ins = v.tensor_mul(hp_[:, BB:BB + 2048], hp_[:, DU:DU + 2048],
                                           hp_[:, BB:BB + 2048])
                        ctx.edge(ins, v, "chV")
                        v.wait_ge(ctx.sem("atAV"), bAV + n + 1)
                        ins = v.tensor_tensor_scan(hp_[:, DA:DA + 2048],
                                                   hp_[:, DA:DA + 2048],
                                                   hp_[:, BB:BB + 2048], 0.0,
                                                   ALU.mult, ALU.add)
                        ctx.edge(ins, v, "chV")
                        v.wait_ge(ctx.sem("atP"), bP + 4 + 8 * n + 8)
                        for hh in range(2):
                            ins = v.tensor_mul(ctx.misc[:, RS:RS + 1024],
                                               hp_[:, DA + hh * 1024: DA + (hh + 1) * 1024],
                                               hp_[:, CB + hh * 1024: CB + (hh + 1) * 1024])
                            ctx.edge(ins, v, "chV")
                            if n == 0:
                                ins = v.tensor_copy(ctx.misc[:, YF + hh * 1024: YF + (hh + 1) * 1024],
                                                    ctx.misc[:, RS:RS + 1024])
                            else:
                                ins = v.tensor_add(ctx.misc[:, YF + hh * 1024: YF + (hh + 1) * 1024],
                                                   ctx.misc[:, YF + hh * 1024: YF + (hh + 1) * 1024],
                                                   ctx.misc[:, RS:RS + 1024])
                            if hh == 0:
                                ctx.edge(ins, v, "chV")
                        ins.then_inc(ctx.sem("pfB"), 1)
                        v.wait_ge(ctx.sem("pfB"), bV + n + 1)
                    for hh in range(2):
                        v.wait_ge(ctx.sem("pfD"), bD0 + hh + 1)
                        ins = v.scalar_tensor_tensor(
                            ctx.misc[:, YF + hh * 1024: YF + (hh + 1) * 1024],
                            ctx.misc[:, RS:RS + 1024],
                            ctx.misc[:, DPC + ci:DPC + ci + 1],
                            ctx.misc[:, YF + hh * 1024: YF + (hh + 1) * 1024],
                            ALU.mult, ALU.add)
                        ins.then_inc(ctx.sem("pfB"), 1)
                        v.wait_ge(ctx.sem("pfB"), bV + 16 + hh + 1)

                @blk.scalar
                def _(s3):
                    for hh in range(2):
                        s3.wait_ge(ctx.sem("pfB"), bV + 16 + hh)
                        s3.copy(ctx.misc[:, RS:RS + 1024],
                                hp_[:, XC + hh * 1024: XC + (hh + 1) * 1024]
                                ).then_inc(ctx.sem("pfD"), 1)
            ctx.bump("atS", 4 + 128)
            ctx.bump("atP", 4 + 128)
            ctx.bump("atAV", 16)
            ctx.bump("pfB", 18)
            ctx.bump("pfD", 2)

            bS = ctx.val("atS"); bP = ctx.val("atP"); bF = ctx.val("pfA")
            bV0 = ctx.val("chV")
            with nc.Block() as blk:
                @blk.tensor
                def _(t):
                    t.wait_ge(ctx.sem(sW), wv)
                    for w in range(4):
                        for j in range(ND):
                            ins = t.matmul(ctx.ps[:, w * 512:(w + 1) * 512],
                                           hp_[:, WSLR + j * P: WSLR + (j + 1) * P],
                                           hp_[:, XT + j * T + w * 512: XT + j * T + (w + 1) * 512],
                                           start=(j == 0), stop=(j == ND - 1))
                        ins.then_inc(ctx.sem("atS"), 1)
                    for tt in range(NT):
                        t.wait_ge(ctx.sem("pfA"), bF + (tt // 8) + 1)
                        if tt >= 1:
                            t.wait_ge(ctx.sem("atP"), bP + 2 + tt)
                        for nn in range(2):
                            ins = t.matmul(ctx.ps[:, 2048 + nn * 512: 2048 + (nn + 1) * 512],
                                           hp_[:, DU + tt * P: DU + (tt + 1) * P],
                                           hp_[:, WOP + nn * 512: WOP + (nn + 1) * 512],
                                           start=True, stop=True)
                        ins.then_inc(ctx.sem("atS"), 1)

                @blk.scalar
                def _(s):
                    for hh in range(2):
                        s.wait_ge(ctx.sem("atS"), bS + 2 * hh + 2)
                        if hh == 1:
                            s.wait_ge(ctx.sem("pfA"), bF + 1)
                        ins = s.activation(ctx.misc[:, RS:RS + 1024],
                                           ctx.ps[:, hh * 1024:(hh + 1) * 1024],
                                           AF.Silu,
                                           bias=ctx.misc[:, INB + 16 + ci:INB + 17 + ci],
                                           scale=1.0)
                        ins.then_inc(ctx.sem("atP"), 1)
                    for tt in range(NT):
                        s.wait_ge(ctx.sem("atS"), bS + 4 + tt + 1)
                        if tt >= 2:
                            s.wait_ge(ctx.sem("chV"), bV0 + tt - 1)
                        s.copy(hp_[:, XB + (tt % 2) * 1024: XB + (tt % 2) * 1024 + 512],
                               ctx.ps[:, 2048:2560])
                        ins = s.copy(hp_[:, XB + (tt % 2) * 1024 + 512: XB + (tt % 2) * 1024 + 1024],
                                     ctx.ps[:, 2560:3072])
                        ins.then_inc(ctx.sem("atP"), 1)

                @blk.vector
                def _(v):
                    for hh in range(2):
                        v.wait_ge(ctx.sem("atP"), bP + hh + 1)
                        ins = v.tensor_mul(hp_[:, DU + hh * 1024: DU + (hh + 1) * 1024],
                                           ctx.misc[:, YF + hh * 1024: YF + (hh + 1) * 1024],
                                           ctx.misc[:, RS:RS + 1024])
                        ins.then_inc(ctx.sem("pfA"), 1)
                        v.wait_ge(ctx.sem("pfA"), bF + hh + 1)
                    for tt in range(NT):
                        v.wait_ge(ctx.sem("atP"), bP + 2 + tt + 1)
                        ins = v.tensor_add(hp_[:, OUT + tt * 1024: OUT + (tt + 1) * 1024],
                                           hp_[:, OUT + tt * 1024: OUT + (tt + 1) * 1024],
                                           hp_[:, XB + (tt % 2) * 1024: XB + (tt % 2) * 1024 + 1024])
                        ctx.edge(ins, v, "chV")
            ctx.bump("atS", 4 + NT)
            ctx.bump("atP", 2 + NT)
            ctx.bump("pfA", 2)

        with nc.Block() as blk:
            @blk.gpsimd
            def _(g):
                load_row(ctx, g, ctx.win[pre + "opb"], WSL, D, sW)
        wvb = ctx.val(sW)
        bS = ctx.val("atS"); bP = ctx.val("atP"); bV0 = ctx.val("chV")
        with nc.Block() as blk:
            @blk.tensor
            def _(t):
                t.wait_ge(ctx.sem(sW), wvb)
                for nn in range(2):
                    ins = t.matmul(ctx.ps[:, nn * 512:(nn + 1) * 512],
                                   ctx.ones1[0:1, 0:P],
                                   hp_[0:1, WSL + nn * 512: WSL + (nn + 1) * 512],
                                   start=True, stop=True)
                ins.then_inc(ctx.sem("atS"), 1)

            @blk.scalar
            def _(s):
                s.wait_ge(ctx.sem("atS"), bS + 1)
                ins = s.copy(ctx.misc[:, RS:RS + 1024], ctx.ps[:, 0:1024])
                ins.then_inc(ctx.sem("atP"), 1)
                for tt in range(NT):
                    if tt >= 2:
                        s.wait_ge(ctx.sem("chV"), bV0 + 2 * (tt - 1))
                    ins = s.copy(ctx.misc[:, YF + (tt % 2) * 1024: YF + (tt % 2) * 1024 + 1024],
                                 hp_[:, OUT + tt * 1024: OUT + (tt + 1) * 1024])
                    ins.then_inc(ctx.sem("atP"), 1)

            @blk.vector
            def _(v):
                for tt in range(NT):
                    v.wait_ge(ctx.sem("atP"), bP + 1 + tt + 1)
                    ins = v.tensor_add(ctx.misc[:, YF + (tt % 2) * 1024: YF + (tt % 2) * 1024 + 1024],
                                       ctx.misc[:, YF + (tt % 2) * 1024: YF + (tt % 2) * 1024 + 1024],
                                       ctx.misc[:, RS:RS + 1024])
                    ctx.edge(ins, v, "chV")
                    ins = v.tensor_add(ctx.x[:, tt * 1024:(tt + 1) * 1024],
                                       ctx.x[:, tt * 1024:(tt + 1) * 1024],
                                       ctx.misc[:, YF + (tt % 2) * 1024: YF + (tt % 2) * 1024 + 1024])
                    ctx.edge(ins, v, "chV")
        ctx.bump("atS", 1)
        ctx.bump("atP", 1 + NT)

    return decl, host, emit



def emit_ffn(ctx, uid, src_off, w1_ap, w2_ap, b1_base, b2_row, outacc_off,
             ht_off, w1s, w2s, Hff=4096, act=AF.Relu, w1_dyn=None, w2_dyn=None,
             w1s2=None):
    nc = ctx.nc
    hp_ = ctx.heap
    sW = "wdma"
    nch = Hff // 512
    dbuf = w1s2 is not None

    def w1buf(c):
        return w1s if (not dbuf or c % 2 == 0) else w1s2

    def issue_w1(g, c):
        if w1_dyn is None:
            load_w(ctx, g, w1_ap[:, c * 512:(c + 1) * 512], w1buf(c), D, 512, sW)
        else:
            load_w_dyn(ctx, g, w1_ap[:, c * 512:], w1_dyn, w1buf(c), D, 512, sW)
        return ctx.val(sW)

    def issue_w2(g, c):
        if w2_dyn is None:
            load_w(ctx, g, w2_ap[c * 512:(c + 1) * 512, :], w2s, 512, D, sW)
        else:
            load_w_dyn(ctx, g, w2_ap[c * 512:(c + 1) * 512, :], w2_dyn,
                       w2s, 512, D, sW)
        return ctx.val(sW)

    w1v, w2v = {}, {}
    if dbuf:
        with nc.Block() as blk:
            @blk.gpsimd
            def _(g):
                w1v[0] = issue_w1(g, 0)
    for c in range(nch):
        if not dbuf:
            with nc.Block() as blk:
                @blk.gpsimd
                def _(g, c=c):
                    issue_w1(g, c)
                    w1v[c] = w2v[c] = issue_w2(g, c)
        bS = ctx.val("atS"); bP = ctx.val("atP"); bV = ctx.val("chV")
        with nc.Block() as blk:
            if dbuf:
                @blk.gpsimd
                def _(g, c=c):
                    w2v[c] = issue_w2(g, c)
                    if c + 1 < nch:
                        w1v[c + 1] = issue_w1(g, c + 1)

            @blk.tensor
            def _(t):
                W1 = w1buf(c)
                t.wait_ge(ctx.sem(sW), w1v[c])
                k = 0
                for mi in range(4):
                    for w in range(4):
                        if k >= 2:
                            t.wait_ge(ctx.sem("atP"), bP + k - 1)
                        for j in range(ND):
                            ins = t.matmul(ctx.ps[:, (k % 2) * 512:(k % 2) * 512 + 512],
                                           hp_[:, W1 + j * 512 + mi * P: W1 + j * 512 + (mi + 1) * P],
                                           hp_[:, src_off + j * T + w * 512: src_off + j * T + (w + 1) * 512],
                                           start=(j == 0), stop=(j == ND - 1))
                        ins.then_inc(ctx.sem("atS"), 1)
                        k += 1
                t.wait_ge(ctx.sem(sW), w2v[c])
                for tt in range(NT):
                    t.wait_ge(ctx.sem("atP"), bP + 16 + tt)
                    if tt >= 2:
                        t.wait_ge(ctx.sem("atP"), bP + 16 + tt - 1)
                    b0 = 2 + (tt % 2) * 2
                    for j in range(4):
                        for nn in range(2):
                            ins = t.matmul(ctx.ps[:, (b0 + nn) * 512:(b0 + nn + 1) * 512],
                                           hp_[:, ht_off + j * T + tt * P: ht_off + j * T + (tt + 1) * P],
                                           hp_[:, w2s + j * D + nn * 512: w2s + j * D + (nn + 1) * 512],
                                           start=(j == 0), stop=(j == 3 and not (c == 0 and b2_row is not None)))
                    if c == 0 and b2_row is not None:
                        rows = b2_row if isinstance(b2_row, (list, tuple)) else [b2_row]
                        for ri, row in enumerate(rows):
                            for nn in range(2):
                                ins = t.matmul(ctx.ps[:, (b0 + nn) * 512:(b0 + nn + 1) * 512],
                                               ctx.ones1[0:1, 0:P],
                                               hp_[0:1, row + nn * 512: row + nn * 512 + 512],
                                               start=False, stop=(ri == len(rows) - 1))
                    ins.then_inc(ctx.sem("atAV"), 1)

            @blk.scalar
            def _(s):
                bAV = ctx.val("atAV")
                k = 0
                for mi in range(4):
                    for w in range(4):
                        s.wait_ge(ctx.sem("atS"), bS + k + 1)
                        bias = (ctx.misc[:, b1_base + c * 4 + mi: b1_base + c * 4 + mi + 1]
                                if b1_base is not None else 0.0)
                        ins = s.activation(hp_[:, ht_off + mi * T + w * 512: ht_off + mi * T + (w + 1) * 512],
                                           ctx.ps[:, (k % 2) * 512:(k % 2) * 512 + 512],
                                           act, bias=bias, scale=1.0)
                        ins.then_inc(ctx.sem("atP"), 1)
                        k += 1
                for tt in range(NT):
                    s.wait_ge(ctx.sem("atAV"), bAV + tt + 1)
                    b0 = 2 + (tt % 2) * 2
                    if c == 0:
                        ins = s.copy(hp_[:, outacc_off + tt * D: outacc_off + (tt + 1) * D],
                                     ctx.ps[:, b0 * 512: b0 * 512 + 1024])
                        ins.then_inc(ctx.sem("atP"), 1)
                    else:
                        if tt >= 2:
                            s.wait_ge(ctx.sem("chV"), bV + tt - 1)
                        ins = s.copy(hp_[:, ctx.ln_tmp_off + (tt % 2) * D: ctx.ln_tmp_off + (tt % 2 + 1) * D],
                                     ctx.ps[:, b0 * 512: b0 * 512 + 1024])
                        ins.then_inc(ctx.sem("atP"), 1)

            if c > 0:
                @blk.vector
                def _(v):
                    bP2 = bP + 16
                    for tt in range(NT):
                        v.wait_ge(ctx.sem("atP"), bP2 + tt + 1)
                        ins = v.tensor_add(hp_[:, outacc_off + tt * D: outacc_off + (tt + 1) * D],
                                           hp_[:, outacc_off + tt * D: outacc_off + (tt + 1) * D],
                                           hp_[:, ctx.ln_tmp_off + (tt % 2) * D: ctx.ln_tmp_off + (tt % 2 + 1) * D])
                        ctx.edge(ins, v, "chV")
        ctx.bump("atS", 16)
        ctx.bump("atP", 16 + NT)
        ctx.bump("atAV", NT)



def prog_l3att(ctx):
    nc = ctx.nc
    hp_ = ctx.heap
    XT, OT = 0, 16384
    Q2T, K2T, V2, PT, STG = 32768, 34816, 36864, 38976, 40000
    OPBR = 40256
    WQ = 41344
    IPB = 3600
    sW = "wdma"
    v1_out = nc.dram_tensor("v1_out", [T, D], F32, kind="ExternalOutput")
    emit_ln(ctx, xt_off=XT, tag="l3a", tok_in_x=True)
    with nc.Block() as blk:
        @blk.gpsimd
        def _(g):
            load_cols(ctx, g, ctx.win["L3A_ipbc"], IPB, 24, sW)
            load_row(ctx, g, ctx.win["L3A_opbr"], OPBR, D, sW)

        @blk.vector
        def _(v):
            v.memset(bass.AP(hp_, V2 + 64, [[hp_.shape[1], P], [65, 2 * NT], [1, 1]]), 1.0)

    for hpair in range(ND):
        with nc.Block() as blk:
            @blk.gpsimd
            def _(g):
                wt = ctx.win["L3A_ipT"]
                load_w(ctx, g, wt[:, hpair * P:(hpair + 1) * P], WQ, D, P, sW)
                load_w(ctx, g, wt[:, D + hpair * P: D + (hpair + 1) * P], WQ + 1024, D, P, sW)
                load_w(ctx, g, wt[:, 2 * D + hpair * P: 2 * D + (hpair + 1) * P], WQ + 2048, D, P, sW)
        wv = ctx.val(sW)

        def evac_q(mi, w, ps, s, hp3=hpair):
            return s.activation(hp_[:, Q2T + w * 512: Q2T + (w + 1) * 512], ps,
                                AF.Identity, bias=ctx.misc[:, IPB + hp3:IPB + hp3 + 1],
                                scale=1.0)

        linearB(ctx, f"qe_{hpair}", XT, WQ, D, P, [(sW, wv)], evac_q)

        def evac_k(mi, w, ps, s, hp3=hpair):
            return s.activation(hp_[:, K2T + w * 512: K2T + (w + 1) * 512], ps,
                                AF.Identity, bias=ctx.misc[:, IPB + 8 + hp3:IPB + 9 + hp3],
                                scale=1.0)

        linearB(ctx, f"ke_{hpair}", XT, WQ + 1024, D, P, [(sW, wv)], evac_k)

        def evac_v(i, ps, s, hp3=hpair):
            dst = bass.AP(hp_, V2 + i * 130, [[hp_.shape[1], P], [65, 2], [1, 64]])
            return s.activation(dst, ps[:, 0:P], AF.Identity,
                                bias=ctx.misc[:, IPB + 16 + hp3:IPB + 17 + hp3],
                                scale=1.0)

        linearA(ctx, f"ve_{hpair}", XT, WQ + 2048, D, P, [(sW, wv)], evac_v)

        emit_attn_pair(ctx, f"ate_{hpair}", hpair, Q2T, K2T, V2, OT,
                       0.125, PT, STG, causal=False)

    with nc.Block() as blk:
        @blk.gpsimd
        def _(g):
            load_w(ctx, g, ctx.win["L3A_opT"], WQ, D, D, sW)

    def evac_o(i, ps, veng):
        return veng.tensor_tensor(ctx.x[:, i * D:(i + 1) * D], ps,
                                  ctx.x[:, i * D:(i + 1) * D], ALU.add)

    linearA(ctx, "oe", OT, WQ, D, D, [(sW, ctx.val(sW))], evac_o,
            bias_row=OPBR, evac_eng="vector")
    with nc.Block() as blk:
        @blk.sync
        def _(sync):
            src = bass.AP(ctx.x, 0, [[NT * D, P], [D, NT], [1, D]])
            sync.dma_start(v1_out.rearrange("(i p) d -> p i d", p=P), src
                           ).then_inc(ctx.sem("xio"), 16)
            sync.wait_ge(ctx.sem("xio"), ctx.bump("xio", 16))


def _ln_gain_tile(ctx, tt, gcols, bcols, out_cb, bC0, bV0, nth):
    pass


def emit_l3_combine(ctx, gb_off, outa, outa_off, sbb_off):
    nc = ctx.nc
    hp_ = ctx.heap
    npertile = 5 if outa else 4
    bA0 = ctx.val("chS"); bV0 = ctx.val("chV")
    with nc.Block() as blk:
        @blk.scalar
        def _(s):
            scrA = ctx.misc[:, 0:1024]
            for i in range(NT):
                xt = ctx.x[:, i * D:(i + 1) * D]
                st = ctx.misc[:, 3072:3080]
                if i >= 1:
                    s.wait_ge(ctx.sem("chV"), bV0 + 5 * i)
                ins = s.activation(scrA, xt, AF.Copy, bias=0.0, scale=1.0 / D,
                                   accum_out=st[:, 0:1])
                ctx.edge(ins, s, "chS")
                s.wait_ge(ctx.sem("chV"), bV0 + 5 * i + 1)
                ins = s.activation(scrA, ctx.misc[:, 1024:2048], AF.Square,
                                   bias=0.0, scale=1.0, accum_out=st[:, 1:2])
                ctx.edge(ins, s, "chS")
                ins = s.activation(st[:, 2:3], st[:, 1:2], AF.Ln, bias=1e-5,
                                   scale=1.0 / D)
                ctx.edge(ins, s, "chS")
                ins = s.activation(st[:, 3:4], st[:, 2:3], AF.Exp, bias=0.0,
                                   scale=-0.5)
                ctx.edge(ins, s, "chS")
                if outa:
                    ins = s.activation(ctx.misc[:, 2048:3072],
                                       hp_[:, outa_off + i * D: outa_off + (i + 1) * D],
                                       AF.Identity, bias=0.0, scale=1.0)
                    ctx.edge(ins, s, "chS")

        @blk.vector
        def _(v):
            for i in range(NT):
                xt = ctx.x[:, i * D:(i + 1) * D]
                st = ctx.misc[:, 3072:3080]
                base_i = bA0 + npertile * i
                v.wait_ge(ctx.sem("chS"), base_i + 1)
                ins = v.tensor_scalar(ctx.misc[:, 1024:2048], xt, st[:, 0:1],
                                      None, ALU.subtract)
                ctx.edge(ins, v, "chV")
                v.wait_ge(ctx.sem("chS"), base_i + 4)
                ins = v.tensor_scalar(ctx.misc[:, 1024:2048],
                                      ctx.misc[:, 1024:2048],
                                      st[:, 3:4], None, ALU.mult)
                ctx.edge(ins, v, "chV")
                ins = v.tensor_copy(hp_[:, sbb_off:sbb_off + 1024],
                                    ctx.misc[:, 1024:2048])
                ctx.edge(ins, v, "chV")
                ins = v.tensor_tensor(ctx.ps[:, 0:1024], hp_[:, sbb_off:sbb_off + 1024],
                                      hp_[:, gb_off:gb_off + 1024], ALU.mult)
                ctx.edge(ins, v, "chV")
                if outa:
                    v.wait_ge(ctx.sem("chS"), base_i + 5)
                ins = v.tensor_add(xt, ctx.ps[:, 0:1024],
                                   ctx.misc[:, 2048:3072])
                ctx.edge(ins, v, "chV")


def prog_l3ffn(ctx):
    nc = ctx.nc
    hp_ = ctx.heap
    XT, OUTA, HT = 0, 16384, 32768
    W1S, W2S, L2BR = 40960, 45056, 49152
    G1B, G2B, SBB = 50176, 51200, 52224
    LN1WR, LN1BR, LN2WR, LN2BR2 = 53248, 54272, 55296, 56320
    L1B, LN1W, LN1B, WSEL = 3600, 3640, 3648, 3672
    sW = "wdma"
    d_out = nc.dram_tensor("delta_out", [T, D], F32, kind="ExternalOutput")
    with nc.Block() as blk:
        @blk.gpsimd
        def _(g):
            load_cols(ctx, g, ctx.win["L3F_l1bc"], L1B, 32, sW)
            load_cols(ctx, g, ctx.win["L3F_ln1w"], LN1W, ND, sW)
            load_cols(ctx, g, ctx.win["L3F_ln1b"], LN1B, ND, sW)
            load_cols(ctx, g, ctx.win["L3F_wsel"], WSEL, 1, sW)
            load_row(ctx, g, ctx.win["L3F_l2br"], L2BR, D, sW)
            load_row(ctx, g, ctx.win["L3F_ln1wr"], LN1WR, D, sW)
            load_row(ctx, g, ctx.win["L3F_ln1br"], LN1BR, D, sW)
            load_row(ctx, g, ctx.win["L3F_ln2wr"], LN2WR, D, sW)
            load_row(ctx, g, ctx.win["L3F_ln2br"], LN2BR2, D, sW)
        wv = ctx.val(sW)

        @blk.vector
        def _(v):
            v.wait_ge(ctx.sem(sW), wv)

    bS = ctx.val("atS")
    with nc.Block() as blk:
        @blk.tensor
        def _(t):
            for half in range(4):
                src = (LN1WR, LN2WR)[half // 2] + (half % 2) * 512
                ins = t.matmul(ctx.ps[:, half * 512:(half + 1) * 512],
                               ctx.ones1[0:1, 0:P], hp_[0:1, src:src + 512],
                               start=True, stop=True)
            ins.then_inc(ctx.sem("atS"), 1)

        @blk.scalar
        def _(s):
            s.wait_ge(ctx.sem("atS"), bS + 1)
            s.copy(hp_[:, G1B:G1B + 1024], ctx.ps[:, 0:1024])
            s.activation(hp_[:, G2B:G2B + 1024], ctx.ps[:, 1024:2048],
                         AF.Identity, bias=0.0,
                         scale=ctx.misc[:, WSEL:WSEL + 1])
    ctx.bump("atS", 1)

    emit_ln(ctx, xt_off=XT, tag="f1", gcols=LN1W, bcols=LN1B)
    emit_ffn(ctx, "f", XT, ctx.win["L3F_l1T"], ctx.win["L3F_l2T"], L1B,
             [L2BR, LN1BR], OUTA, HT, W1S, W2S, Hff=4096, act=AF.Relu)

    emit_l3_combine(ctx, G1B, True, OUTA, SBB)

    bS = ctx.val("atS")
    with nc.Block() as blk:
        @blk.tensor
        def _(t):
            for half in range(2):
                ins = t.matmul(ctx.ps[:, half * 512:(half + 1) * 512],
                               ctx.ones1[0:1, 0:P],
                               hp_[0:1, LN2BR2 + half * 512:LN2BR2 + half * 512 + 512],
                               start=True, stop=True)
            ins.then_inc(ctx.sem("atS"), 1)

        @blk.scalar
        def _(s):
            s.wait_ge(ctx.sem("atS"), bS + 1)
            s.activation(ctx.misc[:, 2048:3072], ctx.ps[:, 0:1024],
                         AF.Identity, bias=0.0,
                         scale=ctx.misc[:, WSEL:WSEL + 1])
    ctx.bump("atS", 1)

    emit_l3_combine(ctx, G2B, False, OUTA, SBB)
    with nc.Block() as blk:
        @blk.sync
        def _(sync):
            src = bass.AP(ctx.x, 0, [[NT * D, P], [D, NT], [1, D]])
            sync.dma_start(d_out.rearrange("(i p) d -> p i d", p=P), src
                           ).then_inc(ctx.sem("xio"), 16)
            sync.wait_ge(ctx.sem("xio"), ctx.bump("xio", 16))


def prog_l3gate(ctx):
    nc = ctx.nc
    hp_ = ctx.heap
    GWT, GBR, XMROW, XMT = 0, 64, 128, 2176
    GLOG = 3600
    sW = "wdma"
    g_out = nc.dram_tensor("gates_out", [1, 4], F32, kind="ExternalOutput")
    with nc.Block() as blk:
        @blk.gpsimd
        def _(g):
            load_w(ctx, g, ctx.win["L3G_gwT"], GWT, D, 4, sW)
            load_row(ctx, g, ctx.win["L3G_gbr"], GBR, 4, sW)
        wv = ctx.val(sW)

        @blk.tensor
        def _(t):
            t.wait_ge(ctx.sem(sW), wv)
            ins = None
            for tt in range(NT):
                for nn in range(2):
                    ins = t.matmul(ctx.ps[0:1, nn * 512:(nn + 1) * 512],
                                   ctx.onescf[:, 0:1],
                                   ctx.x[:, tt * D + nn * 512: tt * D + (nn + 1) * 512],
                                   start=(tt == 0), stop=(tt == NT - 1))
            ins.then_inc(ctx.sem("atS"), 1)
    ctx.bump("atS", 1)
    bS = ctx.val("atS"); bTR = ctx.val("atTR"); bOE = ctx.val("atOE")
    bC = ctx.val("chS")
    with nc.Block() as blk:
        @blk.scalar
        def _(s):
            s.wait_ge(ctx.sem("atS"), bS)
            ins = s.activation(hp_[0:1, XMROW:XMROW + D], ctx.ps[0:1, 0:1024],
                               AF.Identity, bias=0.0, scale=1.0 / 2048.0)
            ctx.edge(ins, s, "chS")

        @blk.tensor
        def _(t):
            t.wait_ge(ctx.sem("chS"), bC + 1)
            for j in range(ND):
                if j >= 2:
                    t.wait_ge(ctx.sem("atOE"), bOE + j - 1)
                ins = t.transpose(ctx.ps_bf[0:P, (j % 2) * 1024:(j % 2) * 1024 + 1],
                                  hp_[0:1, XMROW + j * P: XMROW + (j + 1) * P],
                                  ctx.ident[0:1, 0:1])
                ins.then_inc(ctx.sem("atTR"), 1)
            t.wait_ge(ctx.sem("atOE"), bOE + ND)
            ins = None
            for j in range(ND):
                ins = t.matmul(ctx.ps[0:1, 1024:1028],
                               hp_[:, XMT + j: XMT + j + 1],
                               hp_[:, GWT + j * 4: GWT + (j + 1) * 4],
                               start=(j == 0), stop=False)
            ins = t.matmul(ctx.ps[0:1, 1024:1028], ctx.ones1[0:1, 0:1],
                           hp_[0:1, GBR:GBR + 4], start=False, stop=True)
            ins.then_inc(ctx.sem("atS"), 1)

        @blk.scalar
        def _(s2):
            for j in range(ND):
                s2.wait_ge(ctx.sem("atTR"), bTR + j + 1)
                ins = s2.copy(hp_[:, XMT + j: XMT + j + 1],
                              ctx.ps_bf[0:P, (j % 2) * 1024:(j % 2) * 1024 + 1])
                ins.then_inc(ctx.sem("atOE"), 1)
            s2.wait_ge(ctx.sem("atS"), bS + 1)
            ins = s2.copy(ctx.misc[0:1, GLOG:GLOG + 4], ctx.ps[0:1, 1024:1028])
            ins.then_inc(ctx.sem("pfC"), 1)
    ctx.bump("atS", 1)
    ctx.bump("atTR", ND)
    ctx.bump("atOE", ND)
    ctx.bump("pfC", 1)
    bPC = ctx.val("pfC")
    with nc.Block() as blk:
        @blk.vector
        def _(v):
            ms = ctx.misc
            L = GLOG
            v.wait_ge(ctx.sem("pfC"), bPC)
            ins = v.tensor_tensor(ms[0:1, L + 4:L + 6], ms[0:1, L:L + 2],
                                  ms[0:1, L + 2:L + 4], ALU.max)
            ctx.edge(ins, v, "chV")
            ins = v.tensor_tensor(ms[0:1, L + 6:L + 7], ms[0:1, L + 4:L + 5],
                                  ms[0:1, L + 5:L + 6], ALU.max)
            ctx.edge(ins, v, "chV")
            ins = v.tensor_scalar(ms[0:1, L + 8:L + 12], ms[0:1, L:L + 4],
                                  ms[0:1, L + 6:L + 7], None, ALU.is_ge)
            ctx.edge(ins, v, "chV")
            ins = v.scalar_tensor_tensor(ms[0:1, L + 8:L + 12], ms[0:1, L + 8:L + 12],
                                         -1e30, ms[0:1, L:L + 4], ALU.mult, ALU.add)
            ctx.edge(ins, v, "chV")
            ins = v.tensor_tensor(ms[0:1, L + 12:L + 14], ms[0:1, L + 8:L + 10],
                                  ms[0:1, L + 10:L + 12], ALU.max)
            ctx.edge(ins, v, "chV")
            ins = v.tensor_tensor(ms[0:1, L + 14:L + 15], ms[0:1, L + 12:L + 13],
                                  ms[0:1, L + 13:L + 14], ALU.max)
            ctx.edge(ins, v, "chV")
            ins = v.tensor_scalar(ms[0:1, L + 16:L + 20], ms[0:1, L:L + 4],
                                  ms[0:1, L + 14:L + 15], None, ALU.is_ge)
            ctx.edge(ins, v, "chV")
            ins = v.tensor_scalar(ms[0:1, L + 20:L + 21], ms[0:1, L + 6:L + 7],
                                  -1.0, None, ALU.mult)
            ins.then_inc(ctx.sem("pfB"), 1)

        @blk.scalar
        def _(s):
            s.wait_ge(ctx.sem("pfB"), ctx.val("pfB") + 1)
            ins = s.activation(ctx.misc[0:1, GLOG + 24:GLOG + 28],
                               ctx.misc[0:1, GLOG:GLOG + 4], AF.Exp,
                               bias=ctx.misc[0:1, GLOG + 20:GLOG + 21], scale=1.0,
                               accum_out=ctx.misc[0:1, GLOG + 28:GLOG + 29])
            ins.then_inc(ctx.sem("chS"), 1)
            ctx.bump("chS")
    ctx.bump("pfB", 1)
    bC2 = ctx.val("chS")
    with nc.Block() as blk:
        @blk.vector
        def _(v):
            ms = ctx.misc
            L = GLOG
            v.wait_ge(ctx.sem("chS"), bC2)
            ins = v.reciprocal(ms[0:1, L + 29:L + 30], ms[0:1, L + 28:L + 29])
            ctx.edge(ins, v, "chV")
            ins = v.tensor_tensor(ms[0:1, L + 24:L + 28], ms[0:1, L + 24:L + 28],
                                  ms[0:1, L + 16:L + 20], ALU.mult)
            ctx.edge(ins, v, "chV")
            ins = v.tensor_scalar(ms[0:1, L + 24:L + 28], ms[0:1, L + 24:L + 28],
                                  ms[0:1, L + 29:L + 30], None, ALU.mult)
            ins.then_inc(ctx.sem("chV"), 1)
            ctx.bump("chV")

        @blk.sync
        def _(sync):
            sync.wait_ge(ctx.sem("chV"), ctx.val("chV"))
            sync.dma_start(g_out[:, :], ctx.misc[0:1, GLOG + 24:GLOG + 28]
                           ).then_inc(ctx.sem("xio"), 16)
            sync.wait_ge(ctx.sem("xio"), ctx.bump("xio", 16))


def prog_sumN(ctx, n_terms):
    nc = ctx.nc
    x_out = nc.dram_tensor("x_out2", [T, D], F32, kind="ExternalOutput")
    dts = [nc.dram_tensor(f"dsum_{i}", [T, D], F32, kind="ExternalInput")
           for i in range(n_terms)]
    bW = ctx.val("xio")
    bC = ctx.val("chV")
    with nc.Block() as blk:
        @blk.sync
        def _(sync):
            k = 0
            for tt in range(NT):
                for dt in dts:
                    if k >= 1:
                        sync.wait_ge(ctx.sem("chV"), bC + k)
                    sync.dma_start(ctx.misc[:, 1024 + (k % 2) * 1024: 1024 + (k % 2) * 1024 + 1024],
                                   dt.rearrange("(i p) d -> p i d", p=P)[:, tt]
                                   ).then_inc(ctx.sem("xio"), 16)
                    ctx.bump("xio", 16)
                    k += 1

        @blk.vector
        def _(v):
            k = 0
            for tt in range(NT):
                for i in range(len(dts)):
                    v.wait_ge(ctx.sem("xio"), bW + 16 * (k + 1))
                    ins = v.tensor_add(ctx.x[:, tt * D:(tt + 1) * D],
                                       ctx.x[:, tt * D:(tt + 1) * D],
                                       ctx.misc[:, 1024 + (k % 2) * 1024: 1024 + (k % 2) * 1024 + 1024])
                    ctx.edge(ins, v, "chV")
                    k += 1
    with nc.Block() as blk:
        @blk.sync
        def _(sync):
            src = bass.AP(ctx.x, 0, [[NT * D, P], [D, NT], [1, D]])
            sync.dma_start(x_out.rearrange("(i p) d -> p i d", p=P), src
                           ).then_inc(ctx.sem("xio"), 16)
            sync.wait_ge(ctx.sem("xio"), ctx.bump("xio", 16))



def prog_l0route(ctx):
    nc = ctx.nc
    hp_ = ctx.heap
    XT, LOG, MSK = 0, 16384, 18432
    sW = "wdma"
    LT0 = 1024
    LO, HI, MID, CNT, SEL, TGT = 3600, 3610, 3620, 3630, 3640, 3650
    qs_out = nc.dram_tensor("qs_out", [P, 64], F32, kind="ExternalOutput")
    kvs_out = nc.dram_tensor("kvs_out", [P, 64], F32, kind="ExternalOutput")
    km_out = nc.dram_tensor("km_out", [P, 64], F32, kind="ExternalOutput")
    fac_out = nc.dram_tensor("fac_out", [P, 16], F32, kind="ExternalOutput")
    nsel_out = nc.dram_tensor("nsel_out", [P, 16], F32, kind="ExternalOutput")
    emit_ln(ctx, xt_off=XT, tag="l0r")
    with nc.Block() as blk:
        @blk.gpsimd
        def _(g):
            load_w(ctx, g, ctx.win["L0R_w8"], 40960, D, 8, sW)
            g.dma_start(bass.AP(ctx.misc, TGT, [[ctx.misc.shape[1], 8], [1, 1]]),
                        ctx.win["L0R_tgt"][:, :]).then_inc(ctx.sem(sW), 16)
            ctx.bump(sW, 16)

    def evac_l(i, ps, s):
        return s.copy(hp_[:, LOG + i * 8: LOG + (i + 1) * 8], ps[:, 0:8])

    linearA(ctx, "l0log", XT, 40960, D, 8, [(sW, ctx.val(sW))], evac_l)

    bTR = ctx.val("atTR"); bOE = ctx.val("atOE")
    with nc.Block() as blk:
        @blk.tensor
        def _(t):
            for tt in range(NT):
                if tt >= 2:
                    t.wait_ge(ctx.sem("atOE"), bOE + tt - 1)
                ins = t.transpose(ctx.ps_bf[0:8, (tt % 2) * 1024:(tt % 2) * 1024 + P],
                                  hp_[:, LOG + tt * 8: LOG + (tt + 1) * 8],
                                  ctx.ident[:, :])
                ins.then_inc(ctx.sem("atTR"), 1)

        @blk.scalar
        def _(s):
            for tt in range(NT):
                s.wait_ge(ctx.sem("atTR"), bTR + tt + 1)
                ins = s.copy(ctx.misc[0:8, LT0 + tt * P: LT0 + (tt + 1) * P],
                             ctx.ps_bf[0:8, (tt % 2) * 1024:(tt % 2) * 1024 + P])
                ins.then_inc(ctx.sem("atOE"), 1)
    ctx.bump("atTR", NT)
    ctx.bump("atOE", NT)

    with nc.Block() as blk:
        @blk.vector
        def _(v):
            ms = ctx.misc
            v.memset(ms[0:8, LO:LO + 1], -60.0)
            ins = v.memset(ms[0:8, HI:HI + 1], 60.0)
            ctx.edge(ins, v, "chV")
            for it in range(45):
                ins = v.tensor_tensor(ms[0:8, MID:MID + 1], ms[0:8, LO:LO + 1],
                                      ms[0:8, HI:HI + 1], ALU.add)
                ctx.edge(ins, v, "chV")
                ins = v.tensor_scalar(ms[0:8, MID:MID + 1], ms[0:8, MID:MID + 1],
                                      0.5, None, ALU.mult)
                ctx.edge(ins, v, "chV")
                ins = v.tensor_scalar(hp_[0:8, MSK:MSK + 2048],
                                      ms[0:8, LT0:LT0 + 2048],
                                      ms[0:8, MID:MID + 1], None, ALU.is_ge)
                ins.then_inc(ctx.sem("pfB"), 1)
                v.wait_ge(ctx.sem("chS"), ctx.val("chS") + it + 1)
                ins = v.tensor_tensor(ms[0:8, SEL:SEL + 1], ms[0:8, CNT:CNT + 1],
                                      ms[0:8, TGT:TGT + 1], ALU.is_ge)
                ctx.edge(ins, v, "chV")
                ins = v.tensor_tensor(ms[0:8, SEL + 2:SEL + 3], ms[0:8, MID:MID + 1],
                                      ms[0:8, LO:LO + 1], ALU.subtract)
                ctx.edge(ins, v, "chV")
                ins = v.tensor_tensor(ms[0:8, SEL + 2:SEL + 3], ms[0:8, SEL:SEL + 1],
                                      ms[0:8, SEL + 2:SEL + 3], ALU.mult)
                ctx.edge(ins, v, "chV")
                ins = v.tensor_tensor(ms[0:8, LO:LO + 1], ms[0:8, LO:LO + 1],
                                      ms[0:8, SEL + 2:SEL + 3], ALU.add)
                ctx.edge(ins, v, "chV")
                ins = v.tensor_tensor(ms[0:8, SEL + 4:SEL + 5], ms[0:8, HI:HI + 1],
                                      ms[0:8, MID:MID + 1], ALU.subtract)
                ctx.edge(ins, v, "chV")
                ins = v.tensor_tensor(ms[0:8, SEL + 4:SEL + 5], ms[0:8, SEL:SEL + 1],
                                      ms[0:8, SEL + 4:SEL + 5], ALU.mult)
                ctx.edge(ins, v, "chV")
                ins = v.tensor_tensor(ms[0:8, HI:HI + 1], ms[0:8, MID:MID + 1],
                                      ms[0:8, SEL + 4:SEL + 5], ALU.add)
                ctx.edge(ins, v, "chV")
            ins = v.tensor_scalar(hp_[0:8, MSK:MSK + 2048], ms[0:8, LT0:LT0 + 2048],
                                  ms[0:8, LO:LO + 1], None, ALU.is_ge)
            ins.then_inc(ctx.sem("pfB"), 1)
            v.wait_ge(ctx.sem("chS"), ctx.val("chS") + 46 + 1)
            ins = v.tensor_tensor(hp_[0:8, MSK + 4096:MSK + 6144],
                                  hp_[0:8, MSK + 4096:MSK + 6144],
                                  hp_[0:8, MSK:MSK + 2048], ALU.mult)
            ctx.edge(ins, v, "chV")
            ins = v.tensor_scalar(hp_[0:8, MSK + 2048:MSK + 4096],
                                  hp_[0:8, MSK:MSK + 2048], 1e9, -1e9,
                                  ALU.mult, ALU.add)
            ins.then_inc(ctx.sem("pfB"), 1)

        @blk.scalar
        def _(s):
            bB = ctx.val("pfB")
            for it in range(45):
                s.wait_ge(ctx.sem("pfB"), bB + it + 1)
                ins = s.activation(hp_[0:8, 28672:28672 + 2048],
                                   hp_[0:8, MSK:MSK + 2048], AF.Identity,
                                   bias=0.0, scale=1.0,
                                   accum_out=ctx.misc[0:8, CNT:CNT + 1])
                ins.then_inc(ctx.sem("chS"), 1)
                ctx.bump("chS")
            s.wait_ge(ctx.sem("pfB"), bB + 46)
            ins = s.activation(hp_[0:8, MSK + 4096:MSK + 6144],
                               ctx.misc[0:8, LT0:LT0 + 2048], AF.Sigmoid,
                               bias=0.0, scale=1.0)
            ins.then_inc(ctx.sem("chS"), 1)
            ctx.bump("chS")
            ins = s.copy(ctx.misc[0:1, 0:1], ctx.misc[0:1, 0:1])
            ins.then_inc(ctx.sem("chS"), 1)
            ctx.bump("chS")
    ctx.bump("pfB", 47)

    bTR = ctx.val("atTR"); bOE = ctx.val("atOE"); bB = ctx.val("pfB")
    with nc.Block() as blk:
        @blk.tensor
        def _(t):
            k = 0
            for src in (MSK + 4096, MSK + 2048, MSK):
                for tt in range(NT):
                    if k >= 2:
                        t.wait_ge(ctx.sem("atOE"), bOE + k - 1)
                    ins = t.transpose(ctx.ps_bf[0:P, (k % 2) * 1024:(k % 2) * 1024 + 8],
                                      hp_[0:8, src + tt * P: src + (tt + 1) * P],
                                      ctx.ident[0:8, 0:8])
                    ins.then_inc(ctx.sem("atTR"), 1)
                    k += 1

        @blk.scalar
        def _(s):
            k = 0
            for dsto in (24576, 25600, 26624):
                for tt in range(NT):
                    s.wait_ge(ctx.sem("atTR"), bTR + k + 1)
                    ins = s.copy(hp_[:, dsto + tt * 8: dsto + (tt + 1) * 8],
                                 ctx.ps_bf[0:P, (k % 2) * 1024:(k % 2) * 1024 + 8])
                    ins.then_inc(ctx.sem("atOE"), 1)
                    k += 1
    ctx.bump("atTR", 3 * NT)
    ctx.bump("atOE", 3 * NT)
    bC = ctx.val("chV")
    with nc.Block() as blk:
        @blk.vector
        def _(v):
            ms = ctx.misc
            ins = v.tensor_tensor(ms[:, 0:16],
                                  bass.AP(hp_, 26624, [[hp_.shape[1], P], [8, 16]]),
                                  bass.AP(hp_, 26625, [[hp_.shape[1], P], [8, 16]]),
                                  ALU.add)
            ctx.edge(ins, v, "chV")
            ins = v.tensor_tensor(ms[:, 16:32],
                                  bass.AP(hp_, 26626, [[hp_.shape[1], P], [8, 16]]),
                                  bass.AP(hp_, 26627, [[hp_.shape[1], P], [8, 16]]),
                                  ALU.add)
            ctx.edge(ins, v, "chV")
            ins = v.tensor_tensor(ms[:, 0:16], ms[:, 0:16], ms[:, 16:32], ALU.add)
            ctx.edge(ins, v, "chV")
            ins = v.tensor_scalar(ms[:, 16:32], ms[:, 0:16], 0.0, None, ALU.is_gt)
            ctx.edge(ins, v, "chV")
            ins = v.tensor_scalar(ms[:, 32:48], ms[:, 0:16], 1e-5, None, ALU.max)
            ctx.edge(ins, v, "chV")
            ins = v.reciprocal(ms[:, 32:48], ms[:, 32:48])
            ins.then_inc(ctx.sem("chV"), 1)
            ctx.bump("chV")

        @blk.gpsimd
        def _(g):
            g.wait_ge(ctx.sem("chV"), ctx.val("chV"))
            g.dma_start(qs_out[:, :],
                        bass.AP(ctx.heap, 24576, [[ctx.heap.shape[1], P], [8, 16], [1, 4]])
                        ).then_inc(ctx.sem("wdma"), 16)
            g.dma_start(kvs_out[:, :],
                        bass.AP(ctx.heap, 24580, [[ctx.heap.shape[1], P], [8, 16], [1, 4]])
                        ).then_inc(ctx.sem("wdma"), 16)
            g.dma_start(km_out[:, :],
                        bass.AP(ctx.heap, 25604, [[ctx.heap.shape[1], P], [8, 16], [1, 4]])
                        ).then_inc(ctx.sem("wdma"), 16)
            g.dma_start(fac_out[:, :], ctx.misc[:, 32:48]).then_inc(ctx.sem("wdma"), 16)
            g.dma_start(nsel_out[:, :], ctx.misc[:, 16:32]).then_inc(ctx.sem("wdma"), 16)
            g.wait_ge(ctx.sem("wdma"), ctx.bump("wdma", 80))


def prog_l0g(ctx):
    nc = ctx.nc
    hp_ = ctx.heap
    XT, OT = 0, 16384
    Q2T, K2T, V2 = 24576, 26624, 28672
    PT, STG = 30752, 31776
    KN, VN = 32032, 32064
    WQ = 32832
    WOx = 36928
    QSC, KVC, KMC = 3600, 3620, 3640
    sW = "wdma"
    d_out = nc.dram_tensor("d_out", [T, D], F32, kind="ExternalOutput")
    emit_ln(ctx, xt_off=XT, tag="l0g")
    with nc.Block() as blk:
        @blk.gpsimd
        def _(g):
            load_cols(ctx, g, ctx.win["L0G_qs"], QSC, 16, sW)
            load_cols(ctx, g, ctx.win["L0G_kvs"], KVC, 16, sW)
            load_cols(ctx, g, ctx.win["L0G_km"], KMC, 16, sW)
            g.dma_start(bass.AP(hp_, KN, [[hp_.shape[1], P], [1, 8]]),
                        ctx.win["L0G_kn"][:, :]).then_inc(ctx.sem(sW), 16)
            ctx.bump(sW, 16)
            for gh in range(8):
                g.dma_start(bass.AP(hp_, VN + 200 + gh * 68, [[hp_.shape[1], 1], [1, 64]]),
                            ctx.win["L0G_vn"][gh:gh + 1, :]).then_inc(ctx.sem(sW), 16)
                ctx.bump(sW, 16)
            load_w(ctx, g, ctx.win["L0G_woT"], WOx, 512, D, sW)

        @blk.vector
        def _(v):
            v.memset(bass.AP(hp_, V2 + 64, [[hp_.shape[1], P], [65, 2 * NT], [1, 1]]), 1.0)
            v.memset(bass.AP(hp_, VN + 200 + 64, [[hp_.shape[1], 1], [68, 8], [1, 1]]), 1.0)
    VROW = VN + 200

    def kmbias(qig, ki):
        return ctx.misc[:, KMC + ki:KMC + ki + 1]

    for hpair in range(4):
        with nc.Block() as blk:
            @blk.gpsimd
            def _(g):
                load_w(ctx, g, ctx.win["L0G_wqT"][:, hpair * P:(hpair + 1) * P], WQ, D, P, sW)
                load_w(ctx, g, ctx.win["L0G_wkvT"][:, hpair * P:(hpair + 1) * P], WQ + 1024, D, P, sW)
                load_w(ctx, g, ctx.win["L0G_wkvT"][:, 512 + hpair * P: 512 + (hpair + 1) * P], WQ + 2048, D, P, sW)
        wv = ctx.val(sW)

        def evac_q(mi, w, ps, s):
            return s.copy(hp_[:, Q2T + w * 512: Q2T + (w + 1) * 512], ps)

        linearB(ctx, f"q0_{hpair}", XT, WQ, D, P, [(sW, wv)], evac_q)

        def evac_k(mi, w, ps, s):
            return s.copy(hp_[:, K2T + w * 512: K2T + (w + 1) * 512], ps)

        linearB(ctx, f"k0_{hpair}", XT, WQ + 1024, D, P, [(sW, wv)], evac_k)

        def evac_v(i, ps, s):
            dst = bass.AP(hp_, V2 + i * 130, [[hp_.shape[1], P], [65, 2], [1, 64]])
            return s.activation(dst, ps[:, 0:P], AF.Identity, bias=0.0,
                                scale=ctx.misc[:, KVC + i:KVC + i + 1])

        linearA(ctx, f"v0_{hpair}", XT, WQ + 2048, D, P, [(sW, wv)], evac_v)

        emit_attn_pair(ctx, f"a0_{hpair}", hpair, Q2T, K2T, V2, OT, 0.125,
                       PT, STG, causal=False, bias_fn=kmbias,
                       null_kcol=KN + 2 * hpair,
                       null_vrow=VROW + 2 * hpair * 68)

    def evac_o(i, ps, veng):
        return veng.tensor_scalar(ctx.x[:, i * D:(i + 1) * D], ps,
                                  ctx.misc[:, QSC + i:QSC + i + 1], None, ALU.mult)

    linearA(ctx, "o0", OT, WOx, 512, D, [(sW, ctx.val(sW))], evac_o,
            evac_eng="vector")
    with nc.Block() as blk:
        @blk.sync
        def _(sync):
            src = bass.AP(ctx.x, 0, [[NT * D, P], [D, NT], [1, D]])
            sync.dma_start(d_out.rearrange("(i p) d -> p i d", p=P), src
                           ).then_inc(ctx.sem("xio"), 16)
            sync.wait_ge(ctx.sem("xio"), ctx.bump("xio", 16))


def prog_l0fin(ctx):
    nc = ctx.nc
    hp_ = ctx.heap
    sW = "wdma"
    x_out = nc.dram_tensor("x_out2", [T, D], F32, kind="ExternalOutput")
    dts = [nc.dram_tensor(f"dsum_{i}", [T, D], F32, kind="ExternalInput")
           for i in range(4)]
    FAC, NSL, NTB = 3600, 3620, 2048
    with nc.Block() as blk:
        @blk.gpsimd
        def _(g):
            load_cols(ctx, g, ctx.win["L0F_fac"], FAC, 16, sW)
            load_cols(ctx, g, ctx.win["L0F_nsel"], NSL, 16, sW)
            load_row(ctx, g, ctx.win["L0F_ntr"], 0, D, sW)
        wv = ctx.val(sW)
        bS0 = ctx.val("atS")

        @blk.tensor
        def _(t):
            t.wait_ge(ctx.sem(sW), wv)
            ins = None
            for nn in range(2):
                ins = t.matmul(ctx.ps[:, nn * 512:(nn + 1) * 512],
                               ctx.ones1[0:1, 0:P],
                               hp_[0:1, nn * 512:(nn + 1) * 512],
                               start=True, stop=True)
            ins.then_inc(ctx.sem("atS"), 1)

        @blk.scalar
        def _(s):
            s.wait_ge(ctx.sem("atS"), bS0 + 1)
            ins = s.copy(ctx.misc[:, NTB:NTB + 1024], ctx.ps[:, 0:1024])
            ins.then_inc(ctx.sem("chS"), 1)
            ctx.bump("chS")

        @blk.vector
        def _(v):
            v.wait_ge(ctx.sem(sW), wv)
            ins = v.tensor_tensor(ctx.misc[:, FAC:FAC + 16], ctx.misc[:, FAC:FAC + 16],
                                  ctx.misc[:, NSL:NSL + 16], ALU.mult)
            ctx.edge(ins, v, "chV")
            ins = v.tensor_scalar(ctx.misc[:, NSL:NSL + 16], ctx.misc[:, NSL:NSL + 16],
                                  -1.0, 1.0, ALU.mult, ALU.add)
            ins.then_inc(ctx.sem("chV"), 1)
            ctx.bump("chV")
    ctx.bump("atS", 1)
    bW = ctx.val("xio")
    bC = ctx.val("chV")
    bCS = ctx.val("chS")
    with nc.Block() as blk:
        def ngate(j):
            return 7 * (j // 4) + (j % 4) + 1

        @blk.sync
        def _(sync):
            k = 0
            for tt in range(NT):
                for i in range(4):
                    if k >= 1:
                        sync.wait_ge(ctx.sem("chV"), bC + ngate(k - 1))
                    sync.dma_start(ctx.misc[:, 1024:2048],
                                   dts[i].rearrange("(i p) d -> p i d", p=P)[:, tt]
                                   ).then_inc(ctx.sem("xio"), 16)
                    ctx.bump("xio", 16)
                    k += 1

        @blk.vector
        def _(v):
            v.wait_ge(ctx.sem("chS"), bCS)
            k = 0
            for tt in range(NT):
                for i in range(4):
                    v.wait_ge(ctx.sem("xio"), bW + 16 * (k + 1))
                    if i == 0:
                        ins = v.tensor_copy(ctx.misc[:, 0:1024], ctx.misc[:, 1024:2048])
                    else:
                        ins = v.tensor_add(ctx.misc[:, 0:1024], ctx.misc[:, 0:1024],
                                           ctx.misc[:, 1024:2048])
                    ins.then_inc(ctx.sem("chV"), 1)
                    ctx.bump("chV")
                    k += 1
                v.wait_ge(ctx.sem("chV"), ctx.val("chV"))
                ins = v.tensor_scalar(ctx.misc[:, 0:1024], ctx.misc[:, 0:1024],
                                      ctx.misc[:, FAC + tt:FAC + tt + 1], None, ALU.mult)
                ctx.edge(ins, v, "chV")
                ins = v.scalar_tensor_tensor(ctx.misc[:, 0:1024],
                                             ctx.misc[:, NTB:NTB + 1024],
                                             ctx.misc[:, NSL + tt:NSL + tt + 1],
                                             ctx.misc[:, 0:1024], ALU.mult, ALU.add)
                ctx.edge(ins, v, "chV")
                ins = v.tensor_add(ctx.x[:, tt * D:(tt + 1) * D],
                                   ctx.x[:, tt * D:(tt + 1) * D], ctx.misc[:, 0:1024])
                ctx.edge(ins, v, "chV")
    with nc.Block() as blk:
        @blk.sync
        def _(sync):
            src = bass.AP(ctx.x, 0, [[NT * D, P], [D, NT], [1, D]])
            sync.dma_start(x_out.rearrange("(i p) d -> p i d", p=P), src
                           ).then_inc(ctx.sem("xio"), 16)
            sync.wait_ge(ctx.sem("xio"), ctx.bump("xio", 16))



def make_L0_ffn_grp():
    pre = "L0_ffn_"

    def host(inputs):
        w8 = np.concatenate([np.asarray(inputs[pre + "w_qr"]).T,
                             np.asarray(inputs[pre + "w_kvr"]).T], 1)
        nkv = np.asarray(inputs[pre + "null_kv"], np.float32)
        kn = np.zeros((P, 32), np.float32)
        vn = np.zeros((32, 64), np.float32)
        for g in range(4):
            for gh in range(8):
                kn[(gh % 2) * 64:(gh % 2) * 64 + 64, g * 8 + gh] = nkv[0, g, gh, 0]
                vn[g * 8 + gh] = nkv[1, g, gh, 0]
        wq = np.asarray(inputs[pre + "w_q"], np.float32)
        wkv = np.asarray(inputs[pre + "w_kv"], np.float32)
        wo = np.asarray(inputs[pre + "w_out"], np.float32)
        return {
            pre + "w8": np.ascontiguousarray(w8),
            pre + "tgt": np.asarray([[1024.0]] * 4 + [[512.0]] * 4, np.float32),
            pre + "kn": kn, pre + "vn": vn,
            pre + "ntr": np.ascontiguousarray(
                np.asarray(inputs[pre + "null_token"], np.float32).reshape(1, D)),
            pre + "wqT": np.ascontiguousarray(
                np.concatenate([wq[g].T for g in range(4)], 1)),
            pre + "wkvT": np.ascontiguousarray(
                np.concatenate([wkv[g].T for g in range(4)], 1)),
            pre + "woT": np.ascontiguousarray(
                np.concatenate([wo[g].T for g in range(4)], 1)),
        }

    def decl(ctx):
        nc = ctx.nc
        ctx.win[pre + "w8"] = nc.dram_tensor(pre + "w8", [D, 8], F32, kind="ExternalInput")
        ctx.win[pre + "tgt"] = nc.dram_tensor(pre + "tgt", [8, 1], F32, kind="ExternalInput")
        ctx.win[pre + "kn"] = nc.dram_tensor(pre + "kn", [P, 32], F32, kind="ExternalInput")
        ctx.win[pre + "vn"] = nc.dram_tensor(pre + "vn", [32, 64], F32, kind="ExternalInput")
        ctx.win[pre + "ntr"] = nc.dram_tensor(pre + "ntr", [1, D], F32, kind="ExternalInput")
        ctx.win[pre + "wqT"] = nc.dram_tensor(pre + "wqT", [D, 2048], F32, kind="ExternalInput")
        ctx.win[pre + "wkvT"] = nc.dram_tensor(pre + "wkvT", [D, 4096], F32, kind="ExternalInput")
        ctx.win[pre + "woT"] = nc.dram_tensor(pre + "woT", [512, 4096], F32, kind="ExternalInput")

    def emit(ctx, uid):
        nc = ctx.nc
        hp_ = ctx.heap
        XT, OT, LOG = 0, 16384, 24576
        MSKR, SIGR, KMR, DMP = 24704, 26752, 28800, 30848
        Q2T, K2T, V2 = 32896, 34944, 36992
        PT, STG = 39104, 40128
        KN, VROWS, NTR, W8 = 41152, 41184, 43360, 44384
        WQ, WOx = 44448, 47520
        Sd, KMd, Md = 3600, 3728, 3856
        CNT16, NF0, NM0, SC0 = 3984, 4000, 4016, 4032
        TGT8, LO, HI, MID, CNT, SEL = 64, 66, 68, 70, 72, 74
        LT0 = 1024
        NTB = 2048
        sW = "wdma"
        ms = ctx.misc

        emit_ln(ctx, xt_off=XT, tag=uid)
        with nc.Block() as blk:
            @blk.gpsimd
            def _(g):
                load_w(ctx, g, ctx.win[pre + "w8"], W8, D, 8, sW)
                g.dma_start(bass.AP(ms, TGT8, [[MISC, 8], [1, 1]]),
                            ctx.win[pre + "tgt"][:, :]).then_inc(ctx.sem(sW), 16)
                ctx.bump(sW, 16)
                g.dma_start(bass.AP(hp_, KN, [[hp_.shape[1], P], [1, 32]]),
                            ctx.win[pre + "kn"][:, :]).then_inc(ctx.sem(sW), 16)
                ctx.bump(sW, 16)
                for r in range(32):
                    g.dma_start(bass.AP(hp_, VROWS + r * 68, [[hp_.shape[1], 1], [1, 64]]),
                                ctx.win[pre + "vn"][r:r + 1, :]).then_inc(ctx.sem(sW), 16)
                    ctx.bump(sW, 16)
                load_row(ctx, g, ctx.win[pre + "ntr"], NTR, D, sW)

            @blk.vector
            def _(v):
                v.memset(bass.AP(hp_, V2 + 64, [[hp_.shape[1], P], [65, 2 * NT], [1, 1]]), 1.0)
                v.memset(bass.AP(hp_, VROWS + 64, [[hp_.shape[1], 1], [68, 32], [1, 1]]), 1.0)

        def evac_l(i, ps, s):
            return s.copy(hp_[:, LOG + i * 8: LOG + (i + 1) * 8], ps[:, 0:8])

        linearA(ctx, f"lg{uid}", XT, W8, D, 8, [(sW, ctx.val(sW))], evac_l)

        bTR = ctx.val("atTR"); bOE = ctx.val("atOE")
        with nc.Block() as blk:
            @blk.tensor
            def _(t):
                for tt in range(NT):
                    if tt >= 2:
                        t.wait_ge(ctx.sem("atOE"), bOE + tt - 1)
                    ins = t.transpose(ctx.ps_bf[0:8, (tt % 2) * 1024:(tt % 2) * 1024 + P],
                                      hp_[:, LOG + tt * 8: LOG + (tt + 1) * 8],
                                      ctx.ident[:, :])
                    ins.then_inc(ctx.sem("atTR"), 1)

            @blk.scalar
            def _(s):
                for tt in range(NT):
                    s.wait_ge(ctx.sem("atTR"), bTR + tt + 1)
                    ins = s.copy(ms[0:8, LT0 + tt * P: LT0 + (tt + 1) * P],
                                 ctx.ps_bf[0:8, (tt % 2) * 1024:(tt % 2) * 1024 + P])
                    ins.then_inc(ctx.sem("atOE"), 1)
        ctx.bump("atTR", NT)
        ctx.bump("atOE", NT)

        with nc.Block() as blk:
            @blk.vector
            def _(v):
                v.memset(ms[0:8, LO:LO + 1], -60.0)
                ins = v.memset(ms[0:8, HI:HI + 1], 60.0)
                ctx.edge(ins, v, "chV")
                for it in range(45):
                    ins = v.tensor_tensor(ms[0:8, MID:MID + 1], ms[0:8, LO:LO + 1],
                                          ms[0:8, HI:HI + 1], ALU.add)
                    ctx.edge(ins, v, "chV")
                    ins = v.tensor_scalar(ms[0:8, MID:MID + 1], ms[0:8, MID:MID + 1],
                                          0.5, None, ALU.mult)
                    ctx.edge(ins, v, "chV")
                    ins = v.tensor_scalar(hp_[0:8, MSKR:MSKR + 2048],
                                          ms[0:8, LT0:LT0 + 2048],
                                          ms[0:8, MID:MID + 1], None, ALU.is_ge)
                    ins.then_inc(ctx.sem("pfB"), 1)
                    v.wait_ge(ctx.sem("chS"), ctx.val("chS") + it + 1)
                    ins = v.tensor_tensor(ms[0:8, SEL:SEL + 1], ms[0:8, CNT:CNT + 1],
                                          ms[0:8, TGT8:TGT8 + 1], ALU.is_ge)
                    ctx.edge(ins, v, "chV")
                    ins = v.tensor_tensor(ms[0:8, SEL + 2:SEL + 3], ms[0:8, MID:MID + 1],
                                          ms[0:8, LO:LO + 1], ALU.subtract)
                    ctx.edge(ins, v, "chV")
                    ins = v.tensor_tensor(ms[0:8, SEL + 2:SEL + 3], ms[0:8, SEL:SEL + 1],
                                          ms[0:8, SEL + 2:SEL + 3], ALU.mult)
                    ctx.edge(ins, v, "chV")
                    ins = v.tensor_tensor(ms[0:8, LO:LO + 1], ms[0:8, LO:LO + 1],
                                          ms[0:8, SEL + 2:SEL + 3], ALU.add)
                    ctx.edge(ins, v, "chV")
                    ins = v.tensor_tensor(ms[0:8, SEL + 4:SEL + 5], ms[0:8, HI:HI + 1],
                                          ms[0:8, MID:MID + 1], ALU.subtract)
                    ctx.edge(ins, v, "chV")
                    ins = v.tensor_tensor(ms[0:8, SEL + 4:SEL + 5], ms[0:8, SEL:SEL + 1],
                                          ms[0:8, SEL + 4:SEL + 5], ALU.mult)
                    ctx.edge(ins, v, "chV")
                    ins = v.tensor_tensor(ms[0:8, HI:HI + 1], ms[0:8, MID:MID + 1],
                                          ms[0:8, SEL + 4:SEL + 5], ALU.add)
                    ctx.edge(ins, v, "chV")
                ins = v.tensor_scalar(hp_[0:8, MSKR:MSKR + 2048], ms[0:8, LT0:LT0 + 2048],
                                      ms[0:8, LO:LO + 1], None, ALU.is_ge)
                ins.then_inc(ctx.sem("pfB"), 1)
                v.wait_ge(ctx.sem("chS"), ctx.val("chS") + 46 + 1)
                ins = v.tensor_tensor(hp_[0:8, SIGR:SIGR + 2048],
                                      hp_[0:8, SIGR:SIGR + 2048],
                                      hp_[0:8, MSKR:MSKR + 2048], ALU.mult)
                ctx.edge(ins, v, "chV")
                ins = v.tensor_scalar(hp_[0:8, KMR:KMR + 2048],
                                      hp_[0:8, MSKR:MSKR + 2048], 1e9, -1e9,
                                      ALU.mult, ALU.add)
                ins.then_inc(ctx.sem("pfB"), 1)

            @blk.scalar
            def _(s):
                bB = ctx.val("pfB")
                for it in range(45):
                    s.wait_ge(ctx.sem("pfB"), bB + it + 1)
                    ins = s.activation(hp_[0:8, DMP:DMP + 2048],
                                       hp_[0:8, MSKR:MSKR + 2048], AF.Identity,
                                       bias=0.0, scale=1.0,
                                       accum_out=ms[0:8, CNT:CNT + 1])
                    ins.then_inc(ctx.sem("chS"), 1)
                    ctx.bump("chS")
                s.wait_ge(ctx.sem("pfB"), bB + 46)
                ins = s.activation(hp_[0:8, SIGR:SIGR + 2048],
                                   ms[0:8, LT0:LT0 + 2048], AF.Sigmoid,
                                   bias=0.0, scale=1.0)
                ins.then_inc(ctx.sem("chS"), 1)
                ctx.bump("chS")
                ins = s.copy(ms[0:1, 0:1], ms[0:1, 0:1])
                ins.then_inc(ctx.sem("chS"), 1)
                ctx.bump("chS")
        ctx.bump("pfB", 47)

        bTR = ctx.val("atTR"); bOE = ctx.val("atOE")
        with nc.Block() as blk:
            @blk.tensor
            def _(t):
                k = 0
                for src in (SIGR, KMR, MSKR):
                    for tt in range(NT):
                        if k >= 2:
                            t.wait_ge(ctx.sem("atOE"), bOE + k - 1)
                        ins = t.transpose(ctx.ps_bf[0:P, (k % 2) * 1024:(k % 2) * 1024 + 8],
                                          hp_[0:8, src + tt * P: src + (tt + 1) * P],
                                          ctx.ident[0:8, 0:8])
                        ins.then_inc(ctx.sem("atTR"), 1)
                        k += 1

            @blk.scalar
            def _(s):
                k = 0
                for dsto in (Sd, KMd, Md):
                    for tt in range(NT):
                        s.wait_ge(ctx.sem("atTR"), bTR + k + 1)
                        ins = s.copy(ms[:, dsto + tt * 8: dsto + (tt + 1) * 8],
                                     ctx.ps_bf[0:P, (k % 2) * 1024:(k % 2) * 1024 + 8])
                        ins.then_inc(ctx.sem("atOE"), 1)
                        k += 1
        ctx.bump("atTR", 3 * NT)
        ctx.bump("atOE", 3 * NT)

        bS0 = ctx.val("atS")
        with nc.Block() as blk:
            @blk.tensor
            def _(t):
                t.wait_ge(ctx.sem(sW), ctx.val(sW))
                ins = None
                for nn in range(2):
                    ins = t.matmul(ctx.ps[:, nn * 512:(nn + 1) * 512],
                                   ctx.ones1[0:1, 0:P],
                                   hp_[0:1, NTR + nn * 512: NTR + (nn + 1) * 512],
                                   start=True, stop=True)
                ins.then_inc(ctx.sem("atS"), 1)

            @blk.scalar
            def _(s):
                s.wait_ge(ctx.sem("atS"), bS0 + 1)
                ins = s.copy(ms[:, NTB:NTB + 1024], ctx.ps[:, 0:1024])
                ins.then_inc(ctx.sem("chS"), 1)
                ctx.bump("chS")

            @blk.vector
            def _(v):
                ins = v.tensor_tensor(ms[:, CNT16:CNT16 + 16],
                                      bass.AP(ms, Md, [[MISC, P], [8, 16]]),
                                      bass.AP(ms, Md + 1, [[MISC, P], [8, 16]]),
                                      ALU.add)
                ctx.edge(ins, v, "chV")
                ins = v.tensor_tensor(ms[:, NF0:NF0 + 16],
                                      bass.AP(ms, Md + 2, [[MISC, P], [8, 16]]),
                                      bass.AP(ms, Md + 3, [[MISC, P], [8, 16]]),
                                      ALU.add)
                ctx.edge(ins, v, "chV")
                ins = v.tensor_tensor(ms[:, CNT16:CNT16 + 16], ms[:, CNT16:CNT16 + 16],
                                      ms[:, NF0:NF0 + 16], ALU.add)
                ctx.edge(ins, v, "chV")
                ins = v.tensor_scalar(ms[:, NM0:NM0 + 16], ms[:, CNT16:CNT16 + 16],
                                      0.0, None, ALU.is_gt)
                ctx.edge(ins, v, "chV")
                ins = v.tensor_scalar(ms[:, NF0:NF0 + 16], ms[:, CNT16:CNT16 + 16],
                                      1e-5, None, ALU.max)
                ctx.edge(ins, v, "chV")
                ins = v.reciprocal(ms[:, NF0:NF0 + 16], ms[:, NF0:NF0 + 16])
                ctx.edge(ins, v, "chV")
                ins = v.tensor_tensor(ms[:, NF0:NF0 + 16], ms[:, NF0:NF0 + 16],
                                      ms[:, NM0:NM0 + 16], ALU.mult)
                ctx.edge(ins, v, "chV")
                for tt in range(NT):
                    ins = v.tensor_scalar(ms[:, SC0 + tt * 4:SC0 + (tt + 1) * 4],
                                          ms[:, Sd + tt * 8: Sd + tt * 8 + 4],
                                          ms[:, NF0 + tt:NF0 + tt + 1], None, ALU.mult)
                    ctx.edge(ins, v, "chV")
                ins = v.tensor_scalar(ms[:, NM0:NM0 + 16], ms[:, NM0:NM0 + 16],
                                      -1.0, 1.0, ALU.mult, ALU.add)
                ctx.edge(ins, v, "chV")
        ctx.bump("atS", 1)

        for g in range(4):
            with nc.Block() as blk:
                @blk.gpsimd
                def _(gg, g=g):
                    load_w(ctx, gg, ctx.win[pre + "woT"][:, g * 1024:(g + 1) * 1024],
                           WOx, 512, D, sW)
            for hpair in range(4):
                with nc.Block() as blk:
                    @blk.gpsimd
                    def _(gg, g=g, hpair=hpair):
                        wqt = ctx.win[pre + "wqT"]
                        wkvt = ctx.win[pre + "wkvT"]
                        load_w(ctx, gg, wqt[:, g * 512 + hpair * P: g * 512 + (hpair + 1) * P],
                               WQ, D, P, sW)
                        load_w(ctx, gg, wkvt[:, g * 1024 + hpair * P: g * 1024 + (hpair + 1) * P],
                               WQ + 1024, D, P, sW)
                        load_w(ctx, gg, wkvt[:, g * 1024 + 512 + hpair * P: g * 1024 + 512 + (hpair + 1) * P],
                               WQ + 2048, D, P, sW)
                wv = ctx.val(sW)

                def evac_q(mi, w, ps, s):
                    return s.copy(hp_[:, Q2T + w * 512: Q2T + (w + 1) * 512], ps)

                linearB(ctx, f"q{uid}{g}_{hpair}", XT, WQ, D, P, [(sW, wv)], evac_q)

                def evac_k(mi, w, ps, s):
                    return s.copy(hp_[:, K2T + w * 512: K2T + (w + 1) * 512], ps)

                linearB(ctx, f"k{uid}{g}_{hpair}", XT, WQ + 1024, D, P, [(sW, wv)], evac_k)

                def evac_v(i, ps, s, g=g):
                    dst = bass.AP(hp_, V2 + i * 130, [[hp_.shape[1], P], [65, 2], [1, 64]])
                    return s.activation(dst, ps[:, 0:P], AF.Identity, bias=0.0,
                                        scale=ms[:, Sd + i * 8 + 4 + g:Sd + i * 8 + 5 + g])

                linearA(ctx, f"v{uid}{g}_{hpair}", XT, WQ + 2048, D, P, [(sW, wv)], evac_v)

                def kmbias(qig, ki, g=g):
                    return ms[:, KMd + ki * 8 + 4 + g:KMd + ki * 8 + 5 + g]

                emit_attn_pair(ctx, f"a{uid}{g}_{hpair}", hpair, Q2T, K2T, V2, OT,
                               0.125, PT, STG, causal=False, bias_fn=kmbias,
                               null_kcol=KN + g * 8 + 2 * hpair,
                               null_vrow=VROWS + (g * 8 + 2 * hpair) * 68)

            def evac_o(i, ps, veng, g=g):
                return veng.scalar_tensor_tensor(ctx.x[:, i * D:(i + 1) * D], ps,
                                                 ms[:, SC0 + i * 4 + g:SC0 + i * 4 + g + 1],
                                                 ctx.x[:, i * D:(i + 1) * D],
                                                 ALU.mult, ALU.add)

            linearA(ctx, f"o{uid}{g}", OT, WOx, 512, D, [(sW, ctx.val(sW))], evac_o,
                    evac_eng="vector")

        with nc.Block() as blk:
            @blk.vector
            def _(v):
                for tt in range(NT):
                    ins = v.scalar_tensor_tensor(ctx.x[:, tt * D:(tt + 1) * D],
                                                 ms[:, NTB:NTB + 1024],
                                                 ms[:, NM0 + tt:NM0 + tt + 1],
                                                 ctx.x[:, tt * D:(tt + 1) * D],
                                                 ALU.mult, ALU.add)
                    ctx.edge(ins, v, "chV")

    return decl, host, emit


def prog_l2route(ctx):
    nc = ctx.nc
    hp_ = ctx.heap
    XT, LOG, OH = 0, 16384, 17408
    OFS = 24576
    W8 = 40960
    RB = 41088
    sW = "wdma"
    vsel_out = nc.dram_tensor("vsel_out", [128 * P, 8], F32, kind="ExternalOutput")
    ovf_out = nc.dram_tensor("ovf_out", [128 * P, 1], F32, kind="ExternalOutput")
    xf = ctx.win["L2R_xfull"]
    with nc.Block() as blk:
        @blk.gpsimd
        def _(g):
            load_w(ctx, g, ctx.win["L2R_rw"], W8, D, 8, sW)
            load_row(ctx, g, ctx.win["L2R_rb"], RB, 8, sW)

    for s8 in range(8):
        with nc.Block() as blk:
            @blk.sync
            def _(sync):
                dst = bass.AP(ctx.x, 0, [[NT * D, P], [D, NT], [1, D]])
                sync.dma_start(dst, xf.rearrange("(s i p) d -> s p i d", s=8, p=P)[s8]
                               ).then_inc(ctx.sem("xio"), 16)
                sync.wait_ge(ctx.sem("xio"), ctx.bump("xio", 16))
        emit_ln(ctx, xt_off=XT, tag=f"r{s8}")

        bMM, bEV = ctx.val("amm"), ctx.val("aev")
        bC = ctx.val("chV")
        with nc.Block() as blk:
            @blk.tensor
            def _(t):
                t.wait_ge(ctx.sem(sW), 32)
                for i in range(NT):
                    if i >= 2:
                        t.wait_ge(ctx.sem("aev"), bEV + i - 1)
                    b0 = (i % 2) * 2
                    ins = None
                    for j in range(ND):
                        ins = t.matmul(ctx.ps[:, b0 * 512: b0 * 512 + 8],
                                       hp_[:, XT + j * T + i * P: XT + j * T + (i + 1) * P],
                                       hp_[:, W8 + j * 8: W8 + (j + 1) * 8],
                                       start=(j == 0), stop=False)
                    ins = t.matmul(ctx.ps[:, b0 * 512: b0 * 512 + 8],
                                   ctx.ones1[0:1, 0:P], hp_[0:1, RB:RB + 8],
                                   start=False, stop=True)
                    ins.then_inc(ctx.sem("amm"), 1)
                    gt = s8 * NT + i
                    t.wait_ge(ctx.sem("aev"), bEV + i + 1)
                    ins = t.matmul(ctx.ps[0:1, 2048 + gt * 8: 2048 + gt * 8 + 8],
                                   ctx.onesc[:, 0:1],
                                   hp_[:, OH + gt * 8: OH + (gt + 1) * 8],
                                   start=True, stop=True)
                    ins.then_inc(ctx.sem("atS"), 1)

            @blk.scalar
            def _(s):
                for i in range(NT):
                    b0 = (i % 2) * 2
                    s.wait_ge(ctx.sem("amm"), bMM + i + 1)
                    ins = s.copy(hp_[:, LOG + (i % 2) * 8: LOG + (i % 2) * 8 + 8],
                                 ctx.ps[:, b0 * 512: b0 * 512 + 8])
                    ins.then_inc(ctx.sem("chS"), 1)
                    ctx.bump("chS")

            @blk.vector
            def _(v):
                bCS = ctx.val("chS") - NT
                for i in range(NT):
                    gt = s8 * NT + i
                    lg = hp_[:, LOG + (i % 2) * 8: LOG + (i % 2) * 8 + 8]
                    v.wait_ge(ctx.sem("chS"), bCS + i + 1)
                    ms = ctx.misc
                    ins = v.tensor_tensor(ms[:, 0:4], lg[:, 0:4], lg[:, 4:8], ALU.max)
                    ctx.edge(ins, v, "chV")
                    ins = v.tensor_tensor(ms[:, 4:6], ms[:, 0:2], ms[:, 2:4], ALU.max)
                    ctx.edge(ins, v, "chV")
                    ins = v.tensor_tensor(ms[:, 6:7], ms[:, 4:5], ms[:, 5:6], ALU.max)
                    ctx.edge(ins, v, "chV")
                    ins = v.tensor_scalar(hp_[:, OH + gt * 8: OH + (gt + 1) * 8],
                                          lg, ms[:, 6:7], None, ALU.is_ge)
                    ins.then_inc(ctx.sem("aev"), 1)
        ctx.bump("amm", NT)
        ctx.bump("aev", NT)
        ctx.bump("atS", NT)

    bS = ctx.val("atS"); bC = ctx.val("chS")
    with nc.Block() as blk:
        @blk.scalar
        def _(s):
            s.wait_ge(ctx.sem("atS"), bS)
            ins = s.copy(ctx.misc[0:1, 1024:1024 + 1024], ctx.ps[0:1, 2048:3072])
            ins.then_inc(ctx.sem("chS"), 1)
            ctx.bump("chS")
            ins = s.activation(ctx.misc[0:1, 3584:3584 + P], ctx.ones1[0:1, :],
                               AF.Identity, bias=0.0, scale=1.0)
            ctx.edge(ins, s, "chS")

        @blk.vector
        def _(v):
            v.wait_ge(ctx.sem("chS"), bC + 1)
            ms = ctx.misc
            ins = v.memset(ms[0:1, 2048:2056], 0.0)
            ctx.edge(ins, v, "chV")
            for gt in range(1, 128):
                ins = v.tensor_tensor(ms[0:1, 2048 + gt * 8: 2048 + (gt + 1) * 8],
                                      ms[0:1, 2048 + (gt - 1) * 8: 2048 + gt * 8],
                                      ms[0:1, 1024 + (gt - 1) * 8: 1024 + gt * 8],
                                      ALU.add)
                ctx.edge(ins, v, "chV")
            ins = v.tensor_copy(ms[0:1, 1020:1021], ms[0:1, 1020:1021])
            ins.then_inc(ctx.sem("pfC"), 1)
            ctx.bump("pfC")

    bPC = ctx.val("pfC")
    bTR = ctx.val("atTR"); bOE = ctx.val("atOE")
    bCS2 = ctx.val("chS"); bCV2 = ctx.val("chV")
    with nc.Block() as blk:
        @blk.tensor
        def _(t):
            t.wait_ge(ctx.sem("pfC"), bPC)
            for gt in range(128):
                if gt >= 2:
                    t.wait_ge(ctx.sem("chS"), bCS2 + 4 * gt - 6)
                ins = t.matmul(ctx.ps[:, (gt % 2) * 512:(gt % 2) * 512 + 8],
                               ctx.mtril[:, :],
                               hp_[:, OH + gt * 8: OH + (gt + 1) * 8],
                               start=True, stop=True)
                ins.then_inc(ctx.sem("atTR"), 1)
                ins = t.matmul(ctx.ps[:, 1024 + (gt % 2) * 512: 1024 + (gt % 2) * 512 + 8],
                               ctx.misc[0:1, 3584:3584 + P],
                               bass.AP(ctx.misc, 2048 + gt * 8, [[MISC, 1], [1, 8]]),
                               start=True, stop=True)
                ins.then_inc(ctx.sem("atTR"), 1)

        @blk.scalar
        def _(s):
            for gt in range(128):
                s.wait_ge(ctx.sem("atTR"), bTR + 2 * gt + 2)
                ins = s.copy(ctx.misc[:, 3072 + (gt % 2) * 16: 3072 + (gt % 2) * 16 + 8],
                             ctx.ps[:, (gt % 2) * 512:(gt % 2) * 512 + 8]) if False else None
                ins = s.copy(ctx.misc[:, 0:8], ctx.ps[:, (gt % 2) * 512:(gt % 2) * 512 + 8])
                ctx.edge(ins, s, "chS")
                ins = s.copy(ctx.misc[:, 8:16], ctx.ps[:, 1024 + (gt % 2) * 512: 1024 + (gt % 2) * 512 + 8])
                ctx.edge(ins, s, "chS")
                ins = s.copy(ctx.misc[:, 16:24], hp_[:, OH + gt * 8: OH + (gt + 1) * 8])
                ctx.edge(ins, s, "chS")
                s.wait_ge(ctx.sem("chV"), bCV2 + 6 * gt + 3)
                ins = s.activation(ctx.misc[:, 32:40], ctx.misc[:, 16:24],
                                   AF.Identity, bias=0.0, scale=1.0,
                                   accum_out=ctx.misc[:, 40:41])
                ins.then_inc(ctx.sem("chS"), 1)
                ctx.bump("chS")

        @blk.vector
        def _(v):
            for gt in range(128):
                ms = ctx.misc
                v.wait_ge(ctx.sem("chS"), bCS2 + 4 * gt + 3)
                ins = v.tensor_tensor(ms[:, 24:32], ms[:, 0:8], ms[:, 8:16], ALU.add)
                ctx.edge(ins, v, "chV")
                ins = v.tensor_scalar(ms[:, 24:32], ms[:, 24:32], -1.0, None, ALU.add)
                ctx.edge(ins, v, "chV")
                ins = v.tensor_tensor(ms[:, 16:24], ms[:, 24:32], ms[:, 16:24], ALU.mult)
                ctx.edge(ins, v, "chV")
                v.wait_ge(ctx.sem("chS"), bCS2 + 4 * (gt + 1))
                ins = v.tensor_scalar(ms[:, 41:42], ms[:, 40:41], 1280.0, None, ALU.is_lt)
                ctx.edge(ins, v, "chV")
                ins = v.tensor_scalar(hp_[:, OH + 1024 + gt * 8: OH + 1024 + (gt + 1) * 8],
                                      hp_[:, OH + gt * 8: OH + (gt + 1) * 8],
                                      ms[:, 41:42], None, ALU.mult)
                ctx.edge(ins, v, "chV")
                ins = v.tensor_scalar(ms[:, 42:43], ms[:, 41:42], -1.0, 1.0,
                                      ALU.mult, ALU.add)
                ctx.edge(ins, v, "chV")
                ins = v.tensor_copy(hp_[:, OH + 2048 + gt:OH + 2048 + gt + 1], ms[:, 42:43])
                ins.then_inc(ctx.sem("pfB"), 1)
                ctx.bump("pfB")
    with nc.Block() as blk:
        @blk.gpsimd
        def _(g):
            g.wait_ge(ctx.sem("pfB"), ctx.val("pfB"))
            for h in range(2):
                g.dma_start(bass.AP(vsel_out, h * 64 * 1024,
                                    [[8, P], [1024, 64], [1, 8]]),
                            bass.AP(ctx.heap, OH + 1024 + h * 64 * 8,
                                    [[ctx.heap.shape[1], P], [8, 64], [1, 8]])
                            ).then_inc(ctx.sem("wdma"), 16)
                g.dma_start(bass.AP(ovf_out, h * 64 * P,
                                    [[1, P], [P, 64], [1, 1]]),
                            bass.AP(ctx.heap, OH + 2048 + h * 64,
                                    [[ctx.heap.shape[1], P], [1, 64], [1, 1]]),
                            allow_slow_non_contiguous=True
                            ).then_inc(ctx.sem("wdma"), 16)
            g.wait_ge(ctx.sem("wdma"), ctx.bump("wdma", 64))


def prog_l2exp(ctx):
    nc = ctx.nc
    hp_ = ctx.heap
    XT, OUTA, HT = 0, 16384, 32768
    W1S, W2S, B2R = 40960, 45056, 49152
    MSKC, B1C = 3600, 3620
    sW = "wdma"
    d_out = nc.dram_tensor("delta_out", [T, D], F32, kind="ExternalOutput")
    with nc.Block() as blk:
        @blk.gpsimd
        def _(g):
            load_cols(ctx, g, ctx.win["L2E_msk"], MSKC, 16, sW)
            load_cols(ctx, g, ctx.win["L2E_b1c"], B1C, 32, sW)
            load_row(ctx, g, ctx.win["L2E_b2r"], B2R, D, sW)
        @blk.vector
        def _(v):
            v.wait_ge(ctx.sem(sW), ctx.val(sW))
    emit_ln(ctx, xt_off=XT, tag="l2e")
    emit_ffn(ctx, "l2f", XT, ctx.win["L2E_w1"], ctx.win["L2E_w2"], B1C, B2R,
             OUTA, HT, W1S, W2S, Hff=4096, act=AF.Relu)
    bC = ctx.val("chV")
    with nc.Block() as blk:
        @blk.vector
        def _(v):
            for tt in range(NT):
                ins = v.tensor_scalar(ctx.x[:, tt * D:(tt + 1) * D],
                                      hp_[:, OUTA + tt * D: OUTA + (tt + 1) * D],
                                      ctx.misc[:, MSKC + tt:MSKC + tt + 1], None,
                                      ALU.mult)
                ctx.edge(ins, v, "chV")
    with nc.Block() as blk:
        @blk.sync
        def _(sync):
            src = bass.AP(ctx.x, 0, [[NT * D, P], [D, NT], [1, D]])
            sync.dma_start(d_out.rearrange("(i p) d -> p i d", p=P), src
                           ).then_inc(ctx.sem("xio"), 16)
            sync.wait_ge(ctx.sem("xio"), ctx.bump("xio", 16))


def make_L2_ffn_grp():
    pre = "L2f_"

    def host(inputs):
        rp = "L2_ffn_"
        w1 = np.asarray(inputs[rp + "w1"], np.float32)
        w2 = np.asarray(inputs[rp + "w2"], np.float32)
        ow1 = np.asarray(inputs[rp + "ow1"], np.float32)
        ow2 = np.asarray(inputs[rp + "ow2"], np.float32)
        b1 = np.asarray(inputs[rp + "b1"], np.float32)
        ob1 = np.asarray(inputs[rp + "ob1"], np.float32)
        b2 = np.asarray(inputs[rp + "b2"], np.float32)
        ob2 = np.asarray(inputs[rp + "ob2"], np.float32)
        w1a = np.concatenate([w1[e] for e in range(8)] + [ow1], 1)
        w2a = np.concatenate([w2[e] for e in range(8)] + [ow2], 1)
        b1a = np.concatenate([b1[e].reshape(32, P).T for e in range(8)]
                             + [ob1.reshape(32, P).T], 1)
        b2a = np.concatenate([b2, ob2.reshape(1, D)], 0)
        return {pre + "w1": np.ascontiguousarray(w1a),
                pre + "w2": np.ascontiguousarray(w2a),
                pre + "b1c": np.ascontiguousarray(b1a),
                pre + "b2r": np.ascontiguousarray(b2a)}

    def decl(ctx):
        nc = ctx.nc
        ctx.win[pre + "w1"] = nc.dram_tensor(pre + "w1", [D, 36864], F32, kind="ExternalInput")
        ctx.win[pre + "w2"] = nc.dram_tensor(pre + "w2", [4096, 9216], F32, kind="ExternalInput")
        ctx.win[pre + "b1c"] = nc.dram_tensor(pre + "b1c", [P, 288], F32, kind="ExternalInput")
        ctx.win[pre + "b2r"] = nc.dram_tensor(pre + "b2r", [9, D], F32, kind="ExternalInput")
        if not getattr(ctx, "route_onchip", False) and pre + "msk" not in ctx.win:
            ctx.win[pre + "msk"] = nc.dram_tensor(pre + "msk", [P, 144], F32, kind="ExternalInput")

    def emit(ctx, uid):
        nc = ctx.nc
        hp_ = ctx.heap
        ms = ctx.misc
        XT, OUTA, HT = 0, 16384, 32768
        W1S, W2S, B2R = 40960, 45056, 49152
        MSKC, B1C = 3600, 3744
        sW = "wdma"
        onchip = getattr(ctx, "route_onchip", False)
        if onchip:
            msk_t, msk_base = ctx.route_ms, RT_MSK
        else:
            msk_t, msk_base = ms, MSKC
        if not onchip:
            emit_ln(ctx, xt_off=XT, tag=uid)
        with nc.Block() as blk:
            @blk.gpsimd
            def _(g):
                if not onchip:
                    load_cols(ctx, g, ctx.win[pre + "msk"], MSKC, 144, sW)
                load_cols(ctx, g, ctx.win[pre + "b1c"], B1C, 288, sW)
        for e in range(9):
            with nc.Block() as blk:
                @blk.gpsimd
                def _(g, e=e):
                    load_row_sl(ctx, g, ctx.win[pre + "b2r"], e, B2R, D, sW)
            emit_ffn(ctx, f"l2f{uid}{e}", XT,
                     ctx.win[pre + "w1"][:, e * 4096:(e + 1) * 4096],
                     ctx.win[pre + "w2"][:, e * 1024:(e + 1) * 1024],
                     B1C + e * 32, B2R, OUTA, HT, W1S, W2S, Hff=4096, act=AF.Relu,
                     w1s2=50176)
            with nc.Block() as blk:
                @blk.vector
                def _(v, e=e):
                    for tt in range(NT):
                        ins = v.tensor_scalar(ms[:, 0:1024],
                                              hp_[:, OUTA + tt * D: OUTA + (tt + 1) * D],
                                              msk_t[:, msk_base + e * 16 + tt:msk_base + e * 16 + tt + 1],
                                              None, ALU.mult)
                        ctx.edge(ins, v, "chV")
                        ins = v.tensor_add(ctx.x[:, tt * D:(tt + 1) * D],
                                           ctx.x[:, tt * D:(tt + 1) * D],
                                           ms[:, 0:1024])
                        ctx.edge(ins, v, "chV")

    return decl, host, emit


RT_W8, RT_RB, RT_LOG, RT_OH = 0, 64, 80, 96
RT_MSK, RT_TS, RT_CNTR, RT_CNT8 = 0, 160, 208, 336
RT_OFSR, RT_CGT, RT_PMC, RT_ONES, RT_PRER = 344, 352, 360, 368, 496


def emit_route_common(ctx, uid, mode):
    nc = ctx.nc
    hp_ = ctx.heap
    rsb, rms = ctx.route_sb, ctx.route_ms
    sW = "wdma"
    XT = 0

    emit_ln(ctx, xt_off=XT, tag=f"rt{uid}")

    with nc.Block() as blk:
        @blk.gpsimd
        def _(g):
            src = ctx.win["L2RT_rw"][:, :].rearrange("(j p) n -> p j n", p=P)
            dst = bass.AP(rsb, RT_W8, [[rsb.shape[1], P], [8, ND], [1, 8]])
            g.dma_start(dst, src).then_inc(ctx.sem(sW), 16)
            ctx.bump(sW, 16)
            dst = bass.AP(rsb, RT_RB, [[rsb.shape[1], 1], [1, 8]])
            g.dma_start(dst, ctx.win["L2RT_rb"][:, :]).then_inc(ctx.sem(sW), 16)
            ctx.bump(sW, 16)
            if mode == "B":
                dst = bass.AP(rms, RT_CGT, [[rms.shape[1], 8], [1, 8]])
                g.dma_start(dst, ctx.win["L2RT_cnt8"][:, :]).then_inc(ctx.sem(sW), 16)
                ctx.bump(sW, 16)
                dst = bass.AP(rms, RT_PMC, [[rms.shape[1], 8], [1, 1]])
                g.dma_start(dst, ctx.win["L2RT_pm"][:, :]).then_inc(ctx.sem(sW), 16)
                ctx.bump(sW, 16)
    wv = ctx.val(sW)

    bMM, bEV = ctx.val("amm"), ctx.val("aev")
    bCS = ctx.val("chS")
    with nc.Block() as blk:
        @blk.tensor
        def _(t):
            t.wait_ge(ctx.sem(sW), wv)
            for i in range(NT):
                if i >= 2:
                    t.wait_ge(ctx.sem("aev"), bEV + i - 1)
                b0 = (i % 2) * 2
                ins = None
                for j in range(ND):
                    ins = t.matmul(ctx.ps[:, b0 * 512: b0 * 512 + 8],
                                   hp_[:, XT + j * T + i * P: XT + j * T + (i + 1) * P],
                                   rsb[:, RT_W8 + j * 8: RT_W8 + (j + 1) * 8],
                                   start=(j == 0), stop=False)
                ins = t.matmul(ctx.ps[:, b0 * 512: b0 * 512 + 8],
                               ctx.ones1[0:1, 0:P], rsb[0:1, RT_RB:RT_RB + 8],
                               start=False, stop=True)
                ins.then_inc(ctx.sem("amm"), 1)
                t.wait_ge(ctx.sem("aev"), bEV + i + 1)
                ins = t.matmul(ctx.ps[0:1, 2048 + i * 8: 2048 + (i + 1) * 8],
                               ctx.onesc[:, 0:1],
                               rsb[:, RT_OH + i * 8: RT_OH + (i + 1) * 8],
                               start=True, stop=True)
                ins.then_inc(ctx.sem("atS"), 1)
                ins = t.matmul(ctx.ps[0:1, 2560:2568], ctx.onesc[:, 0:1],
                               rsb[:, RT_OH + i * 8: RT_OH + (i + 1) * 8],
                               start=(i == 0), stop=(i == NT - 1))
                if i == NT - 1:
                    ins.then_inc(ctx.sem("atS"), 1)

        @blk.scalar
        def _(s):
            for i in range(NT):
                b0 = (i % 2) * 2
                s.wait_ge(ctx.sem("amm"), bMM + i + 1)
                ins = s.copy(rsb[:, RT_LOG + (i % 2) * 8: RT_LOG + (i % 2) * 8 + 8],
                             ctx.ps[:, b0 * 512: b0 * 512 + 8])
                ins.then_inc(ctx.sem("chS"), 1)
                ctx.bump("chS")

        @blk.vector
        def _(v):
            for i in range(NT):
                lg = rsb[:, RT_LOG + (i % 2) * 8: RT_LOG + (i % 2) * 8 + 8]
                v.wait_ge(ctx.sem("chS"), bCS + i + 1)
                ins = v.tensor_tensor(rms[:, RT_TS:RT_TS + 4], lg[:, 0:4],
                                      lg[:, 4:8], ALU.max)
                ctx.edge(ins, v, "chV")
                ins = v.tensor_tensor(rms[:, RT_TS + 4:RT_TS + 6],
                                      rms[:, RT_TS:RT_TS + 2],
                                      rms[:, RT_TS + 2:RT_TS + 4], ALU.max)
                ctx.edge(ins, v, "chV")
                ins = v.tensor_tensor(rms[:, RT_TS + 6:RT_TS + 7],
                                      rms[:, RT_TS + 4:RT_TS + 5],
                                      rms[:, RT_TS + 5:RT_TS + 6], ALU.max)
                ctx.edge(ins, v, "chV")
                ins = v.tensor_scalar(rsb[:, RT_OH + i * 8: RT_OH + (i + 1) * 8],
                                      lg, rms[:, RT_TS + 6:RT_TS + 7], None,
                                      ALU.is_ge)
                ins.then_inc(ctx.sem("aev"), 1)
    ctx.bump("amm", NT)
    ctx.bump("aev", NT)
    ctx.bump("atS", NT + 1)
    bS_done = ctx.val("atS")

    if mode == "A":
        with nc.Block() as blk:
            @blk.scalar
            def _(s):
                s.wait_ge(ctx.sem("atS"), bS_done)
                ins = s.copy(rms[0:1, RT_CNT8:RT_CNT8 + 8], ctx.ps[0:1, 2560:2568])
                ins.then_inc(ctx.sem("chS"), 1)
                ctx.bump("chS")
            wcs = ctx.val("chS")

            @blk.gpsimd
            def _(g):
                g.wait_ge(ctx.sem("chS"), wcs)
                g.dma_start(ctx.cnt_out[:, :], rms[0:1, RT_CNT8:RT_CNT8 + 8]
                            ).then_inc(ctx.sem("xio"), 16)
                g.wait_ge(ctx.sem("xio"), ctx.bump("xio", 16))
        return

    bMM2 = ctx.val("amm")
    with nc.Block() as blk:
        @blk.tensor
        def _(t):
            t.wait_ge(ctx.sem(sW), wv)
            t.wait_ge(ctx.sem("atS"), bS_done)
            ins = t.matmul(ctx.ps[0:1, 2816:2824],
                           rms[0:8, RT_PMC:RT_PMC + 1],
                           rms[0:8, RT_CGT:RT_CGT + 8], start=True, stop=True)
            ins.then_inc(ctx.sem("amm"), 1)
            ctx.bump("amm")

        @blk.scalar
        def _(s):
            s.wait_ge(ctx.sem("amm"), bMM2 + 1)
            ins = s.copy(rms[0:1, RT_CNTR:RT_CNTR + 128], ctx.ps[0:1, 2048:2176])
            ctx.edge(ins, s, "chS")
            ins = s.copy(rms[0:1, RT_OFSR:RT_OFSR + 8], ctx.ps[0:1, 2816:2824])
            ctx.edge(ins, s, "chS")
            ins = s.activation(rms[0:1, RT_ONES:RT_ONES + P], ctx.ones1[0:1, :],
                               AF.Identity, bias=0.0, scale=1.0)
            ctx.edge(ins, s, "chS")

    with nc.Block() as blk:
        @blk.vector
        def _(v):
            ins = v.tensor_copy(rms[0:1, RT_PRER:RT_PRER + 8],
                                rms[0:1, RT_OFSR:RT_OFSR + 8])
            ctx.edge(ins, v, "chV")
            for tt in range(1, NT):
                ins = v.tensor_tensor(
                    rms[0:1, RT_PRER + tt * 8: RT_PRER + (tt + 1) * 8],
                    rms[0:1, RT_PRER + (tt - 1) * 8: RT_PRER + tt * 8],
                    rms[0:1, RT_CNTR + (tt - 1) * 8: RT_CNTR + tt * 8],
                    ALU.add)
                ctx.edge(ins, v, "chV")

    bTR = ctx.val("atTR")
    bCS2 = ctx.val("chS"); bCV2 = ctx.val("chV")
    ts = RT_TS
    with nc.Block() as blk:
        @blk.tensor
        def _(t):
            for tt in range(NT):
                if tt >= 2:
                    t.wait_ge(ctx.sem("chS"), bCS2 + 4 * tt - 6)
                ins = t.matmul(ctx.ps[:, (tt % 2) * 512:(tt % 2) * 512 + 8],
                               ctx.mtril[:, :],
                               rsb[:, RT_OH + tt * 8: RT_OH + (tt + 1) * 8],
                               start=True, stop=True)
                ins.then_inc(ctx.sem("atTR"), 1)
                ins = t.matmul(ctx.ps[:, 1024 + (tt % 2) * 512: 1024 + (tt % 2) * 512 + 8],
                               rms[0:1, RT_ONES:RT_ONES + P],
                               bass.AP(rms, RT_PRER + tt * 8, [[rms.shape[1], 1], [1, 8]]),
                               start=True, stop=True)
                ins.then_inc(ctx.sem("atTR"), 1)

        @blk.scalar
        def _(s):
            for tt in range(NT):
                s.wait_ge(ctx.sem("atTR"), bTR + 2 * tt + 2)
                ins = s.copy(rms[:, ts:ts + 8],
                             ctx.ps[:, (tt % 2) * 512:(tt % 2) * 512 + 8])
                ctx.edge(ins, s, "chS")
                ins = s.copy(rms[:, ts + 8:ts + 16],
                             ctx.ps[:, 1024 + (tt % 2) * 512: 1024 + (tt % 2) * 512 + 8])
                ctx.edge(ins, s, "chS")
                ins = s.copy(rms[:, ts + 16:ts + 24],
                             rsb[:, RT_OH + tt * 8: RT_OH + (tt + 1) * 8])
                ctx.edge(ins, s, "chS")
                s.wait_ge(ctx.sem("chV"), bCV2 + 6 * tt + 3)
                ins = s.activation(rms[:, ts + 32:ts + 40], rms[:, ts + 16:ts + 24],
                                   AF.Identity, bias=0.0, scale=1.0,
                                   accum_out=rms[:, ts + 40:ts + 41])
                ins.then_inc(ctx.sem("chS"), 1)
                ctx.bump("chS")

        @blk.vector
        def _(v):
            for tt in range(NT):
                v.wait_ge(ctx.sem("chS"), bCS2 + 4 * tt + 3)
                ins = v.tensor_tensor(rms[:, ts + 24:ts + 32], rms[:, ts:ts + 8],
                                      rms[:, ts + 8:ts + 16], ALU.add)
                ctx.edge(ins, v, "chV")
                ins = v.tensor_scalar(rms[:, ts + 24:ts + 32], rms[:, ts + 24:ts + 32],
                                      -1.0, None, ALU.add)
                ctx.edge(ins, v, "chV")
                ins = v.tensor_tensor(rms[:, ts + 16:ts + 24], rms[:, ts + 24:ts + 32],
                                      rms[:, ts + 16:ts + 24], ALU.mult)
                ctx.edge(ins, v, "chV")
                v.wait_ge(ctx.sem("chS"), bCS2 + 4 * (tt + 1))
                ins = v.tensor_scalar(rms[:, ts + 41:ts + 42], rms[:, ts + 40:ts + 41],
                                      1280.0, None, ALU.is_lt)
                ctx.edge(ins, v, "chV")
                ins = v.tensor_scalar(bass.AP(rms, RT_MSK + tt, [[rms.shape[1], P], [16, 8]]),
                                      rsb[:, RT_OH + tt * 8: RT_OH + (tt + 1) * 8],
                                      rms[:, ts + 41:ts + 42], None, ALU.mult)
                ctx.edge(ins, v, "chV")
                ins = v.tensor_scalar(rms[:, RT_MSK + 128 + tt:RT_MSK + 128 + tt + 1],
                                      rms[:, ts + 41:ts + 42], -1.0, 1.0,
                                      ALU.mult, ALU.add)
                ctx.edge(ins, v, "chV")
    ctx.bump("atTR", 2 * NT)


def make_l2_routeA():
    def host(inputs):
        return {"L2RT_rw": f32c(inputs["L2_ffn_router_w"]),
                "L2RT_rb": f32c(np.asarray(inputs["L2_ffn_router_b"]).reshape(1, 8))}

    def decl(ctx):
        nc = ctx.nc
        if "L2RT_rw" not in ctx.win:
            ctx.win["L2RT_rw"] = nc.dram_tensor("L2RT_rw", [D, 8], F32, kind="ExternalInput")
            ctx.win["L2RT_rb"] = nc.dram_tensor("L2RT_rb", [1, 8], F32, kind="ExternalInput")
        ctx.cnt_out = nc.dram_tensor("cnt_out", [1, 8], F32, kind="ExternalOutput")

    def emit(ctx, uid):
        emit_route_common(ctx, uid, mode="A")

    return decl, host, emit


def make_l2_routeB():
    def host(inputs):
        return {"L2RT_rw": f32c(inputs["L2_ffn_router_w"]),
                "L2RT_rb": f32c(np.asarray(inputs["L2_ffn_router_b"]).reshape(1, 8))}

    def decl(ctx):
        nc = ctx.nc
        if "L2RT_rw" not in ctx.win:
            ctx.win["L2RT_rw"] = nc.dram_tensor("L2RT_rw", [D, 8], F32, kind="ExternalInput")
            ctx.win["L2RT_rb"] = nc.dram_tensor("L2RT_rb", [1, 8], F32, kind="ExternalInput")
        ctx.win["L2RT_cnt8"] = nc.dram_tensor("L2RT_cnt8", [8, 8], F32, kind="ExternalInput")
        ctx.win["L2RT_pm"] = nc.dram_tensor("L2RT_pm", [8, 1], F32, kind="ExternalInput")

    def emit(ctx, uid):
        emit_route_common(ctx, uid, mode="B")

    return decl, host, emit


def make_L3_ffn_grp():
    pre = "L3f_"

    def host(inputs):
        rp = "L3_ffn_"
        ipw = np.asarray(inputs[rp + "in_proj_w"], np.float32)
        opw = np.asarray(inputs[rp + "out_proj_w"], np.float32)
        l1w = np.asarray(inputs[rp + "lin1_w"], np.float32)
        l2w = np.asarray(inputs[rp + "lin2_w"], np.float32)
        return {
            pre + "ipT": np.ascontiguousarray(
                np.concatenate([ipw[e].T for e in range(4)], 1)),
            pre + "opT": np.ascontiguousarray(
                np.concatenate([opw[e].T for e in range(4)], 1)),
            pre + "ipbc": np.ascontiguousarray(np.concatenate(
                [np.asarray(inputs[rp + "in_proj_b"])[e].reshape(24, P).T
                 for e in range(4)], 1)),
            pre + "opbr": np.ascontiguousarray(
                np.asarray(inputs[rp + "out_proj_b"], np.float32)),
            pre + "l1T": np.ascontiguousarray(
                np.concatenate([l1w[e].T for e in range(4)], 1)),
            pre + "l2T": np.ascontiguousarray(
                np.concatenate([l2w[e].T for e in range(4)], 1)),
            pre + "l1bc": np.ascontiguousarray(np.concatenate(
                [np.asarray(inputs[rp + "lin1_b"])[e].reshape(32, P).T
                 for e in range(4)], 1)),
            pre + "l2br": np.ascontiguousarray(
                np.asarray(inputs[rp + "lin2_b"], np.float32)),
            pre + "ln1wc": np.ascontiguousarray(np.concatenate(
                [np.asarray(inputs[rp + "ln1_w"])[e].reshape(ND, P).T
                 for e in range(4)], 1)),
            pre + "ln1bc": np.ascontiguousarray(np.concatenate(
                [np.asarray(inputs[rp + "ln1_b"])[e].reshape(ND, P).T
                 for e in range(4)], 1)),
            pre + "ln1wr": np.ascontiguousarray(np.asarray(inputs[rp + "ln1_w"], np.float32)),
            pre + "ln1br": np.ascontiguousarray(np.asarray(inputs[rp + "ln1_b"], np.float32)),
            pre + "ln2wr": np.ascontiguousarray(np.asarray(inputs[rp + "ln2_w"], np.float32)),
            pre + "ln2br": np.ascontiguousarray(np.asarray(inputs[rp + "ln2_b"], np.float32)),
            pre + "gwT": np.ascontiguousarray(np.asarray(inputs[rp + "gate_w"]).T),
            pre + "gbr": np.ascontiguousarray(
                np.asarray(inputs[rp + "gate_b"], np.float32).reshape(1, 4)),
            pre + "iotar": np.asarray([[0.0, 1.0, 2.0, 3.0]], np.float32),
        }

    def decl(ctx):
        nc = ctx.nc
        shapes = {"ipT": [D, 12288], "opT": [D, 4096], "ipbc": [P, 96],
                  "opbr": [4, D], "l1T": [D, 16384], "l2T": [4096, 4096],
                  "l1bc": [P, P], "l2br": [4, D], "ln1wc": [P, 32],
                  "ln1bc": [P, 32], "ln1wr": [4, D], "ln1br": [4, D],
                  "ln2wr": [4, D], "ln2br": [4, D], "gwT": [D, 4], "gbr": [1, 4],
                  "iotar": [1, 4]}
        for n, sh in shapes.items():
            ctx.win[pre + n] = nc.dram_tensor(pre + n, sh, F32, kind="ExternalInput")
        if not hasattr(ctx, "l3idx"):
            ctx.l3idx = nc.alloc_sbuf_tensor("l3idx", [1, 2], I32)

    def emit(ctx, uid):
        nc = ctx.nc
        hp_ = ctx.heap
        ms = ctx.misc
        sW = "wdma"
        XSP = nc.dram_tensor(f"l3xsp{uid}", [T, D], F32)
        HSP = nc.dram_tensor(f"l3hsp{uid}", [T, D], F32)
        XTSP = nc.dram_tensor(f"l3xtsp{uid}", [P, 16384], BF16)
        DES = [nc.dram_tensor(f"l3de{uid}_{e}", [T, D], F32) for e in range(3)]
        xsrc = bass.AP(ctx.x, 0, [[NT * D, P], [D, NT], [1, D]])

        GWT, GBR, XMROW, XMT = 49152, 49216, 49280, 50304
        GLOG, GW4, ONESF = 3660, 3700, 3712
        ITR, OH1, OH2, PR4, EIF = 3850, 3856, 3860, 3864, 3880
        G2R = EIF + 2
        with nc.Block() as blk:
            @blk.gpsimd
            def _(g):
                load_w(ctx, g, ctx.win[pre + "gwT"], GWT, D, 4, sW)
                load_row(ctx, g, ctx.win[pre + "gbr"], GBR, 4, sW)
                g.dma_start(bass.AP(ms, ITR, [[MISC, 1], [1, 4]]),
                            ctx.win[pre + "iotar"][:, :]).then_inc(ctx.sem(sW), 16)
                ctx.bump(sW, 16)
            wv = ctx.val(sW)

            @blk.tensor
            def _(t):
                t.wait_ge(ctx.sem(sW), wv)
                ins = None
                for tt in range(NT):
                    for nn in range(2):
                        ins = t.matmul(ctx.ps[0:1, nn * 512:(nn + 1) * 512],
                                       ctx.onescf[:, 0:1],
                                       ctx.x[:, tt * D + nn * 512: tt * D + (nn + 1) * 512],
                                       start=(tt == 0), stop=(tt == NT - 1))
                ins.then_inc(ctx.sem("atS"), 1)
        ctx.bump("atS", 1)
        bS = ctx.val("atS"); bTR = ctx.val("atTR"); bOE = ctx.val("atOE")
        bC = ctx.val("chS")
        with nc.Block() as blk:
            @blk.scalar
            def _(s):
                s.wait_ge(ctx.sem("atS"), bS)
                ins = s.activation(hp_[0:1, XMROW:XMROW + D], ctx.ps[0:1, 0:1024],
                                   AF.Identity, bias=0.0, scale=1.0 / 2048.0)
                ctx.edge(ins, s, "chS")

            @blk.tensor
            def _(t):
                t.wait_ge(ctx.sem("chS"), bC + 1)
                for j in range(ND):
                    if j >= 2:
                        t.wait_ge(ctx.sem("atOE"), bOE + j - 1)
                    ins = t.transpose(ctx.ps_bf[0:P, (j % 2) * 1024:(j % 2) * 1024 + 1],
                                      hp_[0:1, XMROW + j * P: XMROW + (j + 1) * P],
                                      ctx.ident[0:1, 0:1])
                    ins.then_inc(ctx.sem("atTR"), 1)
                t.wait_ge(ctx.sem("atOE"), bOE + ND)
                ins = None
                for j in range(ND):
                    ins = t.matmul(ctx.ps[0:1, 1024:1028],
                                   hp_[:, XMT + j: XMT + j + 1],
                                   hp_[:, GWT + j * 4: GWT + (j + 1) * 4],
                                   start=(j == 0), stop=False)
                ins = t.matmul(ctx.ps[0:1, 1024:1028], ctx.ones1[0:1, 0:1],
                               hp_[0:1, GBR:GBR + 4], start=False, stop=True)
                ins.then_inc(ctx.sem("atS"), 1)

            @blk.scalar
            def _(s2):
                for j in range(ND):
                    s2.wait_ge(ctx.sem("atTR"), bTR + j + 1)
                    ins = s2.copy(hp_[:, XMT + j: XMT + j + 1],
                                  ctx.ps_bf[0:P, (j % 2) * 1024:(j % 2) * 1024 + 1])
                    ins.then_inc(ctx.sem("atOE"), 1)
                s2.wait_ge(ctx.sem("atS"), bS + 1)
                ins = s2.copy(ms[0:1, GLOG:GLOG + 4], ctx.ps[0:1, 1024:1028])
                ins.then_inc(ctx.sem("pfC"), 1)
        ctx.bump("atS", 1)
        ctx.bump("atTR", ND)
        ctx.bump("atOE", ND)
        ctx.bump("pfC", 1)
        bPC = ctx.val("pfC")
        with nc.Block() as blk:
            @blk.vector
            def _(v):
                L = GLOG
                v.wait_ge(ctx.sem("pfC"), bPC)
                ins = v.tensor_tensor(ms[0:1, L + 4:L + 6], ms[0:1, L:L + 2],
                                      ms[0:1, L + 2:L + 4], ALU.max)
                ctx.edge(ins, v, "chV")
                ins = v.tensor_tensor(ms[0:1, L + 6:L + 7], ms[0:1, L + 4:L + 5],
                                      ms[0:1, L + 5:L + 6], ALU.max)
                ctx.edge(ins, v, "chV")
                ins = v.tensor_scalar(ms[0:1, L + 8:L + 12], ms[0:1, L:L + 4],
                                      ms[0:1, L + 6:L + 7], None, ALU.is_ge)
                ctx.edge(ins, v, "chV")
                ins = v.scalar_tensor_tensor(ms[0:1, L + 8:L + 12], ms[0:1, L + 8:L + 12],
                                             -1e30, ms[0:1, L:L + 4], ALU.mult, ALU.add)
                ctx.edge(ins, v, "chV")
                ins = v.tensor_tensor(ms[0:1, L + 12:L + 14], ms[0:1, L + 8:L + 10],
                                      ms[0:1, L + 10:L + 12], ALU.max)
                ctx.edge(ins, v, "chV")
                ins = v.tensor_tensor(ms[0:1, L + 14:L + 15], ms[0:1, L + 12:L + 13],
                                      ms[0:1, L + 13:L + 14], ALU.max)
                ctx.edge(ins, v, "chV")
                ins = v.tensor_scalar(ms[0:1, L + 16:L + 20], ms[0:1, L:L + 4],
                                      ms[0:1, L + 14:L + 15], None, ALU.is_ge)
                ctx.edge(ins, v, "chV")
                ins = v.tensor_scalar(ms[0:1, L + 20:L + 21], ms[0:1, L + 6:L + 7],
                                      -1.0, None, ALU.mult)
                ins.then_inc(ctx.sem("pfB"), 1)

            @blk.scalar
            def _(s):
                s.wait_ge(ctx.sem("pfB"), ctx.val("pfB") + 1)
                ins = s.activation(ms[0:1, GLOG + 24:GLOG + 28],
                                   ms[0:1, GLOG:GLOG + 4], AF.Exp,
                                   bias=ms[0:1, GLOG + 20:GLOG + 21], scale=1.0,
                                   accum_out=ms[0:1, GLOG + 28:GLOG + 29])
                ctx.edge(ins, s, "chS")
                ins = s.activation(ms[0:1, ONESF:ONESF + P], ctx.ones1[0:1, :],
                                   AF.Identity, bias=0.0, scale=1.0)
                ins.then_inc(ctx.sem("chS"), 1)
                ctx.bump("chS")
        ctx.bump("pfB", 1)
        bC2 = ctx.val("chS"); bS2 = ctx.val("atS")
        ev_cell = []
        with nc.Block() as blk:
            @blk.vector
            def _(v):
                L = GLOG
                v.wait_ge(ctx.sem("chS"), bC2)
                ins = v.reciprocal(ms[0:1, L + 29:L + 30], ms[0:1, L + 28:L + 29])
                ctx.edge(ins, v, "chV")
                ins = v.tensor_tensor(ms[0:1, L + 24:L + 28], ms[0:1, L + 24:L + 28],
                                      ms[0:1, L + 16:L + 20], ALU.mult)
                ctx.edge(ins, v, "chV")
                ins = v.tensor_scalar(ms[0:1, L + 24:L + 28], ms[0:1, L + 24:L + 28],
                                      ms[0:1, L + 29:L + 30], None, ALU.mult)
                ctx.edge(ins, v, "chV")
                ins = v.tensor_scalar(ms[0:1, OH1:OH1 + 4], ms[0:1, L:L + 4],
                                      ms[0:1, L + 6:L + 7], None, ALU.is_ge)
                ctx.edge(ins, v, "chV")
                ins = v.tensor_tensor(ms[0:1, OH2:OH2 + 4], ms[0:1, L + 16:L + 20],
                                      ms[0:1, OH1:OH1 + 4], ALU.subtract)
                ctx.edge(ins, v, "chV")
                ins = v.tensor_tensor(ms[0:1, PR4:PR4 + 4], ms[0:1, OH1:OH1 + 4],
                                      ms[0:1, ITR:ITR + 4], ALU.mult)
                ctx.edge(ins, v, "chV")
                ins = v.tensor_tensor(ms[0:1, PR4 + 4:PR4 + 8], ms[0:1, OH2:OH2 + 4],
                                      ms[0:1, ITR:ITR + 4], ALU.mult)
                ctx.edge(ins, v, "chV")
                ins = v.tensor_tensor(ms[0:1, PR4 + 8:PR4 + 12], ms[0:1, L + 24:L + 28],
                                      ms[0:1, OH1:OH1 + 4], ALU.mult)
                ctx.edge(ins, v, "chV")
                ins = v.tensor_tensor(ms[0:1, PR4 + 12:PR4 + 16], ms[0:1, L + 24:L + 28],
                                      ms[0:1, OH2:OH2 + 4], ALU.mult)
                ins.then_inc(ctx.sem("chV"), 1)
                ctx.bump("chV")
            bV2done = ctx.val("chV")

            @blk.scalar
            def _(s):
                s.wait_ge(ctx.sem("chV"), bV2done)
                ins = None
                for k in range(4):
                    ins = s.activation(ms[0:1, 3884 + 4 * k:3888 + 4 * k],
                                       ms[0:1, PR4 + 4 * k:PR4 + 4 * k + 4],
                                       AF.Identity, bias=0.0, scale=1.0,
                                       accum_out=ms[0:1, EIF + k:EIF + k + 1])
                ins.then_inc(ctx.sem("chS"), 1)
                ctx.bump("chS")
                s.wait_ge(ctx.sem("atS"), bS2 + 1)
                s.copy(ms[:, GW4:GW4 + 2], ctx.ps[:, 0:2])
            bSacc = ctx.val("chS")

            @blk.tensor
            def _(t):
                t.wait_ge(ctx.sem("chS"), bSacc)
                ins = t.matmul(ctx.ps[:, 0:2], ms[0:1, ONESF:ONESF + P],
                               bass.AP(ms, G2R, [[MISC, 1], [1, 2]]),
                               start=True, stop=True)
                ins.then_inc(ctx.sem("atS"), 1)

            @blk.gpsimd
            def _(g):
                g.wait_ge(ctx.sem("chS"), bSacc)
                ins = g.tensor_copy(ctx.l3idx[0:1, 0:2], ms[0:1, EIF:EIF + 2])
                ins.then_inc(ctx.sem("pfB"), 1)
                g.wait_ge(ctx.sem("pfB"), ctx.bump("pfB"))
                if L3_DBG_DUMP:
                    dbg = nc.dram_tensor(f"l3dbg{uid}", [1, 80], F32,
                                         kind="ExternalOutput")
                    dbgi = nc.dram_tensor(f"l3dbgi{uid}", [1, 2], I32,
                                          kind="ExternalOutput")
                    g.dma_start(dbg[:, 0:64],
                                bass.AP(ms, GLOG, [[MISC, 1], [1, 64]])
                                ).then_inc(ctx.sem("xio"), 16)
                    g.dma_start(dbg[:, 64:80],
                                bass.AP(ms, ITR, [[MISC, 1], [1, 16]])
                                ).then_inc(ctx.sem("xio"), 16)
                    g.dma_start(dbgi[:, :], ctx.l3idx[0:1, 0:2]
                                ).then_inc(ctx.sem("xio"), 16)
                    g.wait_ge(ctx.sem("xio"), ctx.bump("xio", 48))
        ctx.bump("atS", 1)

        stg_ip = nc.dram_tensor(f"l3sip{uid}", [D, 2 * 3072], F32)
        stg_op = nc.dram_tensor(f"l3sop{uid}", [D, 2 * 1024], F32)
        stg_l1 = nc.dram_tensor(f"l3sl1{uid}", [D, 2 * 4096], F32)
        stg_l2 = nc.dram_tensor(f"l3sl2{uid}", [4096, 2 * 1024], F32)
        stg_cb = nc.dram_tensor(f"l3scb{uid}", [P, 2 * 72], F32)
        stg_rw = nc.dram_tensor(f"l3srw{uid}", [2, 6 * 1024], F32)
        bSTG = ctx.val("l3stg")
        n_l3 = getattr(ctx, "l3_cnt", 0)
        ctx.l3_cnt = n_l3 + 1
        stg_eng = [ET.SP, ET.Activation][n_l3]
        with nc.Block() as blk:
            def _stage(g):
                for s in range(2):
                    if L3_DBG_STATIC:
                        ev = s
                    else:
                        ev = nc.values_load(ctx.l3idx[0:1, s:s + 1], engines=[stg_eng],
                                            min_val=0, max_val=3,
                                            skip_runtime_bounds_check=True)
                    e24, e32, e8 = ev * 24, ev * 32, ev * 8
                    e1k, e3k, e4k = ev * 1024, ev * 3072, ev * 4096
                    cps = [
                        (stg_ip[:, s * 3072:(s + 1) * 3072],
                         ctx.win[pre + "ipT"][:, ds(e3k, 3072)]),
                        (stg_op[:, s * 1024:(s + 1) * 1024],
                         ctx.win[pre + "opT"][:, ds(e1k, 1024)]),
                        (stg_l1[:, s * 4096:(s + 1) * 4096],
                         ctx.win[pre + "l1T"][:, ds(e4k, 4096)]),
                        (stg_l2[:, s * 1024:(s + 1) * 1024],
                         ctx.win[pre + "l2T"][:, ds(e1k, 1024)]),
                        (stg_cb[:, s * 72 + 0:s * 72 + 24],
                         ctx.win[pre + "ipbc"][:, ds(e24, 24)]),
                        (stg_cb[:, s * 72 + 24:s * 72 + 56],
                         ctx.win[pre + "l1bc"][:, ds(e32, 32)]),
                        (stg_cb[:, s * 72 + 56:s * 72 + 64],
                         ctx.win[pre + "ln1wc"][:, ds(e8, 8)]),
                        (stg_cb[:, s * 72 + 64:s * 72 + 72],
                         ctx.win[pre + "ln1bc"][:, ds(e8, 8)]),
                        (stg_rw[s:s + 1, 0:1024],
                         ctx.win[pre + "opbr"][ds(ev, 1), :]),
                        (stg_rw[s:s + 1, 1024:2048],
                         ctx.win[pre + "l2br"][ds(ev, 1), :]),
                        (stg_rw[s:s + 1, 2048:3072],
                         ctx.win[pre + "ln1wr"][ds(ev, 1), :]),
                        (stg_rw[s:s + 1, 3072:4096],
                         ctx.win[pre + "ln1br"][ds(ev, 1), :]),
                        (stg_rw[s:s + 1, 4096:5120],
                         ctx.win[pre + "ln2wr"][ds(ev, 1), :]),
                        (stg_rw[s:s + 1, 5120:6144],
                         ctx.win[pre + "ln2br"][ds(ev, 1), :]),
                    ]
                    for dst, src in cps:
                        g.dma_start(dst, src).then_inc(ctx.sem("l3stg"), 16)
                        ctx.bump("l3stg", 16)
            if n_l3 == 0:
                blk.sync(_stage)
            else:
                blk.scalar(_stage)
        stg_done = ctx.val("l3stg")

        with nc.Block() as blk:
            @blk.sync
            def _(sync):
                sync.dma_start(XSP.rearrange("(i p) d -> p i d", p=P), xsrc
                               ).then_inc(ctx.sem("xio"), 16)
                sync.wait_ge(ctx.sem("xio"), ctx.bump("xio", 16))
        emit_ln(ctx, xt_off=0, tag=f"l3h{uid}", tok_in_x=True)
        with nc.Block() as blk:
            @blk.sync
            def _(sync):
                sync.dma_start(HSP.rearrange("(i p) d -> p i d", p=P), xsrc
                               ).then_inc(ctx.sem("xio"), 16)
                sync.dma_start(XTSP[:, :], hp_[:, 0:16384]).then_inc(ctx.sem("xio"), 16)
                sync.wait_ge(ctx.sem("xio"), ctx.bump("xio", 32))

        XT, OT = 0, 16384
        Q2T, K2T, V2, PT, STG = 32768, 34816, 36864, 38976, 40000
        OPBR = 40256
        WQ = 41344
        IPB = 3600
        OUTA, HT = 16384, 32768
        W1S, W2S, L2BR = 40960, 45056, 49152
        G1B, G2B, SBB = 50176, 51200, 52224
        LN1WR, LN1BR, LN2WR, LN2BR2 = 53248, 54272, 55296, 56320
        L1B, LN1W, LN1B = 3600, 3640, 3648
        for e in range(2):
            if e > 0:
                with nc.Block() as blk:
                    @blk.sync
                    def _(sync):
                        sync.dma_start(xsrc, HSP.rearrange("(i p) d -> p i d", p=P)
                                       ).then_inc(ctx.sem("xio"), 16)
                        sync.dma_start(hp_[:, 0:16384], XTSP[:, :]
                                       ).then_inc(ctx.sem("xio"), 16)
                        sync.wait_ge(ctx.sem("xio"), ctx.bump("xio", 32))
            with nc.Block() as blk:
                @blk.gpsimd
                def _(g, s=e):
                    g.wait_ge(ctx.sem("l3stg"), stg_done)
                    load_cols_sl(ctx, g, stg_cb, s * 72 + 0, IPB, 24, sW)
                    load_row_sl(ctx, g, stg_rw[:, 0:1024], s, OPBR, D, sW)

                @blk.vector
                def _(v):
                    v.memset(bass.AP(hp_, V2 + 64,
                                     [[hp_.shape[1], P], [65, 2 * NT], [1, 1]]), 1.0)

            for hpair in range(ND):
                with nc.Block() as blk:
                    @blk.gpsimd
                    def _(g, s=e, hpair=hpair):
                        base = s * 3072
                        load_w(ctx, g, stg_ip[:, base + hpair * P:base + (hpair + 1) * P],
                               WQ, D, P, sW)
                        load_w(ctx, g, stg_ip[:, base + D + hpair * P: base + D + (hpair + 1) * P],
                               WQ + 1024, D, P, sW)
                        load_w(ctx, g, stg_ip[:, base + 2 * D + hpair * P: base + 2 * D + (hpair + 1) * P],
                               WQ + 2048, D, P, sW)
                wv = ctx.val(sW)

                def evac_q(mi, w, ps, s, hp3=hpair):
                    return s.activation(hp_[:, Q2T + w * 512: Q2T + (w + 1) * 512], ps,
                                        AF.Identity, bias=ms[:, IPB + hp3:IPB + hp3 + 1],
                                        scale=1.0)

                linearB(ctx, f"qe{uid}{e}_{hpair}", XT, WQ, D, P, [(sW, wv)], evac_q)

                def evac_k(mi, w, ps, s, hp3=hpair):
                    return s.activation(hp_[:, K2T + w * 512: K2T + (w + 1) * 512], ps,
                                        AF.Identity, bias=ms[:, IPB + 8 + hp3:IPB + 9 + hp3],
                                        scale=1.0)

                linearB(ctx, f"ke{uid}{e}_{hpair}", XT, WQ + 1024, D, P, [(sW, wv)], evac_k)

                def evac_v(i, ps, s, hp3=hpair):
                    dst = bass.AP(hp_, V2 + i * 130, [[hp_.shape[1], P], [65, 2], [1, 64]])
                    return s.activation(dst, ps[:, 0:P], AF.Identity,
                                        bias=ms[:, IPB + 16 + hp3:IPB + 17 + hp3],
                                        scale=1.0)

                linearA(ctx, f"ve{uid}{e}_{hpair}", XT, WQ + 2048, D, P, [(sW, wv)], evac_v)

                emit_attn_pair(ctx, f"ate{uid}{e}_{hpair}", hpair, Q2T, K2T, V2, OT,
                               0.125, PT, STG, causal=False)

            with nc.Block() as blk:
                @blk.gpsimd
                def _(g, s=e):
                    load_w(ctx, g, stg_op[:, s * 1024:(s + 1) * 1024], WQ, D, D, sW)

            def evac_o(i, ps, veng):
                return veng.tensor_tensor(ctx.x[:, i * D:(i + 1) * D], ps,
                                          ctx.x[:, i * D:(i + 1) * D], ALU.add)

            linearA(ctx, f"oe{uid}{e}", OT, WQ, D, D, [(sW, ctx.val(sW))], evac_o,
                    bias_row=OPBR, evac_eng="vector")

            with nc.Block() as blk:
                @blk.gpsimd
                def _(g, s=e):
                    load_cols_sl(ctx, g, stg_cb, s * 72 + 24, L1B, 32, sW)
                    load_cols_sl(ctx, g, stg_cb, s * 72 + 56, LN1W, ND, sW)
                    load_cols_sl(ctx, g, stg_cb, s * 72 + 64, LN1B, ND, sW)
                    load_row_sl(ctx, g, stg_rw[:, 1024:2048], s, L2BR, D, sW)
                    load_row_sl(ctx, g, stg_rw[:, 2048:3072], s, LN1WR, D, sW)
                    load_row_sl(ctx, g, stg_rw[:, 3072:4096], s, LN1BR, D, sW)
                    load_row_sl(ctx, g, stg_rw[:, 4096:5120], s, LN2WR, D, sW)
                    load_row_sl(ctx, g, stg_rw[:, 5120:6144], s, LN2BR2, D, sW)
                wv = ctx.val(sW)

                @blk.vector
                def _(v):
                    v.wait_ge(ctx.sem(sW), wv)

            bS3 = ctx.val("atS")
            with nc.Block() as blk:
                @blk.tensor
                def _(t):
                    for half in range(4):
                        src = (LN1WR, LN2WR)[half // 2] + (half % 2) * 512
                        ins = t.matmul(ctx.ps[:, half * 512:(half + 1) * 512],
                                       ctx.ones1[0:1, 0:P], hp_[0:1, src:src + 512],
                                       start=True, stop=True)
                    ins.then_inc(ctx.sem("atS"), 1)

                @blk.scalar
                def _(s, e=e):
                    s.wait_ge(ctx.sem("atS"), bS3 + 1)
                    s.copy(hp_[:, G1B:G1B + 1024], ctx.ps[:, 0:1024])
                    s.activation(hp_[:, G2B:G2B + 1024], ctx.ps[:, 1024:2048],
                                 AF.Identity, bias=0.0,
                                 scale=ms[:, GW4 + e:GW4 + e + 1])
            ctx.bump("atS", 1)

            emit_ln(ctx, xt_off=XT, tag=f"f1{uid}{e}", gcols=LN1W, bcols=LN1B)
            emit_ffn(ctx, f"f{uid}{e}", XT,
                     stg_l1[:, e * 4096:(e + 1) * 4096],
                     stg_l2[:, e * 1024:(e + 1) * 1024],
                     L1B, [L2BR, LN1BR], OUTA, HT, W1S, W2S, Hff=4096, act=AF.Relu)
            emit_l3_combine(ctx, G1B, True, OUTA, SBB)
            bS4 = ctx.val("atS")
            with nc.Block() as blk:
                @blk.tensor
                def _(t):
                    for half in range(2):
                        ins = t.matmul(ctx.ps[:, half * 512:(half + 1) * 512],
                                       ctx.ones1[0:1, 0:P],
                                       hp_[0:1, LN2BR2 + half * 512:LN2BR2 + half * 512 + 512],
                                       start=True, stop=True)
                    ins.then_inc(ctx.sem("atS"), 1)

                @blk.scalar
                def _(s, e=e):
                    s.wait_ge(ctx.sem("atS"), bS4 + 1)
                    s.activation(ms[:, 2048:3072], ctx.ps[:, 0:1024],
                                 AF.Identity, bias=0.0,
                                 scale=ms[:, GW4 + e:GW4 + e + 1])
            ctx.bump("atS", 1)
            emit_l3_combine(ctx, G2B, False, OUTA, SBB)

            if e < 1:
                with nc.Block() as blk:
                    @blk.sync
                    def _(sync, e=e):
                        sync.dma_start(DES[e].rearrange("(i p) d -> p i d", p=P),
                                       xsrc).then_inc(ctx.sem("xio"), 16)
                        sync.wait_ge(ctx.sem("xio"), ctx.bump("xio", 16))

        terms = [XSP, DES[0]]
        bW = ctx.val("xio")
        bC3 = ctx.val("chV")
        with nc.Block() as blk:
            @blk.sync
            def _(sync):
                k = 0
                for tt in range(NT):
                    for i in range(len(terms)):
                        if k >= 1:
                            sync.wait_ge(ctx.sem("chV"), bC3 + k)
                        sync.dma_start(ms[:, 1024:2048],
                                       terms[i].rearrange("(i p) d -> p i d", p=P)[:, tt]
                                       ).then_inc(ctx.sem("xio"), 16)
                        ctx.bump("xio", 16)
                        k += 1

            @blk.vector
            def _(v):
                k = 0
                for tt in range(NT):
                    for i in range(len(terms)):
                        v.wait_ge(ctx.sem("xio"), bW + 16 * (k + 1))
                        ins = v.tensor_add(ctx.x[:, tt * D:(tt + 1) * D],
                                           ctx.x[:, tt * D:(tt + 1) * D],
                                           ms[:, 1024:2048])
                        ctx.edge(ins, v, "chV")
                        k += 1

    return decl, host, emit


def get_sublayer(name):
    if name == "L2_routeA":
        return make_l2_routeA()
    if name == "L2_routeB":
        return make_l2_routeB()
    L = int(name[1])
    kind = name[3:]
    if kind == "ffn" and L == 0:
        return make_L0_ffn_grp()
    if kind == "ffn" and L == 2:
        return make_L2_ffn_grp()
    if kind == "ffn" and L == 3:
        return make_L3_ffn_grp()
    if kind == "ffn" and L == 1:
        return make_L1_ffn(L)
    if kind == "attn_a" and L == 3:
        return make_mha(L)
    if kind == "attn_b" and L == 0:
        return make_causal_mha_fused(L)
    if kind == "attn_b" and L == 2:
        return make_sharedkv_mha(L)
    if kind == "attn_b" and L == 3:
        return make_swa_mha(L)
    if kind == "attn_a" and L == 1:
        return make_alibi_attn(L)
    if kind == "attn_a" and L == 2:
        return make_performer(L)
    if kind == "attn_b" and L == 1:
        return make_linear_attn(L)
    if kind == "attn_a" and L == 0:
        return make_mamba(L)
    raise KeyError(name)


def build_program(sublayer_names):
    nc = bass.Bass("TRN2", target_bir_lowering=False)
    ctx = Ctx(nc)
    x_in = nc.dram_tensor("x_in", [T, D], F32, kind="ExternalInput")
    ident_in = nc.dram_tensor("ident_in", [P, P], BF16, kind="ExternalInput")
    ones_in = nc.dram_tensor("ones_in", [1, P], BF16, kind="ExternalInput")
    onesc_in = nc.dram_tensor("onesc_in", [P, 1], BF16, kind="ExternalInput")
    onescf_in = nc.dram_tensor("onescf_in", [P, 1], F32, kind="ExternalInput")
    cst_in = nc.dram_tensor("cst_in", [P, len(CONST_VALS)], F32, kind="ExternalInput")
    mtril_in = nc.dram_tensor("mtril_in", [P, P], BF16, kind="ExternalInput")
    mstrict_in = nc.dram_tensor("mstrict_in", [P, P], BF16, kind="ExternalInput")
    mtrilf_in = nc.dram_tensor("mtrilf_in", [P, P], F32, kind="ExternalInput")
    x_out = nc.dram_tensor("x_out", [T, D], F32, kind="ExternalOutput")

    ctx.x = nc.alloc_sbuf_tensor("x_res", [P, NT * D], F32)
    ctx.heap = nc.alloc_sbuf_tensor("heap", [P, HEAP + 2 * D], BF16)
    ctx.ln_tmp_off = HEAP
    ctx.misc = nc.alloc_sbuf_tensor("misc", [P, MISC], F32)
    ctx.ident = nc.alloc_sbuf_tensor("ident", [P, P], BF16)
    ctx.ones1 = nc.alloc_sbuf_tensor("ones1", [1, P], BF16)
    ctx.onesc = nc.alloc_sbuf_tensor("onesc", [P, 1], BF16)
    ctx.onescf = nc.alloc_sbuf_tensor("onescf", [P, 1], F32)
    ctx.ps = nc.alloc_psum_tensor("ps", [P, 3072], F32)
    ctx.ps_bf = nc.alloc_psum_tensor("ps_bf", [P, 2048], BF16)
    ctx.mtril = nc.alloc_sbuf_tensor("mtril", [P, P], BF16)
    ctx.mstrict = nc.alloc_sbuf_tensor("mstrict", [P, P], BF16)
    ctx.mtrilf = nc.alloc_sbuf_tensor("mtrilf", [P, P], F32)
    ctx.cst = nc.alloc_sbuf_tensor("cstbuf", [P, len(CONST_VALS)], F32)
    for k, vv in enumerate(CONST_VALS):
        nc.const_aps.register(F32, vv, ctx.cst[:, k:k + 1])

    ctx.route_onchip = any(n.startswith("L2_route") for n in sublayer_names)
    if ctx.route_onchip:
        ctx.route_sb = nc.alloc_sbuf_tensor("route_sb", [P, 256], BF16)
        ctx.route_ms = nc.alloc_sbuf_tensor("route_ms", [P, 768], F32)

    subs = [(n,) + tuple(get_sublayer(n)) for n in sublayer_names]
    seen = set()
    for n, decl, host, emit in subs:
        if n not in seen:
            seen.add(n)
            decl(ctx)

    with nc.Block() as blk:
        @blk.sync
        def _(sync):
            dst = bass.AP(ctx.x, 0, [[NT * D, P], [D, NT], [1, D]])
            sync.dma_start(dst, x_in.rearrange("(i p) d -> p i d", p=P)).then_inc(ctx.sem("xio"), 16)
            sync.dma_start(ctx.ident[:, :], ident_in[:, :]).then_inc(ctx.sem("xio"), 16)
            sync.dma_start(ctx.ones1[:, :], ones_in[:, :]).then_inc(ctx.sem("xio"), 16)
            sync.dma_start(ctx.cst[:, :], cst_in[:, :]).then_inc(ctx.sem("xio"), 16)
            sync.dma_start(ctx.mtril[:, :], mtril_in[:, :]).then_inc(ctx.sem("xio"), 16)
            sync.dma_start(ctx.mstrict[:, :], mstrict_in[:, :]).then_inc(ctx.sem("xio"), 16)
            sync.dma_start(ctx.mtrilf[:, :], mtrilf_in[:, :]).then_inc(ctx.sem("xio"), 16)
            sync.dma_start(ctx.onesc[:, :], onesc_in[:, :]).then_inc(ctx.sem("xio"), 16)
            sync.dma_start(ctx.onescf[:, :], onescf_in[:, :]).then_inc(ctx.sem("xio"), 16)
            sync.wait_ge(ctx.sem("xio"), 144)
        ctx.bump("xio", 144)

        @blk.gpsimd
        def _(g):
            g.iota(ctx.misc[:, 3098:3099], [[0, 1]], channel_multiplier=1,
                   allow_small_or_imprecise_dtypes=True)

    for si, (n, decl, host, emit) in enumerate(subs):
        emit(ctx, uid=f"_{si}")

    with nc.Block() as blk:
        @blk.sync
        def _(sync):
            src = bass.AP(ctx.x, 0, [[NT * D, P], [D, NT], [1, D]])
            sync.dma_start(x_out.rearrange("(i p) d -> p i d", p=P), src).then_inc(ctx.sem("xio"), 16)
            sync.wait_ge(ctx.sem("xio"), ctx.val("xio") + 16)
        ctx.bump("xio", 16)
    return nc



def build_micro(prog, win_decls, extra_kw=None):
    nc = bass.Bass("TRN2", target_bir_lowering=False)
    ctx = Ctx(nc)
    x_in = nc.dram_tensor("x_in", [T, D], F32, kind="ExternalInput")
    ident_in = nc.dram_tensor("ident_in", [P, P], BF16, kind="ExternalInput")
    ones_in = nc.dram_tensor("ones_in", [1, P], BF16, kind="ExternalInput")
    onesc_in = nc.dram_tensor("onesc_in", [P, 1], BF16, kind="ExternalInput")
    onescf_in = nc.dram_tensor("onescf_in", [P, 1], F32, kind="ExternalInput")
    cst_in = nc.dram_tensor("cst_in", [P, len(CONST_VALS)], F32, kind="ExternalInput")
    mtril_in = nc.dram_tensor("mtril_in", [P, P], BF16, kind="ExternalInput")
    mstrict_in = nc.dram_tensor("mstrict_in", [P, P], BF16, kind="ExternalInput")
    mtrilf_in = nc.dram_tensor("mtrilf_in", [P, P], F32, kind="ExternalInput")

    ctx.x = nc.alloc_sbuf_tensor("x_res", [P, NT * D], F32)
    ctx.heap = nc.alloc_sbuf_tensor("heap", [P, HEAP + 2 * D], BF16)
    ctx.ln_tmp_off = HEAP
    ctx.misc = nc.alloc_sbuf_tensor("misc", [P, MISC], F32)
    ctx.ident = nc.alloc_sbuf_tensor("ident", [P, P], BF16)
    ctx.ones1 = nc.alloc_sbuf_tensor("ones1", [1, P], BF16)
    ctx.onesc = nc.alloc_sbuf_tensor("onesc", [P, 1], BF16)
    ctx.onescf = nc.alloc_sbuf_tensor("onescf", [P, 1], F32)
    ctx.mtril = nc.alloc_sbuf_tensor("mtril", [P, P], BF16)
    ctx.mstrict = nc.alloc_sbuf_tensor("mstrict", [P, P], BF16)
    ctx.mtrilf = nc.alloc_sbuf_tensor("mtrilf", [P, P], F32)
    ctx.ps = nc.alloc_psum_tensor("ps", [P, 3072], F32)
    ctx.ps_bf = nc.alloc_psum_tensor("ps_bf", [P, 2048], BF16)
    ctx.cst = nc.alloc_sbuf_tensor("cstbuf", [P, len(CONST_VALS)], F32)
    for k, vv in enumerate(CONST_VALS):
        nc.const_aps.register(F32, vv, ctx.cst[:, k:k + 1])

    for name, shape in win_decls.items():
        ctx.win[name] = nc.dram_tensor(name, list(shape), F32, kind="ExternalInput")

    with nc.Block() as blk:
        @blk.sync
        def _(sync):
            dst = bass.AP(ctx.x, 0, [[NT * D, P], [D, NT], [1, D]])
            sync.dma_start(dst, x_in.rearrange("(i p) d -> p i d", p=P)).then_inc(ctx.sem("xio"), 16)
            sync.dma_start(ctx.ident[:, :], ident_in[:, :]).then_inc(ctx.sem("xio"), 16)
            sync.dma_start(ctx.ones1[:, :], ones_in[:, :]).then_inc(ctx.sem("xio"), 16)
            sync.dma_start(ctx.onesc[:, :], onesc_in[:, :]).then_inc(ctx.sem("xio"), 16)
            sync.dma_start(ctx.onescf[:, :], onescf_in[:, :]).then_inc(ctx.sem("xio"), 16)
            sync.dma_start(ctx.cst[:, :], cst_in[:, :]).then_inc(ctx.sem("xio"), 16)
            sync.dma_start(ctx.mtril[:, :], mtril_in[:, :]).then_inc(ctx.sem("xio"), 16)
            sync.dma_start(ctx.mstrict[:, :], mstrict_in[:, :]).then_inc(ctx.sem("xio"), 16)
            sync.dma_start(ctx.mtrilf[:, :], mtrilf_in[:, :]).then_inc(ctx.sem("xio"), 16)
            sync.wait_ge(ctx.sem("xio"), 144)
        ctx.bump("xio", 144)

    if extra_kw:
        prog(ctx, **extra_kw)
    else:
        prog(ctx)
    return nc


_PROG_CACHE = {}


def cached_program(key, fn):
    if key not in _PROG_CACHE:
        _PROG_CACHE[key] = fn()
    return _PROG_CACHE[key]


def base_maps(x_shards):
    ident = np.eye(P, dtype=BF)
    maps = []
    for c in range(8):
        maps.append({
            "x_in": np.ascontiguousarray(x_shards[c]),
            "ident_in": ident,
            "ones_in": np.ones((1, P), BF),
            "onesc_in": np.ones((P, 1), BF),
            "onescf_in": np.ones((P, 1), np.float32),
            "cst_in": np.tile(np.asarray(CONST_VALS, np.float32), (P, 1)),
            "mtril_in": np.triu(np.ones((P, P), BF)),
            "mstrict_in": np.tril(np.ones((P, P), BF), -1),
            "mtrilf_in": np.triu(np.ones((P, P), np.float32)),
        })
    return maps


def launch(nc, maps, outs):
    res = run_bass_kernel_spmd(nc, maps, core_ids=list(range(8)))
    return [np.stack([np.asarray(r[o]) for r in res.results]) for o in outs]


def f32c(a):
    return np.ascontiguousarray(np.asarray(a, np.float32))


SEQ_A = ["L0_attn_a", "L0_ffn", "L0_attn_b", "L0_ffn",
         "L1_attn_a", "L1_ffn", "L1_attn_b", "L1_ffn",
         "L2_attn_a", "L2_routeA"]
SEQ_B = ["L2_routeB", "L2_ffn", "L2_attn_b", "L2_routeA"]
SEQ_C = ["L2_routeB", "L2_ffn", "L3_attn_a", "L3_ffn", "L3_attn_b", "L3_ffn"]


def kernel(**inputs):
    x = np.ascontiguousarray(np.asarray(inputs["L0_attn_a_x"], np.float32))
    xs = [x[c] for c in range(8)]
    pm = [np.asarray([[1.0] if r < c else [0.0] for r in range(8)], np.float32)
          for c in range(8)]

    def run_seq(names, key, outs, extra=None):
        nonlocal xs
        nc = cached_program(key, lambda: build_program(names))
        weights = {}
        for n in set(names):
            d_, h_, e_ = get_sublayer(n)
            weights.update(h_(inputs))
        maps = base_maps(xs)
        for c, m in enumerate(maps):
            m.update(weights)
            if extra:
                m.update({k: f32c(v[c]) if isinstance(v, list) else f32c(v)
                          for k, v in extra.items()})
        res = launch(nc, maps, outs)
        xs = [res[0][c] for c in range(8)]
        return res

    _, cnt = run_seq(SEQ_A, "seqA", ["x_out", "cnt_out"])
    cnt8 = np.ascontiguousarray(np.stack([cnt[c].reshape(8) for c in range(8)]))
    _, cnt = run_seq(SEQ_B, "seqB", ["x_out", "cnt_out"],
                     extra={"L2RT_cnt8": cnt8, "L2RT_pm": pm})
    cnt8 = np.ascontiguousarray(np.stack([cnt[c].reshape(8) for c in range(8)]))
    run_seq(SEQ_C, "seqC", ["x_out"],
            extra={"L2RT_cnt8": cnt8, "L2RT_pm": pm})

    return np.ascontiguousarray(np.stack(xs)).astype(np.float32)


def kernel_v7(**inputs):
    x = np.ascontiguousarray(np.asarray(inputs["L0_attn_a_x"], np.float32))
    xs = [x[c] for c in range(8)]

    def run_group(names, key, extra=None):
        nonlocal xs
        nc = cached_program(key, lambda: build_program(names))
        weights = {}
        for n in names:
            d_, h_, e_ = get_sublayer(n)
            weights.update(h_(inputs))
        maps = base_maps(xs)
        for c, m in enumerate(maps):
            m.update(weights)
            if extra:
                m.update({k: f32c(v[c]) if isinstance(v, list) else f32c(v)
                          for k, v in extra.items()})
        (xo,) = launch(nc, maps, ["x_out"])
        xs = [xo[c] for c in range(8)]

    def run_micro(key, prog, decls, per_core_ins, outs, extra_kw=None):
        nc = cached_program(key, lambda: build_micro(prog, decls, extra_kw))
        maps = base_maps(xs)
        for c in range(8):
            maps[c].update({k: f32c(v[c]) if isinstance(v, list) else f32c(v)
                            for k, v in per_core_ins.items()})
        return launch(nc, maps, outs)

    def l0_ffn():
        nonlocal xs
        pre = "L0_ffn_"
        w8 = np.concatenate([np.asarray(inputs[pre + "w_qr"]).T,
                             np.asarray(inputs[pre + "w_kvr"]).T], 1)
        tgt = np.asarray([[1024.0]] * 4 + [[512.0]] * 4, np.float32)
        qs, kvs, km, fac, nsel = run_micro(
            "l0route", prog_l0route,
            {"L0R_w8": (D, 8), "L0R_tgt": (8, 1)},
            {"L0R_w8": w8, "L0R_tgt": tgt},
            ["qs_out", "kvs_out", "km_out", "fac_out", "nsel_out"])
        deltas = []
        nkv = np.asarray(inputs[pre + "null_kv"], np.float32)
        for g in range(4):
            kn = np.zeros((P, 8), np.float32)
            for gh in range(8):
                kn[(gh % 2) * 64:(gh % 2) * 64 + 64, gh] = nkv[0, g, gh, 0]
            vn = np.ascontiguousarray(nkv[1, g, :, 0, :])
            (dg,) = run_micro(
                f"l0g", prog_l0g,
                {"L0G_qs": (P, 16), "L0G_kvs": (P, 16), "L0G_km": (P, 16),
                 "L0G_kn": (P, 8), "L0G_vn": (8, 64),
                 "L0G_wqT": (D, 512), "L0G_wkvT": (D, 1024), "L0G_woT": (512, D)},
                {"L0G_qs": [qs[c][:, g::4] for c in range(8)],
                 "L0G_kvs": [kvs[c][:, g::4] for c in range(8)],
                 "L0G_km": [km[c][:, g::4] for c in range(8)],
                 "L0G_kn": kn, "L0G_vn": vn,
                 "L0G_wqT": np.asarray(inputs[pre + "w_q"])[g].T,
                 "L0G_wkvT": np.asarray(inputs[pre + "w_kv"])[g].T,
                 "L0G_woT": np.asarray(inputs[pre + "w_out"])[g].T},
                ["d_out"])
            deltas.append(dg)
        ins = {"L0F_fac": [fac[c] for c in range(8)],
               "L0F_nsel": [nsel[c] for c in range(8)],
               "L0F_ntr": np.asarray(inputs[pre + "null_token"]).reshape(1, D)}
        for i in range(4):
            ins[f"dsum_{i}"] = [deltas[i][c] for c in range(8)]
        decls = {"L0F_fac": (P, 16), "L0F_nsel": (P, 16), "L0F_ntr": (1, D)}
        (xo,) = run_micro("l0fin", prog_l0fin, decls, ins, ["x_out2"])
        xs = [xo[c] for c in range(8)]

    def l2_ffn():
        nonlocal xs
        pre = "L2_ffn_"
        xfull = np.ascontiguousarray(np.concatenate(xs, 0))
        vsel, ovf = run_micro(
            "l2route", prog_l2route,
            {"L2R_xfull": (8 * T, D), "L2R_rw": (D, 8), "L2R_rb": (1, 8)},
            {"L2R_xfull": xfull, "L2R_rw": np.asarray(inputs[pre + "router_w"]),
             "L2R_rb": np.asarray(inputs[pre + "router_b"]).reshape(1, 8)},
            ["vsel_out", "ovf_out"])
        vsel = vsel[0]
        ovf = ovf[0]
        deltas = []
        for e in range(9):
            if e < 8:
                w1 = np.asarray(inputs[pre + "w1"])[e]
                b1 = np.asarray(inputs[pre + "b1"])[e]
                w2 = np.asarray(inputs[pre + "w2"])[e]
                b2 = np.asarray(inputs[pre + "b2"])[e]
                msk = [vsel[c * T:(c + 1) * T, e].reshape(NT, P).T for c in range(8)]
            else:
                w1 = np.asarray(inputs[pre + "ow1"])
                b1 = np.asarray(inputs[pre + "ob1"])
                w2 = np.asarray(inputs[pre + "ow2"])
                b2 = np.asarray(inputs[pre + "ob2"])
                msk = [ovf[c * T:(c + 1) * T, 0].reshape(NT, P).T for c in range(8)]
            (dg,) = run_micro(
                "l2exp", prog_l2exp,
                {"L2E_msk": (P, 16), "L2E_b1c": (P, 32), "L2E_b2r": (1, D),
                 "L2E_w1": (D, 4096), "L2E_w2": (4096, D)},
                {"L2E_msk": msk, "L2E_b1c": b1.reshape(32, P).T,
                 "L2E_b2r": b2.reshape(1, D), "L2E_w1": w1, "L2E_w2": w2},
                ["delta_out"])
            deltas.append(dg)
        ins = {}
        for i in range(9):
            ins[f"dsum_{i}"] = [deltas[i][c] for c in range(8)]
        (xo,) = run_micro("sum9", prog_sumN, {}, ins, ["x_out2"],
                          extra_kw={"n_terms": 9})
        xs = [xo[c] for c in range(8)]

    def l3_ffn():
        nonlocal xs
        pre = "L3_ffn_"
        (gates,) = run_micro(
            "l3gate", prog_l3gate,
            {"L3G_gwT": (D, 4), "L3G_gbr": (1, 4)},
            {"L3G_gwT": np.asarray(inputs[pre + "gate_w"]).T,
             "L3G_gbr": np.asarray(inputs[pre + "gate_b"]).reshape(1, 4)},
            ["gates_out"])
        deltas = []
        for e in range(4):
            (v1,) = run_micro(
                "l3att", prog_l3att,
                {"L3A_ipT": (D, 3072), "L3A_opT": (D, D), "L3A_ipbc": (P, 24),
                 "L3A_opbr": (1, D)},
                {"L3A_ipT": np.asarray(inputs[pre + "in_proj_w"])[e].T,
                 "L3A_opT": np.asarray(inputs[pre + "out_proj_w"])[e].T,
                 "L3A_ipbc": np.asarray(inputs[pre + "in_proj_b"])[e].reshape(24, P).T,
                 "L3A_opbr": np.asarray(inputs[pre + "out_proj_b"])[e].reshape(1, D)},
                ["v1_out"])
            xs_save = xs
            xs = [v1[c] for c in range(8)]
            (dg,) = run_micro(
                "l3ffn", prog_l3ffn,
                {"L3F_l1T": (D, 4096), "L3F_l2T": (4096, D), "L3F_l1bc": (P, 32),
                 "L3F_l2br": (1, D), "L3F_ln1w": (P, ND), "L3F_ln1b": (P, ND),
                 "L3F_ln1wr": (1, D), "L3F_ln1br": (1, D),
                 "L3F_ln2wr": (1, D), "L3F_ln2br": (1, D), "L3F_wsel": (P, 1)},
                {"L3F_l1T": np.asarray(inputs[pre + "lin1_w"])[e].T,
                 "L3F_l2T": np.asarray(inputs[pre + "lin2_w"])[e].T,
                 "L3F_l1bc": np.asarray(inputs[pre + "lin1_b"])[e].reshape(32, P).T,
                 "L3F_l2br": np.asarray(inputs[pre + "lin2_b"])[e].reshape(1, D),
                 "L3F_ln1w": np.asarray(inputs[pre + "ln1_w"])[e].reshape(ND, P).T,
                 "L3F_ln1b": np.asarray(inputs[pre + "ln1_b"])[e].reshape(ND, P).T,
                 "L3F_ln1wr": np.asarray(inputs[pre + "ln1_w"])[e].reshape(1, D),
                 "L3F_ln1br": np.asarray(inputs[pre + "ln1_b"])[e].reshape(1, D),
                 "L3F_ln2wr": np.asarray(inputs[pre + "ln2_w"])[e].reshape(1, D),
                 "L3F_ln2br": np.asarray(inputs[pre + "ln2_b"])[e].reshape(1, D),
                 "L3F_wsel": [np.tile(gates[c][:, e:e + 1], (P, 1))
                              for c in range(8)]},
                ["delta_out"])
            xs = xs_save
            deltas.append(dg)
        ins = {}
        for i in range(4):
            ins[f"dsum_{i}"] = [deltas[i][c] for c in range(8)]
        (xo,) = run_micro("sum4", prog_sumN, {}, ins, ["x_out2"],
                          extra_kw={"n_terms": 4})
        xs = [xo[c] for c in range(8)]

    run_group(["L0_attn_a", "L0_ffn", "L0_attn_b", "L0_ffn"], "g_l0")
    def l2_masks():
        pre = "L2_ffn_"
        xfull = np.ascontiguousarray(np.concatenate(xs, 0))
        vsel, ovf = run_micro(
            "l2route", prog_l2route,
            {"L2R_xfull": (8 * T, D), "L2R_rw": (D, 8), "L2R_rb": (1, 8)},
            {"L2R_xfull": xfull, "L2R_rw": np.asarray(inputs[pre + "router_w"]),
             "L2R_rb": np.asarray(inputs[pre + "router_b"]).reshape(1, 8)},
            ["vsel_out", "ovf_out"])
        vsel, ovf = vsel[0], ovf[0]
        msks = []
        for c in range(8):
            cols = [vsel[c * T:(c + 1) * T, e].reshape(NT, P).T for e in range(8)]
            cols.append(ovf[c * T:(c + 1) * T, 0].reshape(NT, P).T)
            msks.append(np.ascontiguousarray(np.concatenate(cols, 1)))
        return msks

    run_group(["L1_attn_a", "L1_ffn", "L1_attn_b", "L1_ffn", "L2_attn_a"], "g_mid")
    run_group(["L2_ffn", "L2_attn_b"], "g_l2a", extra={"L2f_msk": l2_masks()})
    run_group(["L2_ffn"], "g_l2b2", extra={"L2f_msk": l2_masks()})
    run_group(["L3_attn_a", "L3_ffn", "L3_attn_b", "L3_ffn"], "g_l3")

    return np.ascontiguousarray(np.stack(xs)).astype(np.float32)


def run_kernel(x_full, inputs, sublayer_names, n_cores=8):
    nc = build_program(sublayer_names)
    weights = {}
    for name in sublayer_names:
        decl, host, emit = get_sublayer(name)
        weights.update(host(inputs))
    ident = np.eye(P, dtype=BF)
    ones = np.ones((1, P), dtype=BF)
    in_maps = []
    for c in range(n_cores):
        m = {"x_in": np.ascontiguousarray(x_full[c]), "ident_in": ident,
             "ones_in": ones,
             "cst_in": np.tile(np.asarray(CONST_VALS, np.float32), (P, 1)),
             "mtril_in": np.triu(np.ones((P, P), BF)),
             "mstrict_in": np.tril(np.ones((P, P), BF), -1),
             "mtrilf_in": np.triu(np.ones((P, P), np.float32)),
             "onesc_in": np.ones((P, 1), BF),
            "onescf_in": np.ones((P, 1), np.float32),
             "onescf_in": np.ones((P, 1), np.float32)}
        m.update(weights)
        in_maps.append(m)
    res = run_bass_kernel_spmd(nc, in_maps, core_ids=list(range(n_cores)))
    return np.stack([r["x_out"] for r in res.results])
